# version 03
# baseline (speedup 1.0000x reference)
import numpy as np
import concourse.bass as bass
import concourse.mybir as mybir
from concourse.bass_utils import run_bass_kernel_spmd
from concourse.alu_op_type import AluOpType as aop
from contextlib import ExitStack

F32 = mybir.dt.float32
BF16 = mybir.dt.bfloat16
FP16 = mybir.dt.float16
AF = mybir.ActivationFunctionType
AX = mybir.AxisListType

P = 128
D = 2048
NT = 16
CH = 1024
MAGIC = float(1.5 * 2.0 ** 23)
EPS_RMS = 1.1920929e-07
EPS_LN = 1e-5
C3LIST = [(0, 704), (704, 704), (1408, 640)]
C4 = 512

ENGS = ["sync", "gpsimd", "vector", "scalar", "tensor"]
SEMS = ["sLD", "sWD", "sSTA", "sSTV", "sPE", "sVE", "sSE"]


def build(dbg=False):
    nc = bass.Bass("TRN2", target_bir_lowering=False)
    SCR = "ExternalOutput" if dbg else "Internal"
    xm = nc.dram_tensor("xm", [NT, P, D], F32, kind="ExternalInput")
    xp = nc.dram_tensor("xp", [NT, P, D], F32, kind="ExternalInput")
    msk = nc.dram_tensor("msk", [P, 1], F32, kind="ExternalInput")
    wqkv = nc.dram_tensor("wqkv", [NT, P, 6144], F32, kind="ExternalInput")
    wsw = nc.dram_tensor("wsw", [NT, P, 16384], F32, kind="ExternalInput")
    wou = nc.dram_tensor("wou", [64, P, D], F32, kind="ExternalInput")
    w1 = nc.dram_tensor("w1", [NT, P, 8192], F32, kind="ExternalInput")
    w2 = nc.dram_tensor("w2", [64, P, D], F32, kind="ExternalInput")
    out = nc.dram_tensor("out", [NT, P, D], F32, kind="ExternalOutput")
    qkvS = nc.dram_tensor("qkvS", [80, P, D], BF16, kind=SCR)
    yS = nc.dram_tensor("yS", [NT, P, D], BF16, kind=SCR)
    x1S = nc.dram_tensor("x1S", [NT, P, D], F32, kind=SCR)
    statS = nc.dram_tensor("statS", [2, D], F32, kind=SCR)
    wq1S = nc.dram_tensor("wq1S", [NT, P, 8192], FP16, kind=SCR)
    wq2S = nc.dram_tensor("wq2S", [64, P, D], FP16, kind=SCR)

    xmR = xm.rearrange("t p c -> p t c")
    xpR = xp.rearrange("t p c -> p t c")
    wqkvR = wqkv.rearrange("t p c -> p t c")
    wswR = wsw.rearrange("t p c -> p t c")
    wouR = wou.rearrange("t p c -> p t c")
    wq1R = wq1S.rearrange("t p c -> p t c")
    wq2R = wq2S.rearrange("t p c -> p t c")
    qkvSR = qkvS.rearrange("t p c -> p t c")
    ySR = yS.rearrange("t p c -> p t c")
    x1SR = x1S.rearrange("t p c -> p t c")
    outR = out.rearrange("t p c -> p t c")

    CNT = {s: 0 for s in SEMS}
    ops = {e: [] for e in ENGS}
    wait_list = []

    def E(eng, fn, inc=None, n=1, drain=False):
        if inc:
            CNT[inc] += n
        ops[eng].append((fn, inc, n, drain))
        return CNT[inc] if inc else None

    def W(eng, sem, val):
        if val is None or val <= 0:
            return
        wait_list.append((sem, val))
        ops[eng].append((lambda e, s=sem, v=val: e.wait_ge(SEMH[s], v), None, 0, False))

    def DMA(eng, o, i, sem):
        return E(eng, lambda e, O=o, I=i: e.dma_start(out=O, in_=I), inc=sem, n=16)

    def TT(o, a, b, alu, inc=None, drain=False):
        return E("vector", lambda e, O=o, A=a, B=b, L=alu: e.tensor_tensor(O, A, B, L), inc=inc, drain=drain)

    def TS(o, a, s1, s2, op0, op1=None, inc=None):
        if op1 is None:
            return E("vector", lambda e, O=o, A=a, X=s1, o0=op0: e.tensor_scalar(O, A, X, None, o0), inc=inc)
        return E("vector", lambda e, O=o, A=a, X=s1, Y=s2, o0=op0, o1=op1: e.tensor_scalar(O, A, X, Y, o0, o1), inc=inc)

    def VC(o, a, inc=None):
        return E("vector", lambda e, O=o, A=a: e.tensor_copy(O, A), inc=inc)

    def RECIP(o, a, inc=None):
        return E("vector", lambda e, O=o, A=a: e.reciprocal(O, A), inc=inc)

    def RED(o, a, inc=None, absv=False):
        return E("vector", lambda e, O=o, A=a, V=absv: e.tensor_reduce(O, A, AX.X, aop.add, apply_absolute_value=V), inc=inc)

    def SCAN(o, a, init, inc=None):
        return E("vector", lambda e, O=o, A=a, I=init: e.tensor_tensor_scan(O, A, A, I, aop.add, aop.bypass), inc=inc)

    def ACT(o, a, f, bias=0.0, scale=1.0, inc=None):
        return E("scalar", lambda e, O=o, A=a, F=f, B=bias, S=scale: e.activation(O, A, F, bias=B, scale=S), inc=inc)

    def SC(o, a, inc=None):
        return E("scalar", lambda e, O=o, A=a: e.copy(O, A), inc=inc)

    def SDRAIN():
        return E("scalar", lambda e: e.drain())

    def MM(o, l, r, st, sp, inc=None):
        return E("tensor", lambda e, O=o, L=l, R=r, S=st, T=sp: e.matmul(O, L, R, start=S, stop=T), inc=inc)

    def MS(eng, ap, val):
        return E(eng, lambda e, A=ap, V=val: e.memset(A, V))

    with ExitStack() as ctx:
        sbt = lambda n_, s, d: ctx.enter_context(nc.sbuf_tensor(n_, s, d))
        pst = lambda n_, s: ctx.enter_context(nc.psum_tensor(n_, s, F32))
        xz = sbt("xz", [P, 4096], F32)
        big = sbt("big", [P, 22528], F32)
        yzp = sbt("yzp", [P, 5632], F32)
        wza = sbt("wza", [P, 4096], F32)
        wzb = sbt("wzb", [P, 4096], F32)
        tzp = sbt("tzp", [P, 4096], F32)
        bzp = sbt("bzp", [P, 4096], F32)
        szw = sbt("szw", [1, 4096], F32)
        szp = sbt("szp", [1, 32], F32)
        cz = sbt("cz", [P, 128], F32)
        oc1 = sbt("oc1", [P, 1], F32)
        oc1b = sbt("oc1b", [P, 1], BF16)
        or1 = sbt("or1", [1, P], F32)
        mskb = sbt("mskb", [P, 1], F32)
        cst = sbt("cst", [P, 8], F32)
        WZ = [wza, wzb]

        pA = pst("pA", [P, 512])
        pB = pst("pB", [P, 512])
        pC = pst("pC", [P, 512])
        pBC = pst("pBC", [P, 512])
        pSa = pst("pSa", [1, 512])
        pSb = pst("pSb", [1, 512])
        pSc = pst("pSc", [1, 512])
        pSd = pst("pSd", [1, 512])

        SEMH = {s: ctx.enter_context(nc.semaphore(s)) for s in SEMS}

        def view(pool, off_b, dt, ncols):
            esz = 4 if dt == F32 else 2
            nf4 = (ncols * esz + 3) // 4
            ap = pool[:, off_b // 4: off_b // 4 + nf4]
            if dt != F32:
                ap = ap.bitcast(dt)
            return ap[:, 0:ncols]

        def view3(pool, off_b, dt, a, c):
            return view(pool, off_b, dt, a * c).rearrange("p (a c) -> p a c", a=a)

        def row(r, c0=0, n=1024):
            return szw[0:1, r * 1024 + c0: r * 1024 + c0 + n]

        def VF(n=3):
            for _ in range(n):
                VC(cst[:, 6:7], cst[:, 7:8])

        def barrier(engs=("vector", "scalar", "tensor")):
            for eng in engs:
                for s in ("sVE", "sSE", "sPE"):
                    W(eng, s, CNT[s])

        MS("vector", oc1[:, :], 1.0)
        MS("vector", oc1b[:, :], 1.0)
        MS("vector", or1[:, :], 1.0)
        MS("vector", cz[:, :], 0.0)
        MS("vector", cst[:, 0:1], EPS_LN)
        MS("vector", cst[:, 1:2], EPS_RMS)
        E("vector", lambda e: e.memset(cst[:, 2:3], -MAGIC), inc="sVE")
        nc.const_aps.register(F32, EPS_LN, cst[:, 0:1])
        nc.const_aps.register(F32, EPS_RMS, cst[:, 1:2])
        nc.const_aps.register(F32, -MAGIC, cst[:, 2:3])
        MS("vector", cst[:, 3:4], MAGIC)
        nc.const_aps.register(F32, MAGIC, cst[:, 3:4])
        ldMsk = DMA("sync", mskb[:, :], msk[:, :], "sLD")

        def BCAST(dst, srcrow, n):
            W("tensor", "sVE", CNT["sVE"])
            c0_ = 0
            while c0_ < n:
                nn = min(512, n - c0_)
                W("tensor", "sSE", CNT["sSE"])
                c = MM(pBC[0:P, 0:nn], or1[0:1, :], srcrow[:, c0_:c0_ + nn], True, True, inc="sPE")
                W("scalar", "sPE", c)
                SC(dst[:, c0_:c0_ + nn], pBC[0:P, 0:nn], inc="sSE")
                c0_ += nn
            return CNT["sSE"]

        st = {"tz": {}, "w": {}, "eb": {}, "ps": {}, "p2": {}, "yb": {}, "sg": 0,
              "x1e": 0, "oe": 0, "xz_v": 0, "xz_s": 0, "statS": 0, "wq_s1": 0}

        wz_last_v = 0
        wz_last_s = 0
        for widx, (wsrc, wdst, nrt, ncols) in enumerate([(w1, wq1S, NT, 8192), (w2, wq2S, 64, D)]):
            nel = float(nrt * P * ncols)
            if ncols == 8192:
                tiles = [(t, 1, c) for t in range(nrt) for c in (0, 4096)]
            else:
                tiles = [(t, 2, 0) for t in range(0, nrt, 2)]
            relv = {}
            for i, (t, na, c0) in enumerate(tiles):
                par = i % 2
                if i >= 2:
                    W("sync", "sVE", relv[par])
                else:
                    W("sync", "sVE", wz_last_v)
                    W("sync", "sSE", wz_last_s)
                if na == 1:
                    src = wsrc[t, :, c0:c0 + 4096]
                else:
                    src = wsrc[t:t + 2, :, :]
                dst = WZ[par][:, 0:4096] if na == 1 else view3(WZ[par], 0, F32, 2, ncols)
                lc = DMA("sync", dst, src, "sLD")
                W("vector", "sLD", lc)
                RED(cz[:, 96 + par:97 + par], WZ[par][:, 0:4096], absv=True)
                VF()
                a, b = (98, 99) if i % 2 == 0 else (99, 98)
                relv[par] = TT(cz[:, b:b + 1], cz[:, a:a + 1], cz[:, 96 + par:97 + par], aop.add, inc="sVE")
            wz_last_v = CNT["sVE"]
            fin = 99 if len(tiles) % 2 == 1 else 98
            W("tensor", "sVE", CNT["sVE"])
            c = MM(pSa[0:1, 0:1], oc1[:, 0:1], cz[:, fin:fin + 1], True, True, inc="sPE")
            W("vector", "sPE", c)
            TS(row(0, 0, 1), pSa[0:1, 0:1], 1.0 / nel, 1e-5, aop.mult, aop.max)
            VF()
            VC(szp[0:1, widx:widx + 1], row(0, 0, 1))
            RECIP(row(1, 0, 1), row(0, 0, 1))
            VF()
            VC(szp[0:1, 2 + widx:3 + widx], row(1, 0, 1), inc="sVE")
            BCAST(cst[:, 4 + widx:5 + widx], szp[0:1, 2 + widx:3 + widx], 1)
            E("vector", lambda e: e.memset(cz[:, 98:100], 0.0), inc="sVE")
            swbc = cst[:, 4 + widx:5 + widx]
            oz2d = view(bzp, 0, FP16, 8192)
            q2v = view(big, 0, F32, 4096)
            q3v = view(big, 16384, F32, 4096)
            rels = {}
            relst = None
            for i, (t, na, c0) in enumerate(tiles):
                par = i % 2
                if i >= 2:
                    W("sync", "sVE", rels[par])
                else:
                    W("sync", "sVE", wz_last_v)
                    W("sync", "sSE", wz_last_s)
                if na == 1:
                    src = wsrc[t, :, c0:c0 + 4096]
                    ddst = wdst[t, :, c0:c0 + 4096]
                else:
                    src = wsrc[t:t + 2, :, :]
                    ddst = wdst[t:t + 2, :, :]
                dst = WZ[par][:, 0:4096] if na == 1 else view3(WZ[par], 0, F32, 2, ncols)
                lc = DMA("sync", dst, src, "sLD")
                W("vector", "sLD", lc)
                W("vector", "sSE", st["wq_s1"])
                cv = TS(tzp[:, 0:4096], WZ[par][:, 0:4096], swbc, None, aop.mult, inc="sVE")
                rels[par] = cv
                W("scalar", "sVE", cv)
                c1 = ACT(q2v, tzp[:, 0:4096], AF.Copy, bias=MAGIC, scale=1.0, inc="sSE")
                st["wq_s1"] = c1
                W("vector", "sSE", c1)
                cv2 = TS(q3v, q2v, MAGIC + 1.0, MAGIC - 1.0, aop.min, aop.max, inc="sVE")
                W("scalar", "sVE", cv2)
                if relst is not None:
                    W("scalar", "sSTA", relst)
                ACT(oz2d[:, 0:4096], q3v, AF.Copy, bias=-MAGIC, scale=1.0, inc="sSE")
                SDRAIN()
                if na == 1:
                    relst = DMA("scalar", ddst, oz2d[:, 0:4096], "sSTA")
                else:
                    relst = DMA("scalar", ddst, view3(bzp, 0, FP16, 2, ncols), "sSTA")
            wz_last_v = CNT["sVE"]
            wz_last_s = CNT["sSE"]
        wq_stores_done = CNT["sSTA"]

        xr_tile = lambda ct: view(big, ct * 2048, BF16, CH)
        sq_par = lambda p_: view(tzp, p_ * 2048, BF16, CH)
        xbf_par = lambda p_: view(tzp, 4096 + p_ * 2048, BF16, CH)
        aff_t = view(tzp, 8192, F32, CH)
        sqq_par = lambda p_: view(tzp, 12288 + p_ * 1024, BF16, 512)
        a2bc = view(bzp, 0, F32, CH)
        b2bc = view(bzp, 4096, F32, CH)
        rqbc = view(bzp, 8192, F32, CH)
        rkbc = view(bzp, 12288, F32, CH)
        eb_par = lambda p_: view3(big, 61440 + p_ * 8192, BF16, 4, CH)
        eb_sl = lambda p_, m, h: view(big, 61440 + p_ * 8192 + m * 2048 + h * 1024, BF16, 512)


        def x_pass(xR_, c0, consumer, rel_sem):
            rel = {}
            for j in range(8):
                par = j % 2
                if j >= 2:
                    W("sync", rel_sem, rel[par])
                else:
                    W("sync", "sVE", st["xz_v"])
                    W("sync", "sSE", st["xz_s"])
                dst = view3(xz, par * 8192, F32, 2, CH)
                ldc = DMA("sync", dst, xR_[:, 2 * j:2 * j + 2, c0:c0 + CH], "sLD")
                for k in range(2):
                    ct = 2 * j + k
                    xap = view(xz, (par * 2 + k) * 4096, F32, CH)
                    rel[par] = consumer(ct, xap, ldc)
            st["xz_v"] = CNT["sVE"]
            st["xz_s"] = CNT["sSE"]

        def emit_ln_affine(xR_, c0, save_main):
            pS = {0: pSa, 1: pSb}
            pQ = {0: pSc, 1: pSd}
            W("scalar", "sVE", CNT["sVE"])

            def stat_consumer(ct, xap, ldc):
                par = ct % 2
                W("scalar", "sLD", ldc)
                W("scalar", "sPE", st["tz"].get(("st", par), 0))
                ACT(sq_par(par), xap, AF.Square)
                c2 = SC(xbf_par(par), xap, inc="sSE")
                W("tensor", "sSE", c2)
                for h in (0, 1):
                    MM(pS[h][0:1, :], oc1b[:, 0:1], xbf_par(par)[:, h * 512:(h + 1) * 512], ct == 0, ct == 15)
                    cc = MM(pQ[h][0:1, :], oc1b[:, 0:1], sq_par(par)[:, h * 512:(h + 1) * 512], ct == 0, ct == 15,
                            inc="sPE" if h == 1 else None)
                st["tz"][("st", par)] = cc
                return c2
            x_pass(xR_, c0, stat_consumer, "sSE")
            cstats = CNT["sPE"]
            W("vector", "sPE", cstats)
            W("vector", "sSTV", st["statS"])
            for h in (0, 1):
                TS(row(0, h * 512, 512), pS[h][0:1, :], 1.0 / 2048.0, None, aop.mult)
                TS(row(1, h * 512, 512), pQ[h][0:1, :], 1.0 / 2048.0, None, aop.mult)
            TT(row(2), row(0), row(0), aop.mult)
            cv = TT(row(1), row(1), row(2), aop.subtract, inc="sVE")
            W("scalar", "sVE", cv)
            ACT(row(2), row(1), AF.Sqrt, bias=EPS_LN, scale=1.0, inc="sSE")
            W("vector", "sSE", CNT["sSE"])
            RECIP(row(3), row(2))
            TT(row(2), row(1), row(3), aop.mult)
            cv = TT(row(1), row(2), row(3), aop.mult, inc="sVE")
            W("scalar", "sVE", cv)
            ACT(row(2), row(1), AF.Sqrt, bias=EPS_RMS, scale=1.0, inc="sSE")
            W("vector", "sSE", CNT["sSE"])
            RECIP(row(1), row(2))
            TT(row(2), row(3), row(1), aop.mult)
            TT(row(1), row(0), row(2), aop.mult, inc="sVE")
            BCAST(a2bc, row(2), CH)
            BCAST(b2bc, row(1), CH)
            if save_main:
                W("vector", "sPE", CNT["sPE"])
                cbm = TT(row(2), row(0), row(3), aop.mult, inc="sVE", drain=True)
                W("scalar", "sVE", cbm)
                DMA("scalar", statS[0:1, c0:c0 + CH], row(3), "sSTV")
                st["statS"] = DMA("scalar", statS[1:2, c0:c0 + CH], row(2), "sSTV")
            W("vector", "sSE", CNT["sSE"])
            W("vector", "sPE", CNT["sPE"])

            def aff_consumer(ct, xap, ldc):
                W("vector", "sLD", ldc)
                TT(aff_t, xap, a2bc, aop.mult)
                return TT(xr_tile(ct), aff_t, b2bc, aop.subtract, inc="sVE")
            x_pass(xR_, c0, aff_consumer, "sVE")

        def emit_qkv(c0, groups, rowbase, qgrp, kgrp):
            prot = [pA, pB, pC]
            pri = 0
            W("tensor", "sVE", CNT["sVE"])
            for gi, (g, oc0) in enumerate(groups):
                par = gi % 2
                W("gpsimd", "sPE", st["w"].get(("qkv", par), 0))
                W("gpsimd", "sSE", st["w"].get(("qkvS", par), 0))
                wdst = view3(WZ[par], 0, BF16, NT, 512)
                lw = DMA("gpsimd", wdst, wqkvR[:, :, oc0:oc0 + 512], "sWD")
                epar = gi % 2
                W("scalar", "sSTA", st["eb"].get(epar, 0))
                W("tensor", "sWD", lw)
                lastmm = None
                is_q = g in qgrp
                is_k = g in kgrp
                for m in range(4):
                    for h in (0, 1):
                        pp = prot[pri % 3]
                        pri += 1
                        W("tensor", "sSE", st["ps"].get(("qk", id(pp)), 0))
                        for t in range(NT):
                            lastmm = MM(pp[:, :], wdst[:, t, m * 128:(m + 1) * 128],
                                        xr_tile(t)[:, h * 512:(h + 1) * 512], t == 0, t == NT - 1,
                                        inc="sPE" if t == NT - 1 else None)
                        W("scalar", "sPE", lastmm)
                        ce = SC(eb_sl(epar, m, h), pp[:, :], inc="sSE")
                        if is_q or is_k:
                            spar = (m * 2 + h) % 2
                            W("scalar", "sPE", st["tz"].get(("sqq", spar), 0))
                            cs = ACT(sqq_par(spar), pp[:, :], AF.Square, inc="sSE")
                            pQ = (pSa if h == 0 else pSb) if is_q else (pSc if h == 0 else pSd)
                            first = (g == (qgrp[0] if is_q else kgrp[0]) and m == 0)
                            last = (g == (qgrp[-1] if is_q else kgrp[-1]) and m == 3)
                            W("tensor", "sSE", cs)
                            cq = MM(pQ[0:1, :], oc1b[:, 0:1], sqq_par(spar)[:, :], first, last, inc="sPE")
                            st["tz"][("sqq", spar)] = cq
                            ce = cs
                        st["ps"][("qk", id(pp))] = ce
                st["w"][("qkv", par)] = lastmm
                SDRAIN()
                sb = DMA("scalar", qkvSR[:, rowbase + g * 4: rowbase + (g + 1) * 4, c0:c0 + CH], eb_par(epar), "sSTA")
                st["eb"][epar] = sb

        def emit_rms_rows(pH0, pH1, dst_bc, r0, r1):
            W("scalar", "sPE", CNT["sPE"])
            ACT(row(r0, 0, 512), pH0[0:1, :], AF.Sqrt, bias=EPS_RMS, scale=1.0 / 2048.0)
            ACT(row(r0, 512, 512), pH1[0:1, :], AF.Sqrt, bias=EPS_RMS, scale=1.0 / 2048.0, inc="sSE")
            W("vector", "sSE", CNT["sSE"])
            RECIP(row(r1), row(r0), inc="sVE")
            BCAST(dst_bc, row(r1), CH)

        barrier()
        W("gpsimd", "sSE", CNT["sSE"])
        tz0 = view(tzp, 0, F32, CH)
        tz1 = view(tzp, 4096, F32, CH)
        tz2 = view(tzp, 8192, F32, CH)
        tz3 = view(tzp, 12288, F32, CH)
        for pci in range(2):
            c0 = pci * CH
            emit_ln_affine(xpR, c0, save_main=False)
            emit_qkv(c0, [(g, 2048 + g * 512) for g in range(8)], 48, qgrp=[], kgrp=[0, 1, 2, 3])
            emit_rms_rows(pSc, pSd, rkbc, 0, 1)
            W("sync", "sSTA", CNT["sSTA"])
            W("vector", "sSE", CNT["sSE"])
            for ct in range(NT):
                par = ct % 2
                W("sync", "sVE", st["p2"].get(par, 0))
                dst = view3(big, 32768 + par * 6144, BF16, 2, CH)
                lc = DMA("sync", dst, qkvSR[:, 48 + ct:48 + ct + 17:16, c0:c0 + CH], "sLD")
                kap = view(big, 32768 + par * 6144, BF16, CH)
                vap = view(big, 32768 + par * 6144 + 2048, BF16, CH)
                W("vector", "sLD", lc)
                cv = TT(tz0, kap, rkbc, aop.mult, inc="sVE")
                W("scalar", "sVE", cv)
                cs = ACT(tz1, tz0, AF.Exp, inc="sSE")
                W("vector", "sSE", cs)
                TT(tz2, tz1, vap, aop.mult)
                RED(cz[:, pci * 16 + ct: pci * 16 + ct + 1], tz1)
                cr = RED(cz[:, 32 + pci * 16 + ct: 32 + pci * 16 + ct + 1], tz2, inc="sVE")
                st["p2"][par] = cr
        W("vector", "sLD", ldMsk)
        VF(3)
        TT(cz[:, 64:80], cz[:, 0:16], cz[:, 16:32], aop.add)
        VF(3)
        TT(cz[:, 80:96], cz[:, 32:48], cz[:, 48:64], aop.add)
        VF(3)
        TS(cz[:, 96:112], cz[:, 64:80], mskb[:, 0:1], None, aop.mult)
        TS(cz[:, 112:128], cz[:, 80:96], mskb[:, 0:1], None, aop.mult)
        VF(3)

        for ci in range(2):
            c0 = ci * CH
            emit_ln_affine(xmR, c0, save_main=True)
            emit_qkv(c0, [(g, g * 512) for g in range(12)], 0, qgrp=[0, 1, 2, 3], kgrp=[4, 5, 6, 7])
            emit_rms_rows(pSa, pSb, rqbc, 0, 1)
            emit_rms_rows(pSc, pSd, rkbc, 2, 3)
            W("sync", "sSTA", CNT["sSTA"])
            W("vector", "sSE", CNT["sSE"])
            for ct in range(NT):
                par = ct % 2
                ypar = (ct // 4) % 2
                W("sync", "sVE", st["p2"].get(par, 0))
                dst = view3(big, 32768 + par * 6144, BF16, 3, CH)
                lc = DMA("sync", dst, qkvSR[:, ct:ct + 33:16, c0:c0 + CH], "sLD")
                qap = view(big, 32768 + par * 6144, BF16, CH)
                kap = view(big, 32768 + par * 6144 + 2048, BF16, CH)
                vap = view(big, 32768 + par * 6144 + 4096, BF16, CH)
                W("vector", "sLD", lc)
                cv = TT(tz0, kap, rkbc, aop.mult, inc="sVE")
                W("scalar", "sVE", cv)
                cs = ACT(tz1, tz0, AF.Exp, inc="sSE")
                W("vector", "sSE", cs)
                TT(tz0, tz1, vap, aop.mult)
                SCAN(tz2, tz0, cz[:, 112 + ct:113 + ct])
                SCAN(tz3, tz1, cz[:, 96 + ct:97 + ct])
                VC(cz[:, 112 + ct:113 + ct], tz2[:, CH - 1:CH])
                VC(cz[:, 96 + ct:97 + ct], tz3[:, CH - 1:CH])
                TS(tz1, tz3, 1e-6, None, aop.add)
                RECIP(tz0, tz1)
                cq = TT(tz1, qap, rqbc, aop.mult, inc="sVE")
                st["p2"][par] = cq
                W("scalar", "sVE", cq)
                cs = ACT(tz3, tz1, AF.Sigmoid, inc="sSE")
                W("vector", "sSE", cs)
                TT(tz1, tz3, tz2, aop.mult)
                if ct % 4 == 0:
                    W("vector", "sSTV", st["yb"].get(ypar, 0))
                yb = view(big, 45056 + ypar * 8192 + (ct % 4) * 2048, BF16, CH)
                cyb = TT(yb, tz1, tz0, aop.mult, inc="sVE", drain=(ct % 4 == 3))
                if ct % 4 == 3:
                    W("scalar", "sVE", cyb)
                    sb = DMA("scalar", ySR[:, ct - 3:ct + 1, c0:c0 + CH],
                             view3(big, 45056 + ypar * 8192, BF16, 4, CH), "sSTV")
                    st["yb"][ypar] = sb

        ug_tile = lambda j, Cc: view(big, j * 1408, FP16, Cc)
        barrier()
        W("sync", "sVE", CNT["sVE"])
        W("sync", "sSTV", CNT["sSTV"])
        W("sync", "sSTA", CNT["sSTA"])
        for c3i, (o3, Cc) in enumerate(C3LIST):
            hl = Cc // 2
            halves = [(0, hl), (hl, hl)]
            W("sync", "sPE", CNT["sPE"])
            l1 = DMA("sync", szw[0:1, 0:Cc], statS[0:1, o3:o3 + Cc], "sLD")
            l2 = DMA("sync", szw[0:1, 1024:1024 + Cc], statS[1:2, o3:o3 + Cc], "sLD")
            W("vector", "sLD", l2)
            TS(row(2, 0, Cc), row(0, 0, Cc), 1.0, None, aop.add, inc="sVE")
            rs1bc = view(bzp, 0, F32, Cc)
            bmbc = view(bzp, 4096, F32, Cc)
            BCAST(rs1bc, row(2, 0, Cc), Cc)
            BCAST(bmbc, row(1, 1024 - 1024 + 0, Cc) if False else szw[0:1, 1024:1024 + Cc], Cc)
            ly = DMA("sync", view3(yzp, 0, BF16, NT, Cc), ySR[:, :, o3:o3 + Cc], "sLD")
            ytl = lambda t: view(yzp, t * (Cc * 2), BF16, Cc)
            W("vector", "sPE", st["ps"].get("ug", 0))
            W("vector", "sSTA", CNT["sSTA"])
            W("vector", "sSTV", CNT["sSTV"])
            sgap = view(tzp, 0, F32, 352)
            pupair = [pA, pB]
            pgpair = [pC, pBC]
            W("tensor", "sSE", CNT["sSE"])
            W("tensor", "sLD", ly)
            st["w"][("sw", 0)] = st["w"][("sw", 1)] = CNT["sPE"]
            for grp in range(32):
                par = grp % 2
                W("gpsimd", "sPE", st["w"].get(("sw", par), 0))
                du = view3(WZ[par], 0, BF16, NT, 256)
                dg = view3(WZ[par], 8192, BF16, NT, 256)
                DMA("gpsimd", du, wswR[:, :, grp * 256:(grp + 1) * 256], "sWD")
                lw = DMA("gpsimd", dg, wswR[:, :, 8192 + grp * 256: 8192 + (grp + 1) * 256], "sWD")
                W("tensor", "sWD", lw)
                lastmm = None
                for j2 in range(2):
                    j = grp * 2 + j2
                    for hi, (hc0, hn) in enumerate(halves):
                        pu = pupair[(j2 + hi) % 2]
                        pg = pgpair[(j2 + hi) % 2]
                        W("tensor", "sVE", st["ps"].get(("p3u", id(pu)), 0))
                        W("tensor", "sSE", st["ps"].get(("p3g", id(pg)), 0))
                        for t in range(NT):
                            MM(pu[:, 0:hn], du[:, t, j2 * 128:(j2 + 1) * 128], ytl(t)[:, hc0:hc0 + hn],
                               t == 0, t == NT - 1, inc="sPE" if t == NT - 1 else None)
                        for t in range(NT):
                            lastmm = MM(pg[:, 0:hn], dg[:, t, j2 * 128:(j2 + 1) * 128], ytl(t)[:, hc0:hc0 + hn],
                                        t == 0, t == NT - 1, inc="sPE" if t == NT - 1 else None)
                        W("scalar", "sPE", lastmm)
                        W("scalar", "sVE", st["sg"])
                        cs = ACT(sgap[:, 0:hn], pg[:, 0:hn], AF.Silu, inc="sSE")
                        st["ps"][("p3g", id(pg))] = cs
                        W("vector", "sSE", cs)
                        cu = TT(ug_tile(j, Cc)[:, hc0:hc0 + hn], pu[:, 0:hn], sgap[:, 0:hn], aop.mult, inc="sVE")
                        st["sg"] = cu
                        st["ps"][("p3u", id(pu))] = cu
                st["w"][("sw", par)] = lastmm
            pocs = [pA, pB, pC, pBC]
            st["w"][("wo", 0)] = st["w"][("wo", 1)] = CNT["sPE"]
            e1 = view(tzp, 1408, F32, Cc)
            e2 = view(tzp, 4224, F32, Cc)
            x1e = lambda k_: view(tzp, 7040 + k_ * Cc * 4, F32, Cc)
            W("tensor", "sSE", CNT["sSE"])
            W("tensor", "sVE", CNT["sVE"])
            xl = {}
            for ocg in range(8):
                for o2 in range(2):
                    o = ocg * 2 + o2
                    xsl = view(xz, ((ocg % 2) * 2 + o2) * 4096, F32, Cc)
                    W("sync", "sVE", st["ps"].get(("p3x", ocg % 2, o2), st["xz_v"]))
                    xl[o2] = DMA("sync", xsl, xmR[:, o, o3:o3 + Cc], "sLD")
                lastmm = None
                for isg in range(4):
                    par = isg % 2
                    W("gpsimd", "sPE", st["w"].get(("wo", par), 0))
                    dw = view3(WZ[par], 0, FP16, NT, 256)
                    lw = DMA("gpsimd", dw, wouR[:, isg * 16:(isg + 1) * 16, ocg * 256:(ocg + 1) * 256], "sWD")
                    W("tensor", "sWD", lw)
                    if isg == 0:
                        for pp in pocs:
                            W("tensor", "sVE", st["ps"].get(("p3o", id(pp)), 0))
                    for o2 in range(2):
                        for hi, (hc0, hn) in enumerate(halves):
                            pp = pocs[o2 * 2 + hi]
                            for t in range(NT):
                                lastmm = MM(pp[:, 0:hn], dw[:, t, o2 * 128:(o2 + 1) * 128],
                                            ug_tile(isg * 16 + t, Cc)[:, hc0:hc0 + hn],
                                            isg == 0 and t == 0, isg == 3 and t == NT - 1,
                                            inc="sPE" if (t == NT - 1 and o2 == 1 and hi == 1) else None)
                    st["w"][("wo", par)] = lastmm
                W("vector", "sPE", CNT["sPE"])
                for o2 in range(2):
                    xsl = view(xz, ((ocg % 2) * 2 + o2) * 4096, F32, Cc)
                    W("vector", "sLD", xl[o2])
                    TT(e1, xsl, rs1bc, aop.mult)
                    TT(e2, e1, bmbc, aop.subtract)
                    if o2 == 0:
                        W("vector", "sSTV", st["x1e"])
                    for hi, (hc0, hn) in enumerate(halves):
                        pp = pocs[o2 * 2 + hi]
                        cvx = TT(x1e(o2)[:, hc0:hc0 + hn], pp[:, 0:hn], e2[:, hc0:hc0 + hn], aop.add, inc="sVE",
                                 drain=(o2 == 1 and hi == 1))
                        st["ps"][("p3o", id(pp))] = cvx
                        st["ps"][("p3x", ocg % 2, o2)] = cvx
                x1es = view3(tzp, 7040, F32, 2, Cc)
                W("scalar", "sVE", CNT["sVE"])
                st["x1e"] = DMA("scalar", x1SR[:, ocg * 2:(ocg + 1) * 2, o3:o3 + Cc], x1es, "sSTV")
            st["ps"]["ug"] = CNT["sPE"]
        x1_stores = CNT["sSTV"]

        htile = lambda j: view(big, j * 1024, FP16, C4)
        xq1t = lambda t: view(big, 65536 + t * 1024, FP16, C4)
        mm_b = lambda k_: view(tzp, 8192 + k_ * 2048, F32, C4)
        barrier()
        W("sync", "sSTA", wq_stores_done)
        W("sync", "sSTV", x1_stores)
        st["ps"]["ffn1"] = CNT["sPE"]
        st["ps"]["ffn2"] = CNT["sPE"]
        st["w"][("f1", 0)] = st["w"][("f1", 1)] = CNT["sPE"]
        st["w"][("f2", 0)] = st["w"][("f2", 1)] = CNT["sPE"]
        st["xz_v"] = CNT["sVE"]
        st["xz_s"] = CNT["sSE"]

        def ptree():
            E("vector", lambda e: e.drain(), inc="sVE")
            for stp in range(7):
                sz = 64 >> stp
                W("sync", "sVE", CNT["sVE"])
                l1 = DMA("sync", mm_b(1)[0:sz, :], mm_b(0)[sz:2 * sz, :], "sLD")
                l2 = DMA("sync", mm_b(3)[0:sz, :], mm_b(2)[sz:2 * sz, :], "sLD")
                W("vector", "sLD", l2)
                TT(mm_b(0)[0:sz, :], mm_b(0)[0:sz, :], mm_b(1)[0:sz, :], aop.max)
                TT(mm_b(2)[0:sz, :], mm_b(2)[0:sz, :], mm_b(3)[0:sz, :], aop.min, inc="sVE", drain=True)
        for c4i in range(4):
            o4 = c4i * C4

            def x1_pass(consumer, rel_sem):
                rel = {}
                for jj in range(4):
                    par = jj % 2
                    if jj >= 2:
                        W("sync", rel_sem, rel[par])
                    else:
                        W("sync", "sVE", st["xz_v"])
                        W("sync", "sSE", st["xz_s"])
                    dst = view3(xz, par * 8192, F32, 4, C4)
                    ldc = DMA("sync", dst, x1SR[:, 4 * jj:4 * jj + 4, o4:o4 + C4], "sLD")
                    for k in range(4):
                        ct = 4 * jj + k
                        xap = view(xz, par * 8192 + k * 2048, F32, C4)
                        rel[par] = consumer(ct, xap, ldc)
                st["xz_v"] = CNT["sVE"]
                st["xz_s"] = CNT["sSE"]
            def stat_c(ct, xap, ldc):
                par = ct % 2
                W("scalar", "sLD", ldc)
                W("scalar", "sPE", st["tz"].get(("st4", par), 0))
                ACT(view(tzp, par * 2048, BF16, C4), xap, AF.Square)
                c2 = SC(view(tzp, 4096 + par * 2048, BF16, C4), xap, inc="sSE")
                W("tensor", "sSE", c2)
                MM(pSa[0:1, :], oc1b[:, 0:1], view(tzp, 4096 + par * 2048, BF16, C4), ct == 0, ct == 15)
                cc = MM(pSb[0:1, :], oc1b[:, 0:1], view(tzp, par * 2048, BF16, C4), ct == 0, ct == 15, inc="sPE")
                st["tz"][("st4", par)] = cc
                return c2
            x1_pass(stat_c, "sSE")
            W("vector", "sPE", CNT["sPE"])
            TS(row(0, 0, 512), pSa[0:1, :], 1.0 / 2048.0, None, aop.mult)
            TS(row(1, 0, 512), pSb[0:1, :], 1.0 / 2048.0, None, aop.mult)
            TT(row(2, 0, 512), row(0, 0, 512), row(0, 0, 512), aop.mult)
            cv = TT(row(1, 0, 512), row(1, 0, 512), row(2, 0, 512), aop.subtract, inc="sVE")
            W("scalar", "sVE", cv)
            ACT(row(2, 0, 512), row(1, 0, 512), AF.Sqrt, bias=EPS_LN, scale=1.0, inc="sSE")
            W("vector", "sSE", CNT["sSE"])
            RECIP(row(0, 512, 512), row(2, 0, 512), inc="sVE")
            m4bc = view(bzp, 0, F32, C4)
            BCAST(m4bc, row(0, 0, 512), C4)
            W("vector", "sSE", CNT["sSE"])
            dbuf = view(tzp, 0, F32, C4)

            def mm_c(ct, xap, ldc):
                W("vector", "sLD", ldc)
                TT(dbuf, xap, m4bc, aop.subtract)
                if ct == 0:
                    VC(mm_b(0), dbuf)
                    cc = VC(mm_b(2), dbuf, inc="sVE")
                else:
                    TT(mm_b(0), mm_b(0), dbuf, aop.max)
                    cc = TT(mm_b(2), mm_b(2), dbuf, aop.min, inc="sVE")
                return cc
            x1_pass(mm_c, "sVE")
            ptree()
            mxf, mnf = mm_b(0)[0:1, :], mm_b(2)[0:1, :]
            TS(row(2, 0, 512), mnf, -1.0, None, aop.mult)
            TT(row(2, 512, 512), row(2, 0, 512), mxf, aop.max)
            TT(row(1, 0, 512), row(2, 512, 512), row(0, 512, 512), aop.mult)
            TS(row(3, 0, 512), row(1, 0, 512), 1e-5, None, aop.max)
            RECIP(row(3, 512, 512), row(3, 0, 512))
            TS(row(1, 0, 512), row(3, 512, 512), 127.0, None, aop.mult)
            TT(row(1, 512, 512), row(1, 0, 512), row(0, 512, 512), aop.mult)
            TT(row(2, 0, 512), row(0, 0, 512), row(1, 512, 512), aop.mult)
            TS(row(2, 512, 512), row(3, 0, 512), 1.0 / 127.0, None, aop.mult)
            TS(row(3, 512, 512), row(2, 512, 512), szp[0:1, 0:1], None, aop.mult, inc="sVE")
            abc = view(bzp, 2048, F32, C4)
            bbc = view(bzp, 4096, F32, C4)
            d1bc = view(bzp, 6144, F32, C4)
            BCAST(abc, row(1, 512, 512), C4)
            BCAST(bbc, row(2, 0, 512), C4)
            BCAST(d1bc, row(3, 512, 512), C4)
            W("vector", "sSE", CNT["sSE"])
            W("vector", "sPE", st["ps"]["ffn1"])
            t1 = view(tzp, 0, F32, C4)
            t2 = view(tzp, 2048, F32, C4)

            def q_c(ct, xap, ldc):
                W("vector", "sLD", ldc)
                TT(t1, xap, abc, aop.mult)
                TT(t2, t1, bbc, aop.subtract)
                return TS(xq1t(ct), t2, MAGIC, MAGIC, aop.add, aop.subtract, inc="sVE")
            x1_pass(q_c, "sVE")
            xq1_done = CNT["sVE"]
            prot = [pA, pB, pC]
            pri = 0
            zsl = lambda p_: view(tzp, 4096 + p_ * 2048, F32, C4)
            W("scalar", "sPE", st["ps"]["ffn2"])
            W("tensor", "sVE", xq1_done)
            st["w"][("f1", 0)] = st["w"][("f1", 1)] = CNT["sPE"]
            hm_init = False
            for g in range(16):
                par = g % 2
                W("sync", "sPE", st["w"].get(("f1", par), 0))
                dw = view3(WZ[par], 0, FP16, NT, 512)
                lw = DMA("sync", dw, wq1R[:, :, g * 512:(g + 1) * 512], "sLD")
                W("tensor", "sLD", lw)
                lastmm = None
                for j2 in range(4):
                    j = g * 4 + j2
                    pp = prot[pri % 3]
                    pri += 1
                    W("tensor", "sVE", st["ps"].get(("f1", id(pp)), 0))
                    for t in range(NT):
                        lastmm = MM(pp[:, :], dw[:, t, j2 * 128:(j2 + 1) * 128], xq1t(t), t == 0, t == NT - 1,
                                    inc="sPE" if t == NT - 1 else None)
                    zp = j % 2
                    W("vector", "sPE", lastmm)
                    W("vector", "sSE", st["tz"].get(("z", zp), 0))
                    cz1 = TT(zsl(zp), pp[:, :], d1bc, aop.mult, inc="sVE")
                    st["ps"][("f1", id(pp))] = cz1
                    W("scalar", "sVE", cz1)
                    cs = ACT(htile(j), zsl(zp), AF.Gelu, inc="sSE")
                    st["tz"][("z", zp)] = cs
                    W("vector", "sSE", cs)
                    if not hm_init:
                        VC(mm_b(0), htile(j))
                        VC(mm_b(2), htile(j), inc="sVE")
                        hm_init = True
                    else:
                        TT(mm_b(0), mm_b(0), htile(j), aop.max)
                        TT(mm_b(2), mm_b(2), htile(j), aop.min, inc="sVE")
                st["w"][("f1", par)] = lastmm
            st["ps"]["ffn1"] = CNT["sPE"]
            ptree()
            mxf, mnf = mm_b(0)[0:1, :], mm_b(2)[0:1, :]
            TS(row(2, 0, 512), mnf, -1.0, None, aop.mult)
            TT(row(2, 512, 512), row(2, 0, 512), mxf, aop.max)
            TS(row(3, 0, 512), row(2, 512, 512), 1e-5, None, aop.max)
            RECIP(row(3, 512, 512), row(3, 0, 512))
            TS(row(1, 0, 512), row(3, 512, 512), 127.0, None, aop.mult)
            TS(row(2, 512, 512), row(3, 0, 512), 1.0 / 127.0, None, aop.mult)
            TS(row(3, 512, 512), row(2, 512, 512), szp[0:1, 1:2], None, aop.mult, inc="sVE")
            s2bc = view(bzp, 0, F32, C4)
            d2bc = view(bzp, 2048, F32, C4)
            BCAST(s2bc, row(1, 0, 512), C4)
            BCAST(d2bc, row(3, 512, 512), C4)
            W("vector", "sSE", CNT["sSE"])
            for j in range(64):
                TT(t1, htile(j), s2bc, aop.mult)
                TS(htile(j), t1, MAGIC, MAGIC, aop.add, aop.subtract, inc="sVE" if j == 63 else None)
            hq_done = CNT["sVE"]
            pocs = [pA, pB, pC, pBC]
            st["w"][("f2", 0)] = st["w"][("f2", 1)] = CNT["sPE"]
            oeb = lambda k_: view(yzp, k_ * 2048, F32, C4)
            f2l = {}
            for ocg in range(4):
                W("sync", "sVE", st["ps"].get(("f2x", ocg % 2), st["xz_v"]))
                dstx = view3(xz, (ocg % 2) * 8192, F32, 4, C4)
                f2l[ocg] = DMA("sync", dstx, x1SR[:, 4 * ocg:4 * ocg + 4, o4:o4 + C4], "sLD")
                lastmm = None
                for isg in range(4):
                    par = isg % 2
                    W("sync", "sPE", st["w"].get(("f2", par), 0))
                    dw = view3(WZ[par], 0, FP16, NT, 512)
                    lw = DMA("sync", dw, wq2R[:, isg * 16:(isg + 1) * 16, ocg * 512:(ocg + 1) * 512], "sLD")
                    W("tensor", "sLD", lw)
                    if isg == 0:
                        W("tensor", "sVE", hq_done)
                        for pp in pocs:
                            W("tensor", "sVE", st["ps"].get(("f2", id(pp)), 0))
                    for oi in range(4):
                        pp = pocs[oi]
                        for t in range(NT):
                            lastmm = MM(pp[:, :], dw[:, t, oi * 128:(oi + 1) * 128],
                                        htile(isg * 16 + t), isg == 0 and t == 0, isg == 3 and t == NT - 1,
                                        inc="sPE" if (t == NT - 1 and oi == 3) else None)
                    st["w"][("f2", par)] = lastmm
                W("vector", "sPE", CNT["sPE"])
                W("vector", "sLD", f2l[ocg])
                for oi in range(4):
                    o = ocg * 4 + oi
                    pp = pocs[oi]
                    TT(t1, pp[:, :], d2bc, aop.mult)
                    xap = view(xz, (ocg % 2) * 8192 + oi * 2048, F32, C4)
                    if oi % 2 == 0:
                        W("vector", "sSTV", st["oe"])
                    cvx = TT(oeb(oi % 2), t1, xap, aop.add, inc="sVE", drain=(oi % 2 == 1))
                    st["ps"][("f2", id(pp))] = cvx
                    st["ps"][("f2x", ocg % 2)] = cvx
                    if oi % 2 == 1:
                        oes = view3(yzp, 0, F32, 2, C4)
                        W("scalar", "sVE", cvx)
                        st["oe"] = DMA("scalar", outR[:, o - 1:o + 1, o4:o4 + C4], oes, "sSTV")
            st["ps"]["ffn2"] = CNT["sPE"]
            st["xz_v"] = CNT["sVE"]

        if dbg:
            dcz = nc.dram_tensor("dbg_cz", [P, 128], F32, kind="ExternalOutput")
            dsw = nc.dram_tensor("dbg_szw", [1, 4096], F32, kind="ExternalOutput")
            dsp = nc.dram_tensor("dbg_szp", [1, 32], F32, kind="ExternalOutput")
            dh = nc.dram_tensor("dbg_h", [P, 32768], FP16, kind="ExternalOutput")
            dxq = nc.dram_tensor("dbg_xq", [P, 8192], FP16, kind="ExternalOutput")
            W("sync", "sVE", CNT["sVE"])
            W("sync", "sSE", CNT["sSE"])
            W("sync", "sPE", CNT["sPE"])
            DMA("sync", dcz[:, :], cz[:, :], "sLD")
            DMA("sync", dsw[:, :], szw[:, :], "sLD")
            DMA("sync", dsp[:, :], szp[:, :], "sLD")
            DMA("sync", dh[:, :], view(big, 0, FP16, 32768), "sLD")
            DMA("sync", dxq[:, :], view(big, 65536, FP16, 8192), "sLD")
        W("sync", "sSTV", CNT["sSTV"])
        W("sync", "sSTA", CNT["sSTA"])
        W("sync", "sLD", CNT["sLD"])

        for s, v in wait_list:
            assert v <= CNT[s], f"wait {s} {v} > final {CNT[s]}"

        with nc.Block() as block:
            def replay(lst):
                def go(e):
                    for fn, inc, n, drn in lst:
                        ins = fn(e)
                        if inc:
                            if drn:
                                e.drain().then_inc(SEMH[inc], n)
                            else:
                                ins.then_inc(SEMH[inc], n)
                return go
            block.sync(replay(ops["sync"]))
            block.gpsimd(replay(ops["gpsimd"]))
            block.vector(replay(ops["vector"]))
            block.scalar(replay(ops["scalar"]))
            block.tensor(replay(ops["tensor"]))
    return nc


_NC_CACHE = None


def kernel(L0_attn_x, L0_attn_w_qkv, L0_attn_w_swiglu, L0_attn_w_out, L0_ffn_w1, L0_ffn_w2):
    global _NC_CACHE
    x = np.asarray(L0_attn_x, dtype=np.float32)
    wqkvH = np.ascontiguousarray(np.asarray(L0_attn_w_qkv, np.float32).T).reshape(NT, P, 6144)
    wswH = np.ascontiguousarray(np.asarray(L0_attn_w_swiglu, np.float32).T).reshape(NT, P, 16384)
    wouH = np.ascontiguousarray(np.asarray(L0_attn_w_out, np.float32).T).reshape(64, P, D)
    w1H = np.ascontiguousarray(np.asarray(L0_ffn_w1, np.float32).T).reshape(NT, P, 8192)
    w2H = np.ascontiguousarray(np.asarray(L0_ffn_w2, np.float32).T).reshape(64, P, D)
    if _NC_CACHE is None:
        _NC_CACHE = build()
    nc = _NC_CACHE
    in_maps = []
    for c in range(8):
        b, h = c // 2, c % 2
        xT = np.ascontiguousarray(x[b].T)
        xmH = np.ascontiguousarray(xT[:, h * 2048:(h + 1) * 2048]).reshape(NT, P, D)
        xpH = np.ascontiguousarray(xT[:, 0:2048]).reshape(NT, P, D)
        mk = np.full((P, 1), float(h), np.float32)
        in_maps.append({"xm": xmH, "xp": xpH, "msk": mk, "wqkv": wqkvH, "wsw": wswH,
                        "wou": wouH, "w1": w1H, "w2": w2H})
    res = run_bass_kernel_spmd(nc, in_maps, core_ids=list(range(8)))
    outp = np.empty((4, 4096, D), np.float32)
    for c in range(8):
        b, h = c // 2, c % 2
        o = res.results[c]["out"].reshape(D, D)
        outp[b, h * 2048:(h + 1) * 2048, :] = o.T
    return outp


# version 05
# speedup vs baseline: 1.0026x; 1.0026x over previous
import numpy as np
import concourse.bass as bass
import concourse.mybir as mybir
from concourse.bass_utils import run_bass_kernel_spmd
from concourse.alu_op_type import AluOpType as aop
from contextlib import ExitStack

F32 = mybir.dt.float32
BF16 = mybir.dt.bfloat16
FP16 = mybir.dt.float16
AF = mybir.ActivationFunctionType
AX = mybir.AxisListType

P = 128
D = 2048
NT = 16
CH = 1024
MAGIC = float(1.5 * 2.0 ** 23)
EPS_RMS = 1.1920929e-07
EPS_LN = 1e-5
C3LIST = [(0, 704), (704, 704), (1408, 640)]
C4 = 512

ENGS = ["sync", "gpsimd", "vector", "scalar", "tensor"]
SEMS = ["sLD", "sWD", "sSTA", "sSTV", "sPE", "sVE", "sSE"]


def build(dbg=False):
    nc = bass.Bass("TRN2", target_bir_lowering=False)
    SCR = "ExternalOutput" if dbg else "Internal"
    xm = nc.dram_tensor("xm", [NT, P, D], F32, kind="ExternalInput")
    xp = nc.dram_tensor("xp", [NT, P, D], F32, kind="ExternalInput")
    msk = nc.dram_tensor("msk", [P, 1], F32, kind="ExternalInput")
    wqkv = nc.dram_tensor("wqkv", [NT, P, 6144], F32, kind="ExternalInput")
    wsw = nc.dram_tensor("wsw", [NT, P, 16384], F32, kind="ExternalInput")
    wou = nc.dram_tensor("wou", [64, P, D], F32, kind="ExternalInput")
    w1 = nc.dram_tensor("w1", [NT, P, 8192], F32, kind="ExternalInput")
    w2 = nc.dram_tensor("w2", [64, P, D], F32, kind="ExternalInput")
    out = nc.dram_tensor("out", [NT, P, D], F32, kind="ExternalOutput")
    qkvS = nc.dram_tensor("qkvS", [80, P, D], BF16, kind=SCR)
    yS = nc.dram_tensor("yS", [NT, P, D], BF16, kind=SCR)
    x1S = nc.dram_tensor("x1S", [NT, P, D], F32, kind=SCR)
    statS = nc.dram_tensor("statS", [2, D], F32, kind=SCR)
    wq1S = nc.dram_tensor("wq1S", [NT, P, 8192], FP16, kind=SCR)
    wq2S = nc.dram_tensor("wq2S", [64, P, D], FP16, kind=SCR)

    xmR = xm.rearrange("t p c -> p t c")
    xpR = xp.rearrange("t p c -> p t c")
    wqkvR = wqkv.rearrange("t p c -> p t c")
    wswR = wsw.rearrange("t p c -> p t c")
    wouR = wou.rearrange("t p c -> p t c")
    wq1R = wq1S.rearrange("t p c -> p t c")
    wq2R = wq2S.rearrange("t p c -> p t c")
    qkvSR = qkvS.rearrange("t p c -> p t c")
    ySR = yS.rearrange("t p c -> p t c")
    x1SR = x1S.rearrange("t p c -> p t c")
    outR = out.rearrange("t p c -> p t c")

    CNT = {s: 0 for s in SEMS}
    ops = {e: [] for e in ENGS}
    wait_list = []

    def E(eng, fn, inc=None, n=1, drain=False):
        if inc:
            CNT[inc] += n
        ops[eng].append((fn, inc, n, drain))
        return CNT[inc] if inc else None

    def W(eng, sem, val):
        if val is None or val <= 0:
            return
        wait_list.append((sem, val))
        ops[eng].append((lambda e, s=sem, v=val: e.wait_ge(SEMH[s], v), None, 0, False))

    def DMA(eng, o, i, sem):
        return E(eng, lambda e, O=o, I=i: e.dma_start(out=O, in_=I), inc=sem, n=16)

    def TT(o, a, b, alu, inc=None, drain=False):
        return E("vector", lambda e, O=o, A=a, B=b, L=alu: e.tensor_tensor(O, A, B, L), inc=inc, drain=drain)

    def TS(o, a, s1, s2, op0, op1=None, inc=None):
        if op1 is None:
            return E("vector", lambda e, O=o, A=a, X=s1, o0=op0: e.tensor_scalar(O, A, X, None, o0), inc=inc)
        return E("vector", lambda e, O=o, A=a, X=s1, Y=s2, o0=op0, o1=op1: e.tensor_scalar(O, A, X, Y, o0, o1), inc=inc)

    def VC(o, a, inc=None):
        return E("vector", lambda e, O=o, A=a: e.tensor_copy(O, A), inc=inc)

    def RECIP(o, a, inc=None):
        return E("vector", lambda e, O=o, A=a: e.reciprocal(O, A), inc=inc)

    def RED(o, a, inc=None, absv=False):
        return E("vector", lambda e, O=o, A=a, V=absv: e.tensor_reduce(O, A, AX.X, aop.add, apply_absolute_value=V), inc=inc)

    def SCAN(o, a, init, inc=None):
        return E("vector", lambda e, O=o, A=a, I=init: e.tensor_tensor_scan(O, A, A, I, aop.add, aop.bypass), inc=inc)

    def ACT(o, a, f, bias=0.0, scale=1.0, inc=None):
        return E("scalar", lambda e, O=o, A=a, F=f, B=bias, S=scale: e.activation(O, A, F, bias=B, scale=S), inc=inc)

    def SC(o, a, inc=None):
        return E("scalar", lambda e, O=o, A=a: e.copy(O, A), inc=inc)

    def SDRAIN():
        return E("scalar", lambda e: e.drain())

    def MM(o, l, r, st, sp, inc=None):
        return E("tensor", lambda e, O=o, L=l, R=r, S=st, T=sp: e.matmul(O, L, R, start=S, stop=T), inc=inc)

    def MS(eng, ap, val):
        return E(eng, lambda e, A=ap, V=val: e.memset(A, V))

    with ExitStack() as ctx:
        sbt = lambda n_, s, d: ctx.enter_context(nc.sbuf_tensor(n_, s, d))
        pst = lambda n_, s: ctx.enter_context(nc.psum_tensor(n_, s, F32))
        xz = sbt("xz", [P, 4096], F32)
        big = sbt("big", [P, 22528], F32)
        yzp = sbt("yzp", [P, 5632], F32)
        wza = sbt("wza", [P, 4096], F32)
        wzb = sbt("wzb", [P, 4096], F32)
        tzp = sbt("tzp", [P, 4096], F32)
        bzp = sbt("bzp", [P, 4096], F32)
        szw = sbt("szw", [1, 4096], F32)
        szp = sbt("szp", [1, 32], F32)
        cz = sbt("cz", [P, 128], F32)
        oc1 = sbt("oc1", [P, 1], F32)
        oc1b = sbt("oc1b", [P, 1], BF16)
        or1 = sbt("or1", [1, P], F32)
        mskb = sbt("mskb", [P, 1], F32)
        cst = sbt("cst", [P, 8], F32)
        WZ = [wza, wzb]

        pA = pst("pA", [P, 512])
        pB = pst("pB", [P, 512])
        pC = pst("pC", [P, 512])
        pBC = pst("pBC", [P, 512])
        pSa = pst("pSa", [1, 512])
        pSb = pst("pSb", [1, 512])
        pSc = pst("pSc", [1, 512])
        pSd = pst("pSd", [1, 512])

        SEMH = {s: ctx.enter_context(nc.semaphore(s)) for s in SEMS}

        def view(pool, off_b, dt, ncols):
            esz = 4 if dt == F32 else 2
            nf4 = (ncols * esz + 3) // 4
            ap = pool[:, off_b // 4: off_b // 4 + nf4]
            if dt != F32:
                ap = ap.bitcast(dt)
            return ap[:, 0:ncols]

        def view3(pool, off_b, dt, a, c):
            return view(pool, off_b, dt, a * c).rearrange("p (a c) -> p a c", a=a)

        def row(r, c0=0, n=1024):
            return szw[0:1, r * 1024 + c0: r * 1024 + c0 + n]

        def VF(n=3):
            for _ in range(n):
                VC(cst[:, 6:7], cst[:, 7:8])

        def barrier(engs=("vector", "scalar", "tensor")):
            for eng in engs:
                for s in ("sVE", "sSE", "sPE"):
                    W(eng, s, CNT[s])

        MS("vector", oc1[:, :], 1.0)
        MS("vector", oc1b[:, :], 1.0)
        MS("vector", or1[:, :], 1.0)
        MS("vector", cz[:, :], 0.0)
        MS("vector", cst[:, 0:1], EPS_LN)
        MS("vector", cst[:, 1:2], EPS_RMS)
        E("vector", lambda e: e.memset(cst[:, 2:3], -MAGIC), inc="sVE")
        nc.const_aps.register(F32, EPS_LN, cst[:, 0:1])
        nc.const_aps.register(F32, EPS_RMS, cst[:, 1:2])
        nc.const_aps.register(F32, -MAGIC, cst[:, 2:3])
        MS("vector", cst[:, 3:4], MAGIC)
        nc.const_aps.register(F32, MAGIC, cst[:, 3:4])
        ldMsk = DMA("sync", mskb[:, :], msk[:, :], "sLD")

        def BCAST(dst, srcrow, n):
            W("tensor", "sVE", CNT["sVE"])
            c0_ = 0
            while c0_ < n:
                nn = min(512, n - c0_)
                W("tensor", "sSE", CNT["sSE"])
                c = MM(pBC[0:P, 0:nn], or1[0:1, :], srcrow[:, c0_:c0_ + nn], True, True, inc="sPE")
                W("scalar", "sPE", c)
                SC(dst[:, c0_:c0_ + nn], pBC[0:P, 0:nn], inc="sSE")
                c0_ += nn
            return CNT["sSE"]

        st = {"tz": {}, "w": {}, "eb": {}, "ps": {}, "p2": {}, "yb": {}, "sg": 0,
              "x1e": 0, "oe": 0, "xz_v": 0, "xz_s": 0, "statS": 0, "wq_s1": {}}

        wz_last_v = 0
        wz_last_s = 0
        for widx, (wsrc, wdst, nrt, ncols) in enumerate([(w1, wq1S, NT, 8192), (w2, wq2S, 64, D)]):
            nel = float(nrt * P * ncols)
            if ncols == 8192:
                tiles = [(t, 1, c) for t in range(nrt) for c in (0, 4096)]
            else:
                tiles = [(t, 2, 0) for t in range(0, nrt, 2)]
            relv = {}
            for i, (t, na, c0) in enumerate(tiles):
                par = i % 2
                q_ = "sync" if par == 0 else "gpsimd"
                if i >= 2:
                    W(q_, "sVE", relv[par])
                else:
                    W(q_, "sVE", wz_last_v)
                    W(q_, "sSE", wz_last_s)
                if na == 1:
                    src = wsrc[t, :, c0:c0 + 4096]
                else:
                    src = wsrc[t:t + 2, :, :]
                dst = WZ[par][:, 0:4096] if na == 1 else view3(WZ[par], 0, F32, 2, ncols)
                if par == 0:
                    lc = DMA("sync", dst, src, "sLD")
                    W("vector", "sLD", lc)
                else:
                    lc = DMA("gpsimd", dst, src, "sWD")
                    W("vector", "sWD", lc)
                RED(cz[:, 96 + par:97 + par], WZ[par][:, 0:4096], absv=True)
                VF()
                a, b = (98, 99) if i % 2 == 0 else (99, 98)
                relv[par] = TT(cz[:, b:b + 1], cz[:, a:a + 1], cz[:, 96 + par:97 + par], aop.add, inc="sVE")
            wz_last_v = CNT["sVE"]
            fin = 99 if len(tiles) % 2 == 1 else 98
            W("tensor", "sVE", CNT["sVE"])
            c = MM(pSa[0:1, 0:1], oc1[:, 0:1], cz[:, fin:fin + 1], True, True, inc="sPE")
            W("vector", "sPE", c)
            TS(row(0, 0, 1), pSa[0:1, 0:1], 1.0 / nel, 1e-5, aop.mult, aop.max)
            VF()
            VC(szp[0:1, widx:widx + 1], row(0, 0, 1))
            RECIP(row(1, 0, 1), row(0, 0, 1))
            VF()
            VC(szp[0:1, 2 + widx:3 + widx], row(1, 0, 1), inc="sVE")
            BCAST(cst[:, 4 + widx:5 + widx], szp[0:1, 2 + widx:3 + widx], 1)
            E("vector", lambda e: e.memset(cz[:, 98:100], 0.0), inc="sVE")
            swbc = cst[:, 4 + widx:5 + widx]
            oz2d = lambda p_: view(bzp, p_ * 8192, FP16, 4096)
            q1v = lambda p_: view(big, p_ * 16384, F32, 4096)
            q2v = lambda p_: view(big, 32768 + p_ * 16384, F32, 4096)
            q3v = lambda p_: view(big, p_ * 16384, F32, 4096)
            rels = {}
            relst = {}
            for i, (t, na, c0) in enumerate(tiles):
                par = i % 2
                if na == 1:
                    src = wsrc[t, :, c0:c0 + 4096]
                    ddst = wdst[t, :, c0:c0 + 4096]
                else:
                    src = wsrc[t:t + 2, :, :]
                    ddst = wdst[t:t + 2, :, :]
                dst = WZ[par][:, 0:4096] if na == 1 else view3(WZ[par], 0, F32, 2, ncols)
                if par == 0:
                    if i >= 2:
                        W("sync", "sVE", rels[par])
                    else:
                        W("sync", "sVE", wz_last_v)
                        W("sync", "sSE", wz_last_s)
                    lc = DMA("sync", dst, src, "sLD")
                    W("vector", "sLD", lc)
                else:
                    if i >= 2:
                        W("gpsimd", "sVE", rels[par])
                    else:
                        W("gpsimd", "sVE", wz_last_v)
                        W("gpsimd", "sSE", wz_last_s)
                    lc = DMA("gpsimd", dst, src, "sWD")
                    W("vector", "sWD", lc)
                W("vector", "sSE", st["wq_s1"].get(par, 0))
                cv = TS(q1v(par), WZ[par][:, 0:4096], swbc, None, aop.mult, inc="sVE")
                rels[par] = cv
                W("scalar", "sVE", cv)
                ACT(q2v(par), q1v(par), AF.Copy, bias=MAGIC, scale=1.0, inc="sSE")
                W("vector", "sSE", CNT["sSE"])
                cv2 = TS(q3v(par), q2v(par), MAGIC + 1.0, MAGIC - 1.0, aop.min, aop.max, inc="sVE")
                W("scalar", "sVE", cv2)
                if relst.get(par) is not None:
                    W("scalar", "sSTA", relst[par])
                c2 = ACT(oz2d(par)[:, 0:4096], q3v(par), AF.Copy, bias=-MAGIC, scale=1.0, inc="sSE")
                st["wq_s1"][par] = c2
                SDRAIN()
                if na == 1:
                    relst[par] = DMA("scalar", ddst, oz2d(par)[:, 0:4096], "sSTA")
                else:
                    relst[par] = DMA("scalar", ddst, view3(bzp, par * 8192, FP16, 2, ncols), "sSTA")
            wz_last_v = CNT["sVE"]
            wz_last_s = CNT["sSE"]
        wq_stores_done = CNT["sSTA"]

        xr_tile = lambda ct: view(big, ct * 2048, BF16, CH)
        sq_par = lambda p_: view(tzp, p_ * 2048, BF16, CH)
        xbf_par = lambda p_: view(tzp, 4096 + p_ * 2048, BF16, CH)
        aff_t = view(tzp, 8192, F32, CH)
        sqq_par = lambda p_: view(tzp, 12288 + p_ * 1024, BF16, 512)
        a2bc = view(bzp, 0, F32, CH)
        b2bc = view(bzp, 4096, F32, CH)
        rqbc = view(bzp, 8192, F32, CH)
        rkbc = view(bzp, 12288, F32, CH)
        eb_par = lambda p_: view3(big, 61440 + p_ * 8192, BF16, 4, CH)
        eb_sl = lambda p_, m, h: view(big, 61440 + p_ * 8192 + m * 2048 + h * 1024, BF16, 512)


        def x_pass(xR_, c0, consumer, rel_sem):
            rel = {}
            for j in range(8):
                par = j % 2
                if j >= 2:
                    W("sync", rel_sem, rel[par])
                else:
                    W("sync", "sVE", st["xz_v"])
                    W("sync", "sSE", st["xz_s"])
                dst = view3(xz, par * 8192, F32, 2, CH)
                ldc = DMA("sync", dst, xR_[:, 2 * j:2 * j + 2, c0:c0 + CH], "sLD")
                for k in range(2):
                    ct = 2 * j + k
                    xap = view(xz, (par * 2 + k) * 4096, F32, CH)
                    rel[par] = consumer(ct, xap, ldc)
            st["xz_v"] = CNT["sVE"]
            st["xz_s"] = CNT["sSE"]

        def emit_ln_affine(xR_, c0, save_main):
            pS = {0: pSa, 1: pSb}
            pQ = {0: pSc, 1: pSd}
            W("scalar", "sVE", CNT["sVE"])

            def stat_consumer(ct, xap, ldc):
                par = ct % 2
                W("scalar", "sLD", ldc)
                W("scalar", "sPE", st["tz"].get(("st", par), 0))
                ACT(sq_par(par), xap, AF.Square)
                c2 = SC(xbf_par(par), xap, inc="sSE")
                W("tensor", "sSE", c2)
                for h in (0, 1):
                    MM(pS[h][0:1, :], oc1b[:, 0:1], xbf_par(par)[:, h * 512:(h + 1) * 512], ct == 0, ct == 15)
                    cc = MM(pQ[h][0:1, :], oc1b[:, 0:1], sq_par(par)[:, h * 512:(h + 1) * 512], ct == 0, ct == 15,
                            inc="sPE" if h == 1 else None)
                st["tz"][("st", par)] = cc
                return c2
            x_pass(xR_, c0, stat_consumer, "sSE")
            cstats = CNT["sPE"]
            W("vector", "sPE", cstats)
            W("vector", "sSTV", st["statS"])
            for h in (0, 1):
                TS(row(0, h * 512, 512), pS[h][0:1, :], 1.0 / 2048.0, None, aop.mult)
                TS(row(1, h * 512, 512), pQ[h][0:1, :], 1.0 / 2048.0, None, aop.mult)
            TT(row(2), row(0), row(0), aop.mult)
            cv = TT(row(1), row(1), row(2), aop.subtract, inc="sVE")
            W("scalar", "sVE", cv)
            ACT(row(2), row(1), AF.Sqrt, bias=EPS_LN, scale=1.0, inc="sSE")
            W("vector", "sSE", CNT["sSE"])
            RECIP(row(3), row(2))
            TT(row(2), row(1), row(3), aop.mult)
            cv = TT(row(1), row(2), row(3), aop.mult, inc="sVE")
            W("scalar", "sVE", cv)
            ACT(row(2), row(1), AF.Sqrt, bias=EPS_RMS, scale=1.0, inc="sSE")
            W("vector", "sSE", CNT["sSE"])
            RECIP(row(1), row(2))
            TT(row(2), row(3), row(1), aop.mult)
            TT(row(1), row(0), row(2), aop.mult, inc="sVE")
            BCAST(a2bc, row(2), CH)
            BCAST(b2bc, row(1), CH)
            if save_main:
                W("vector", "sPE", CNT["sPE"])
                cbm = TT(row(2), row(0), row(3), aop.mult, inc="sVE", drain=True)
                W("scalar", "sVE", cbm)
                DMA("scalar", statS[0:1, c0:c0 + CH], row(3), "sSTV")
                st["statS"] = DMA("scalar", statS[1:2, c0:c0 + CH], row(2), "sSTV")
            W("vector", "sSE", CNT["sSE"])
            W("vector", "sPE", CNT["sPE"])

            def aff_consumer(ct, xap, ldc):
                W("vector", "sLD", ldc)
                TT(aff_t, xap, a2bc, aop.mult)
                return TT(xr_tile(ct), aff_t, b2bc, aop.subtract, inc="sVE")
            x_pass(xR_, c0, aff_consumer, "sVE")

        def emit_qkv(c0, groups, rowbase, qgrp, kgrp):
            prot = [pA, pB, pC]
            pri = 0
            W("tensor", "sVE", CNT["sVE"])
            for gi, (g, oc0) in enumerate(groups):
                par = gi % 2
                W("gpsimd", "sPE", st["w"].get(("qkv", par), 0))
                W("gpsimd", "sSE", st["w"].get(("qkvS", par), 0))
                wdst = view3(WZ[par], 0, BF16, NT, 512)
                lw = DMA("gpsimd", wdst, wqkvR[:, :, oc0:oc0 + 512], "sWD")
                epar = gi % 2
                W("scalar", "sSTA", st["eb"].get(epar, 0))
                W("tensor", "sWD", lw)
                lastmm = None
                is_q = g in qgrp
                is_k = g in kgrp
                for m in range(4):
                    for h in (0, 1):
                        pp = prot[pri % 3]
                        pri += 1
                        W("tensor", "sSE", st["ps"].get(("qk", id(pp)), 0))
                        for t in range(NT):
                            lastmm = MM(pp[:, :], wdst[:, t, m * 128:(m + 1) * 128],
                                        xr_tile(t)[:, h * 512:(h + 1) * 512], t == 0, t == NT - 1,
                                        inc="sPE" if t == NT - 1 else None)
                        W("scalar", "sPE", lastmm)
                        ce = SC(eb_sl(epar, m, h), pp[:, :], inc="sSE")
                        if is_q or is_k:
                            spar = (m * 2 + h) % 2
                            W("scalar", "sPE", st["tz"].get(("sqq", spar), 0))
                            cs = ACT(sqq_par(spar), pp[:, :], AF.Square, inc="sSE")
                            pQ = (pSa if h == 0 else pSb) if is_q else (pSc if h == 0 else pSd)
                            first = (g == (qgrp[0] if is_q else kgrp[0]) and m == 0)
                            last = (g == (qgrp[-1] if is_q else kgrp[-1]) and m == 3)
                            W("tensor", "sSE", cs)
                            cq = MM(pQ[0:1, :], oc1b[:, 0:1], sqq_par(spar)[:, :], first, last, inc="sPE")
                            st["tz"][("sqq", spar)] = cq
                            ce = cs
                        st["ps"][("qk", id(pp))] = ce
                st["w"][("qkv", par)] = lastmm
                SDRAIN()
                sb = DMA("scalar", qkvSR[:, rowbase + g * 4: rowbase + (g + 1) * 4, c0:c0 + CH], eb_par(epar), "sSTA")
                st["eb"][epar] = sb

        def emit_rms_rows(pH0, pH1, dst_bc, r0, r1):
            W("scalar", "sPE", CNT["sPE"])
            ACT(row(r0, 0, 512), pH0[0:1, :], AF.Sqrt, bias=EPS_RMS, scale=1.0 / 2048.0)
            ACT(row(r0, 512, 512), pH1[0:1, :], AF.Sqrt, bias=EPS_RMS, scale=1.0 / 2048.0, inc="sSE")
            W("vector", "sSE", CNT["sSE"])
            RECIP(row(r1), row(r0), inc="sVE")
            BCAST(dst_bc, row(r1), CH)

        barrier()
        W("gpsimd", "sSE", CNT["sSE"])
        W("gpsimd", "sVE", CNT["sVE"])
        tz0 = view(tzp, 0, F32, CH)
        tz1 = view(tzp, 4096, F32, CH)
        tz2 = view(tzp, 8192, F32, CH)
        tz3 = view(tzp, 12288, F32, CH)
        for pci in range(2):
            c0 = pci * CH
            emit_ln_affine(xpR, c0, save_main=False)
            emit_qkv(c0, [(g, 2048 + g * 512) for g in range(8)], 48, qgrp=[], kgrp=[0, 1, 2, 3])
            emit_rms_rows(pSc, pSd, rkbc, 0, 1)
            W("sync", "sSTA", CNT["sSTA"])
            W("vector", "sSE", CNT["sSE"])

            def ms(ct, k):
                if ct % 2 == 0:
                    return view(tzp, k * 4096, F32, CH) if k < 4 else view(big, 16384, F32, CH)
                return view(big, (k * 4096) if k < 4 else 20480, F32, CH)
            ce = {}
            vp = {}

            def pre1(ct):
                par = ct % 2
                W("sync", "sVE", st["p2"].get(par, 0))
                dst = view3(big, 32768 + par * 6144, BF16, 2, CH)
                lc = DMA("sync", dst, qkvSR[:, 48 + ct:48 + ct + 17:16, c0:c0 + CH], "sLD")
                kap = view(big, 32768 + par * 6144, BF16, CH)
                vp[ct] = view(big, 32768 + par * 6144 + 2048, BF16, CH)
                W("vector", "sLD", lc)
                cv = TT(ms(ct, 0), kap, rkbc, aop.mult, inc="sVE")
                W("scalar", "sVE", cv)
                ce[ct] = ACT(ms(ct, 1), ms(ct, 0), AF.Exp, inc="sSE")

            def pre2(ct):
                W("vector", "sSE", ce[ct])
                cr0 = TT(ms(ct, 0), ms(ct, 1), vp[ct], aop.mult, inc="sVE")
                st["p2"][ct % 2] = cr0
                RED(cz[:, pci * 16 + ct: pci * 16 + ct + 1], ms(ct, 1))
                RED(cz[:, 32 + pci * 16 + ct: 32 + pci * 16 + ct + 1], ms(ct, 0), inc="sVE")
            pre1(0)
            for ct in range(1, NT):
                pre1(ct)
                pre2(ct - 1)
            pre2(NT - 1)
        W("vector", "sLD", ldMsk)
        VF(3)
        TT(cz[:, 64:80], cz[:, 0:16], cz[:, 16:32], aop.add)
        VF(3)
        TT(cz[:, 80:96], cz[:, 32:48], cz[:, 48:64], aop.add)
        VF(3)
        TS(cz[:, 96:112], cz[:, 64:80], mskb[:, 0:1], None, aop.mult)
        TS(cz[:, 112:128], cz[:, 80:96], mskb[:, 0:1], None, aop.mult)
        VF(3)

        for ci in range(2):
            c0 = ci * CH
            emit_ln_affine(xmR, c0, save_main=True)
            emit_qkv(c0, [(g, g * 512) for g in range(12)], 0, qgrp=[0, 1, 2, 3], kgrp=[4, 5, 6, 7])
            emit_rms_rows(pSa, pSb, rqbc, 0, 1)
            emit_rms_rows(pSc, pSd, rkbc, 2, 3)
            W("sync", "sSTA", CNT["sSTA"])
            W("vector", "sSE", CNT["sSE"])

            def ms(ct, k):
                if ct % 2 == 0:
                    return view(tzp, k * 4096, F32, CH) if k < 4 else view(big, 16384, F32, CH)
                return view(big, (k * 4096) if k < 4 else 20480, F32, CH)
            ce = {}
            cse = {}
            qp = {}
            vp = {}

            def mx1(ct):
                par = ct % 2
                W("sync", "sVE", st["p2"].get(par, 0))
                dst = view3(big, 32768 + par * 6144, BF16, 3, CH)
                lc = DMA("sync", dst, qkvSR[:, ct:ct + 33:16, c0:c0 + CH], "sLD")
                qp[ct] = view(big, 32768 + par * 6144, BF16, CH)
                kap = view(big, 32768 + par * 6144 + 2048, BF16, CH)
                vp[ct] = view(big, 32768 + par * 6144 + 4096, BF16, CH)
                W("vector", "sLD", lc)
                cv = TT(ms(ct, 0), kap, rkbc, aop.mult, inc="sVE")
                W("scalar", "sVE", cv)
                ce[ct] = ACT(ms(ct, 1), ms(ct, 0), AF.Exp, inc="sSE")

            def mx2(ct):
                ypar = (ct // 4) % 2
                cq = TT(ms(ct, 2), qp[ct], rqbc, aop.mult, inc="sVE")
                W("scalar", "sVE", cq)
                cse[ct] = ACT(ms(ct, 3), ms(ct, 2), AF.Sigmoid, inc="sSE")
                W("vector", "sSE", ce[ct])
                cwv = TT(ms(ct, 0), ms(ct, 1), vp[ct], aop.mult, inc="sVE")
                st["p2"][ct % 2] = cwv
                SCAN(ms(ct, 4), ms(ct, 0), cz[:, 112 + ct:113 + ct])
                SCAN(ms(ct, 0), ms(ct, 1), cz[:, 96 + ct:97 + ct])
                VC(cz[:, 112 + ct:113 + ct], ms(ct, 4)[:, CH - 1:CH])
                VC(cz[:, 96 + ct:97 + ct], ms(ct, 0)[:, CH - 1:CH])
                TS(ms(ct, 1), ms(ct, 0), 1e-6, None, aop.add)
                RECIP(ms(ct, 0), ms(ct, 1))
                W("vector", "sSE", cse[ct])
                TT(ms(ct, 1), ms(ct, 3), ms(ct, 4), aop.mult)
                if ct % 4 == 0:
                    W("vector", "sSTV", st["yb"].get(ypar, 0))
                yb = view(big, 45056 + ypar * 8192 + (ct % 4) * 2048, BF16, CH)
                cyb = TT(yb, ms(ct, 1), ms(ct, 0), aop.mult, inc="sVE", drain=(ct % 4 == 3))
                if ct % 4 == 3:
                    W("scalar", "sVE", cyb)
                    sb = DMA("scalar", ySR[:, ct - 3:ct + 1, c0:c0 + CH],
                             view3(big, 45056 + ypar * 8192, BF16, 4, CH), "sSTV")
                    st["yb"][ypar] = sb
            mx1(0)
            for ct in range(1, NT):
                mx1(ct)
                mx2(ct - 1)
            mx2(NT - 1)

        ug_tile = lambda j, Cc: view(big, j * 1408, FP16, Cc)
        barrier()
        W("sync", "sVE", CNT["sVE"])
        W("sync", "sSTV", CNT["sSTV"])
        W("sync", "sSTA", CNT["sSTA"])
        for c3i, (o3, Cc) in enumerate(C3LIST):
            hl = Cc // 2
            halves = [(0, hl), (hl, hl)]
            W("sync", "sPE", CNT["sPE"])
            l1 = DMA("sync", szw[0:1, 0:Cc], statS[0:1, o3:o3 + Cc], "sLD")
            l2 = DMA("sync", szw[0:1, 1024:1024 + Cc], statS[1:2, o3:o3 + Cc], "sLD")
            W("vector", "sLD", l2)
            TS(row(2, 0, Cc), row(0, 0, Cc), 1.0, None, aop.add, inc="sVE")
            rs1bc = view(bzp, 0, F32, Cc)
            bmbc = view(bzp, 4096, F32, Cc)
            BCAST(rs1bc, row(2, 0, Cc), Cc)
            BCAST(bmbc, row(1, 1024 - 1024 + 0, Cc) if False else szw[0:1, 1024:1024 + Cc], Cc)
            ly = DMA("sync", view3(yzp, 0, BF16, NT, Cc), ySR[:, :, o3:o3 + Cc], "sLD")
            ytl = lambda t: view(yzp, t * (Cc * 2), BF16, Cc)
            W("vector", "sPE", st["ps"].get("ug", 0))
            W("vector", "sSTA", CNT["sSTA"])
            W("vector", "sSTV", CNT["sSTV"])
            sgap = view(tzp, 0, F32, 352)
            pupair = [pA, pB]
            pgpair = [pC, pBC]
            W("tensor", "sSE", CNT["sSE"])
            W("tensor", "sLD", ly)
            st["w"][("sw", 0)] = st["w"][("sw", 1)] = CNT["sPE"]
            for grp in range(32):
                par = grp % 2
                W("gpsimd", "sPE", st["w"].get(("sw", par), 0))
                du = view3(WZ[par], 0, BF16, NT, 256)
                dg = view3(WZ[par], 8192, BF16, NT, 256)
                DMA("gpsimd", du, wswR[:, :, grp * 256:(grp + 1) * 256], "sWD")
                lw = DMA("gpsimd", dg, wswR[:, :, 8192 + grp * 256: 8192 + (grp + 1) * 256], "sWD")
                W("tensor", "sWD", lw)
                lastmm = None
                for j2 in range(2):
                    j = grp * 2 + j2
                    for hi, (hc0, hn) in enumerate(halves):
                        pu = pupair[(j2 + hi) % 2]
                        pg = pgpair[(j2 + hi) % 2]
                        W("tensor", "sVE", st["ps"].get(("p3u", id(pu)), 0))
                        W("tensor", "sSE", st["ps"].get(("p3g", id(pg)), 0))
                        for t in range(NT):
                            MM(pu[:, 0:hn], du[:, t, j2 * 128:(j2 + 1) * 128], ytl(t)[:, hc0:hc0 + hn],
                               t == 0, t == NT - 1, inc="sPE" if t == NT - 1 else None)
                        for t in range(NT):
                            lastmm = MM(pg[:, 0:hn], dg[:, t, j2 * 128:(j2 + 1) * 128], ytl(t)[:, hc0:hc0 + hn],
                                        t == 0, t == NT - 1, inc="sPE" if t == NT - 1 else None)
                        W("scalar", "sPE", lastmm)
                        W("scalar", "sVE", st["sg"])
                        cs = ACT(sgap[:, 0:hn], pg[:, 0:hn], AF.Silu, inc="sSE")
                        st["ps"][("p3g", id(pg))] = cs
                        W("vector", "sSE", cs)
                        cu = TT(ug_tile(j, Cc)[:, hc0:hc0 + hn], pu[:, 0:hn], sgap[:, 0:hn], aop.mult, inc="sVE")
                        st["sg"] = cu
                        st["ps"][("p3u", id(pu))] = cu
                st["w"][("sw", par)] = lastmm
            pocs = [pA, pB, pC, pBC]
            st["w"][("wo", 0)] = st["w"][("wo", 1)] = CNT["sPE"]
            e1 = view(tzp, 1408, F32, Cc)
            e2 = view(tzp, 4224, F32, Cc)
            x1e = lambda k_: view(tzp, 7040 + k_ * Cc * 4, F32, Cc)
            W("tensor", "sSE", CNT["sSE"])
            W("tensor", "sVE", CNT["sVE"])
            xl = {}
            for ocg in range(8):
                for o2 in range(2):
                    o = ocg * 2 + o2
                    xsl = view(xz, ((ocg % 2) * 2 + o2) * 4096, F32, Cc)
                    W("sync", "sVE", st["ps"].get(("p3x", ocg % 2, o2), st["xz_v"]))
                    xl[o2] = DMA("sync", xsl, xmR[:, o, o3:o3 + Cc], "sLD")
                lastmm = None
                for isg in range(4):
                    par = isg % 2
                    W("gpsimd", "sPE", st["w"].get(("wo", par), 0))
                    dw = view3(WZ[par], 0, FP16, NT, 256)
                    lw = DMA("gpsimd", dw, wouR[:, isg * 16:(isg + 1) * 16, ocg * 256:(ocg + 1) * 256], "sWD")
                    W("tensor", "sWD", lw)
                    if isg == 0:
                        for pp in pocs:
                            W("tensor", "sVE", st["ps"].get(("p3o", id(pp)), 0))
                    for o2 in range(2):
                        for hi, (hc0, hn) in enumerate(halves):
                            pp = pocs[o2 * 2 + hi]
                            for t in range(NT):
                                lastmm = MM(pp[:, 0:hn], dw[:, t, o2 * 128:(o2 + 1) * 128],
                                            ug_tile(isg * 16 + t, Cc)[:, hc0:hc0 + hn],
                                            isg == 0 and t == 0, isg == 3 and t == NT - 1,
                                            inc="sPE" if (t == NT - 1 and o2 == 1 and hi == 1) else None)
                    st["w"][("wo", par)] = lastmm
                W("vector", "sPE", CNT["sPE"])
                for o2 in range(2):
                    xsl = view(xz, ((ocg % 2) * 2 + o2) * 4096, F32, Cc)
                    W("vector", "sLD", xl[o2])
                    TT(e1, xsl, rs1bc, aop.mult)
                    TT(e2, e1, bmbc, aop.subtract)
                    if o2 == 0:
                        W("vector", "sSTV", st["x1e"])
                    for hi, (hc0, hn) in enumerate(halves):
                        pp = pocs[o2 * 2 + hi]
                        cvx = TT(x1e(o2)[:, hc0:hc0 + hn], pp[:, 0:hn], e2[:, hc0:hc0 + hn], aop.add, inc="sVE",
                                 drain=(o2 == 1 and hi == 1))
                        st["ps"][("p3o", id(pp))] = cvx
                        st["ps"][("p3x", ocg % 2, o2)] = cvx
                x1es = view3(tzp, 7040, F32, 2, Cc)
                W("scalar", "sVE", CNT["sVE"])
                st["x1e"] = DMA("scalar", x1SR[:, ocg * 2:(ocg + 1) * 2, o3:o3 + Cc], x1es, "sSTV")
            st["ps"]["ug"] = CNT["sPE"]
        x1_stores = CNT["sSTV"]

        htile = lambda j: view(big, j * 1024, FP16, C4)
        xq1t = lambda t: view(big, 65536 + t * 1024, FP16, C4)
        mm_b = lambda k_: view(tzp, 8192 + k_ * 2048, F32, C4)
        barrier()
        W("sync", "sSTA", wq_stores_done)
        W("sync", "sSTV", x1_stores)
        st["ps"]["ffn1"] = CNT["sPE"]
        st["ps"]["ffn2"] = CNT["sPE"]
        st["w"][("f1", 0)] = st["w"][("f1", 1)] = CNT["sPE"]
        st["w"][("f2", 0)] = st["w"][("f2", 1)] = CNT["sPE"]
        st["xz_v"] = CNT["sVE"]
        st["xz_s"] = CNT["sSE"]

        def ptree():
            E("vector", lambda e: e.drain(), inc="sVE")
            for stp in range(7):
                sz = 64 >> stp
                W("sync", "sVE", CNT["sVE"])
                l1 = DMA("sync", mm_b(1)[0:sz, :], mm_b(0)[sz:2 * sz, :], "sLD")
                l2 = DMA("sync", mm_b(3)[0:sz, :], mm_b(2)[sz:2 * sz, :], "sLD")
                W("vector", "sLD", l2)
                TT(mm_b(0)[0:sz, :], mm_b(0)[0:sz, :], mm_b(1)[0:sz, :], aop.max)
                TT(mm_b(2)[0:sz, :], mm_b(2)[0:sz, :], mm_b(3)[0:sz, :], aop.min, inc="sVE", drain=True)
        for c4i in range(4):
            o4 = c4i * C4

            def x1_pass(consumer, rel_sem):
                rel = {}
                for jj in range(4):
                    par = jj % 2
                    if jj >= 2:
                        W("sync", rel_sem, rel[par])
                    else:
                        W("sync", "sVE", st["xz_v"])
                        W("sync", "sSE", st["xz_s"])
                    dst = view3(xz, par * 8192, F32, 4, C4)
                    ldc = DMA("sync", dst, x1SR[:, 4 * jj:4 * jj + 4, o4:o4 + C4], "sLD")
                    for k in range(4):
                        ct = 4 * jj + k
                        xap = view(xz, par * 8192 + k * 2048, F32, C4)
                        rel[par] = consumer(ct, xap, ldc)
                st["xz_v"] = CNT["sVE"]
                st["xz_s"] = CNT["sSE"]
            def stat_c(ct, xap, ldc):
                par = ct % 2
                W("scalar", "sLD", ldc)
                W("scalar", "sPE", st["tz"].get(("st4", par), 0))
                ACT(view(tzp, par * 2048, BF16, C4), xap, AF.Square)
                c2 = SC(view(tzp, 4096 + par * 2048, BF16, C4), xap, inc="sSE")
                W("tensor", "sSE", c2)
                MM(pSa[0:1, :], oc1b[:, 0:1], view(tzp, 4096 + par * 2048, BF16, C4), ct == 0, ct == 15)
                cc = MM(pSb[0:1, :], oc1b[:, 0:1], view(tzp, par * 2048, BF16, C4), ct == 0, ct == 15, inc="sPE")
                st["tz"][("st4", par)] = cc
                return c2
            x1_pass(stat_c, "sSE")
            W("vector", "sPE", CNT["sPE"])
            TS(row(0, 0, 512), pSa[0:1, :], 1.0 / 2048.0, None, aop.mult)
            TS(row(1, 0, 512), pSb[0:1, :], 1.0 / 2048.0, None, aop.mult)
            TT(row(2, 0, 512), row(0, 0, 512), row(0, 0, 512), aop.mult)
            cv = TT(row(1, 0, 512), row(1, 0, 512), row(2, 0, 512), aop.subtract, inc="sVE")
            W("scalar", "sVE", cv)
            ACT(row(2, 0, 512), row(1, 0, 512), AF.Sqrt, bias=EPS_LN, scale=1.0, inc="sSE")
            W("vector", "sSE", CNT["sSE"])
            RECIP(row(0, 512, 512), row(2, 0, 512), inc="sVE")
            m4bc = view(bzp, 0, F32, C4)
            BCAST(m4bc, row(0, 0, 512), C4)
            W("vector", "sSE", CNT["sSE"])
            dbuf = view(tzp, 0, F32, C4)

            def mm_c(ct, xap, ldc):
                W("vector", "sLD", ldc)
                TT(dbuf, xap, m4bc, aop.subtract)
                if ct == 0:
                    VC(mm_b(0), dbuf)
                    cc = VC(mm_b(2), dbuf, inc="sVE")
                else:
                    TT(mm_b(0), mm_b(0), dbuf, aop.max)
                    cc = TT(mm_b(2), mm_b(2), dbuf, aop.min, inc="sVE")
                return cc
            x1_pass(mm_c, "sVE")
            ptree()
            mxf, mnf = mm_b(0)[0:1, :], mm_b(2)[0:1, :]
            TS(row(2, 0, 512), mnf, -1.0, None, aop.mult)
            TT(row(2, 512, 512), row(2, 0, 512), mxf, aop.max)
            TT(row(1, 0, 512), row(2, 512, 512), row(0, 512, 512), aop.mult)
            TS(row(3, 0, 512), row(1, 0, 512), 1e-5, None, aop.max)
            RECIP(row(3, 512, 512), row(3, 0, 512))
            TS(row(1, 0, 512), row(3, 512, 512), 127.0, None, aop.mult)
            TT(row(1, 512, 512), row(1, 0, 512), row(0, 512, 512), aop.mult)
            TT(row(2, 0, 512), row(0, 0, 512), row(1, 512, 512), aop.mult)
            TS(row(2, 512, 512), row(3, 0, 512), 1.0 / 127.0, None, aop.mult)
            TS(row(3, 512, 512), row(2, 512, 512), szp[0:1, 0:1], None, aop.mult, inc="sVE")
            abc = view(bzp, 2048, F32, C4)
            bbc = view(bzp, 4096, F32, C4)
            d1bc = view(bzp, 6144, F32, C4)
            BCAST(abc, row(1, 512, 512), C4)
            BCAST(bbc, row(2, 0, 512), C4)
            BCAST(d1bc, row(3, 512, 512), C4)
            W("vector", "sSE", CNT["sSE"])
            W("vector", "sPE", st["ps"]["ffn1"])
            t1 = view(tzp, 0, F32, C4)
            t2 = view(tzp, 2048, F32, C4)

            def q_c(ct, xap, ldc):
                W("vector", "sLD", ldc)
                TT(t1, xap, abc, aop.mult)
                TT(t2, t1, bbc, aop.subtract)
                return TS(xq1t(ct), t2, MAGIC, MAGIC, aop.add, aop.subtract, inc="sVE")
            x1_pass(q_c, "sVE")
            xq1_done = CNT["sVE"]
            prot = [pA, pB, pC]
            pri = 0
            zsl = lambda p_: view(tzp, 4096 + p_ * 2048, F32, C4)
            W("scalar", "sPE", st["ps"]["ffn2"])
            W("tensor", "sVE", xq1_done)
            st["w"][("f1", 0)] = st["w"][("f1", 1)] = CNT["sPE"]
            hm_init = False
            for g in range(16):
                par = g % 2
                W("sync", "sPE", st["w"].get(("f1", par), 0))
                dw = view3(WZ[par], 0, FP16, NT, 512)
                lw = DMA("sync", dw, wq1R[:, :, g * 512:(g + 1) * 512], "sLD")
                W("tensor", "sLD", lw)
                lastmm = None
                for j2 in range(4):
                    j = g * 4 + j2
                    pp = prot[pri % 3]
                    pri += 1
                    W("tensor", "sVE", st["ps"].get(("f1", id(pp)), 0))
                    for t in range(NT):
                        lastmm = MM(pp[:, :], dw[:, t, j2 * 128:(j2 + 1) * 128], xq1t(t), t == 0, t == NT - 1,
                                    inc="sPE" if t == NT - 1 else None)
                    zp = j % 2
                    W("vector", "sPE", lastmm)
                    W("vector", "sSE", st["tz"].get(("z", zp), 0))
                    cz1 = TT(zsl(zp), pp[:, :], d1bc, aop.mult, inc="sVE")
                    st["ps"][("f1", id(pp))] = cz1
                    W("scalar", "sVE", cz1)
                    cs = ACT(htile(j), zsl(zp), AF.Gelu, inc="sSE")
                    st["tz"][("z", zp)] = cs
                    W("vector", "sSE", cs)
                    if not hm_init:
                        VC(mm_b(0), htile(j))
                        VC(mm_b(2), htile(j), inc="sVE")
                        hm_init = True
                    else:
                        TT(mm_b(0), mm_b(0), htile(j), aop.max)
                        TT(mm_b(2), mm_b(2), htile(j), aop.min, inc="sVE")
                st["w"][("f1", par)] = lastmm
            st["ps"]["ffn1"] = CNT["sPE"]
            ptree()
            mxf, mnf = mm_b(0)[0:1, :], mm_b(2)[0:1, :]
            TS(row(2, 0, 512), mnf, -1.0, None, aop.mult)
            TT(row(2, 512, 512), row(2, 0, 512), mxf, aop.max)
            TS(row(3, 0, 512), row(2, 512, 512), 1e-5, None, aop.max)
            RECIP(row(3, 512, 512), row(3, 0, 512))
            TS(row(1, 0, 512), row(3, 512, 512), 127.0, None, aop.mult)
            TS(row(2, 512, 512), row(3, 0, 512), 1.0 / 127.0, None, aop.mult)
            TS(row(3, 512, 512), row(2, 512, 512), szp[0:1, 1:2], None, aop.mult, inc="sVE")
            s2bc = view(bzp, 0, F32, C4)
            d2bc = view(bzp, 2048, F32, C4)
            BCAST(s2bc, row(1, 0, 512), C4)
            BCAST(d2bc, row(3, 512, 512), C4)
            W("vector", "sSE", CNT["sSE"])
            for j in range(64):
                TT(t1, htile(j), s2bc, aop.mult)
                TS(htile(j), t1, MAGIC, MAGIC, aop.add, aop.subtract, inc="sVE" if j == 63 else None)
            hq_done = CNT["sVE"]
            pocs = [pA, pB, pC, pBC]
            st["w"][("f2", 0)] = st["w"][("f2", 1)] = CNT["sPE"]
            oeb = lambda k_: view(yzp, k_ * 2048, F32, C4)
            f2l = {}
            for ocg in range(4):
                W("sync", "sVE", st["ps"].get(("f2x", ocg % 2), st["xz_v"]))
                dstx = view3(xz, (ocg % 2) * 8192, F32, 4, C4)
                f2l[ocg] = DMA("sync", dstx, x1SR[:, 4 * ocg:4 * ocg + 4, o4:o4 + C4], "sLD")
                lastmm = None
                for isg in range(4):
                    par = isg % 2
                    W("sync", "sPE", st["w"].get(("f2", par), 0))
                    dw = view3(WZ[par], 0, FP16, NT, 512)
                    lw = DMA("sync", dw, wq2R[:, isg * 16:(isg + 1) * 16, ocg * 512:(ocg + 1) * 512], "sLD")
                    W("tensor", "sLD", lw)
                    if isg == 0:
                        W("tensor", "sVE", hq_done)
                        for pp in pocs:
                            W("tensor", "sVE", st["ps"].get(("f2", id(pp)), 0))
                    for oi in range(4):
                        pp = pocs[oi]
                        for t in range(NT):
                            lastmm = MM(pp[:, :], dw[:, t, oi * 128:(oi + 1) * 128],
                                        htile(isg * 16 + t), isg == 0 and t == 0, isg == 3 and t == NT - 1,
                                        inc="sPE" if (t == NT - 1 and oi == 3) else None)
                    st["w"][("f2", par)] = lastmm
                W("vector", "sPE", CNT["sPE"])
                W("vector", "sLD", f2l[ocg])
                for oi in range(4):
                    o = ocg * 4 + oi
                    pp = pocs[oi]
                    TT(t1, pp[:, :], d2bc, aop.mult)
                    xap = view(xz, (ocg % 2) * 8192 + oi * 2048, F32, C4)
                    if oi % 2 == 0:
                        W("vector", "sSTV", st["oe"])
                    cvx = TT(oeb(oi % 2), t1, xap, aop.add, inc="sVE", drain=(oi % 2 == 1))
                    st["ps"][("f2", id(pp))] = cvx
                    st["ps"][("f2x", ocg % 2)] = cvx
                    if oi % 2 == 1:
                        oes = view3(yzp, 0, F32, 2, C4)
                        W("scalar", "sVE", cvx)
                        st["oe"] = DMA("scalar", outR[:, o - 1:o + 1, o4:o4 + C4], oes, "sSTV")
            st["ps"]["ffn2"] = CNT["sPE"]
            st["xz_v"] = CNT["sVE"]

        if dbg:
            dcz = nc.dram_tensor("dbg_cz", [P, 128], F32, kind="ExternalOutput")
            dsw = nc.dram_tensor("dbg_szw", [1, 4096], F32, kind="ExternalOutput")
            dsp = nc.dram_tensor("dbg_szp", [1, 32], F32, kind="ExternalOutput")
            dh = nc.dram_tensor("dbg_h", [P, 32768], FP16, kind="ExternalOutput")
            dxq = nc.dram_tensor("dbg_xq", [P, 8192], FP16, kind="ExternalOutput")
            W("sync", "sVE", CNT["sVE"])
            W("sync", "sSE", CNT["sSE"])
            W("sync", "sPE", CNT["sPE"])
            DMA("sync", dcz[:, :], cz[:, :], "sLD")
            DMA("sync", dsw[:, :], szw[:, :], "sLD")
            DMA("sync", dsp[:, :], szp[:, :], "sLD")
            DMA("sync", dh[:, :], view(big, 0, FP16, 32768), "sLD")
            DMA("sync", dxq[:, :], view(big, 65536, FP16, 8192), "sLD")
        W("sync", "sSTV", CNT["sSTV"])
        W("sync", "sSTA", CNT["sSTA"])
        W("sync", "sLD", CNT["sLD"])

        for s, v in wait_list:
            assert v <= CNT[s], f"wait {s} {v} > final {CNT[s]}"

        with nc.Block() as block:
            def replay(lst):
                def go(e):
                    for fn, inc, n, drn in lst:
                        ins = fn(e)
                        if inc:
                            if drn:
                                e.drain().then_inc(SEMH[inc], n)
                            else:
                                ins.then_inc(SEMH[inc], n)
                return go
            block.sync(replay(ops["sync"]))
            block.gpsimd(replay(ops["gpsimd"]))
            block.vector(replay(ops["vector"]))
            block.scalar(replay(ops["scalar"]))
            block.tensor(replay(ops["tensor"]))
    return nc


_NC_CACHE = None


def kernel(L0_attn_x, L0_attn_w_qkv, L0_attn_w_swiglu, L0_attn_w_out, L0_ffn_w1, L0_ffn_w2):
    global _NC_CACHE
    x = np.asarray(L0_attn_x, dtype=np.float32)
    wqkvH = np.ascontiguousarray(np.asarray(L0_attn_w_qkv, np.float32).T).reshape(NT, P, 6144)
    wswH = np.ascontiguousarray(np.asarray(L0_attn_w_swiglu, np.float32).T).reshape(NT, P, 16384)
    wouH = np.ascontiguousarray(np.asarray(L0_attn_w_out, np.float32).T).reshape(64, P, D)
    w1H = np.ascontiguousarray(np.asarray(L0_ffn_w1, np.float32).T).reshape(NT, P, 8192)
    w2H = np.ascontiguousarray(np.asarray(L0_ffn_w2, np.float32).T).reshape(64, P, D)
    if _NC_CACHE is None:
        _NC_CACHE = build()
    nc = _NC_CACHE
    in_maps = []
    for c in range(8):
        b, h = c // 2, c % 2
        xT = np.ascontiguousarray(x[b].T)
        xmH = np.ascontiguousarray(xT[:, h * 2048:(h + 1) * 2048]).reshape(NT, P, D)
        xpH = np.ascontiguousarray(xT[:, 0:2048]).reshape(NT, P, D)
        mk = np.full((P, 1), float(h), np.float32)
        in_maps.append({"xm": xmH, "xp": xpH, "msk": mk, "wqkv": wqkvH, "wsw": wswH,
                        "wou": wouH, "w1": w1H, "w2": w2H})
    res = run_bass_kernel_spmd(nc, in_maps, core_ids=list(range(8)))
    outp = np.empty((4, 4096, D), np.float32)
    for c in range(8):
        b, h = c // 2, c % 2
        o = res.results[c]["out"].reshape(D, D)
        outp[b, h * 2048:(h + 1) * 2048, :] = o.T
    return outp


# version 06 (final)
# speedup vs baseline: 1.0179x; 1.0153x over previous
import numpy as np
import concourse.bass as bass
import concourse.mybir as mybir
from concourse.bass_utils import run_bass_kernel_spmd
from concourse.alu_op_type import AluOpType as aop
from contextlib import ExitStack

F32 = mybir.dt.float32
BF16 = mybir.dt.bfloat16
FP16 = mybir.dt.float16
AF = mybir.ActivationFunctionType
AX = mybir.AxisListType

P = 128
D = 2048
NT = 16
CH = 1024
MAGIC = float(1.5 * 2.0 ** 23)
EPS_RMS = 1.1920929e-07
EPS_LN = 1e-5
C3LIST = [(0, 704), (704, 704), (1408, 640)]
C4 = 512

ENGS = ["sync", "gpsimd", "vector", "scalar", "tensor"]
SEMS = ["sLD", "sWD", "sSTA", "sSTV", "sPE", "sVE", "sSE"]


def build(dbg=False):
    nc = bass.Bass("TRN2", target_bir_lowering=False)
    SCR = "ExternalOutput" if dbg else "Internal"
    xm = nc.dram_tensor("xm", [NT, P, D], F32, kind="ExternalInput")
    xp = nc.dram_tensor("xp", [NT, P, D], F32, kind="ExternalInput")
    msk = nc.dram_tensor("msk", [P, 1], F32, kind="ExternalInput")
    wqkv = nc.dram_tensor("wqkv", [NT, P, 6144], F32, kind="ExternalInput")
    wsw = nc.dram_tensor("wsw", [NT, P, 16384], F32, kind="ExternalInput")
    wou = nc.dram_tensor("wou", [64, P, D], F32, kind="ExternalInput")
    w1 = nc.dram_tensor("w1", [NT, P, 8192], F32, kind="ExternalInput")
    w2 = nc.dram_tensor("w2", [64, P, D], F32, kind="ExternalInput")
    out = nc.dram_tensor("out", [NT, P, D], F32, kind="ExternalOutput")
    qkvS = nc.dram_tensor("qkvS", [80, P, D], BF16, kind=SCR)
    yS = nc.dram_tensor("yS", [NT, P, D], BF16, kind=SCR)
    x1S = nc.dram_tensor("x1S", [NT, P, D], F32, kind=SCR)
    statS = nc.dram_tensor("statS", [2, D], F32, kind=SCR)
    wq1S = nc.dram_tensor("wq1S", [NT, P, 8192], FP16, kind=SCR)
    wq2S = nc.dram_tensor("wq2S", [64, P, D], FP16, kind=SCR)

    xmR = xm.rearrange("t p c -> p t c")
    xpR = xp.rearrange("t p c -> p t c")
    wqkvR = wqkv.rearrange("t p c -> p t c")
    wswR = wsw.rearrange("t p c -> p t c")
    wouR = wou.rearrange("t p c -> p t c")
    wq1R = wq1S.rearrange("t p c -> p t c")
    wq2R = wq2S.rearrange("t p c -> p t c")
    qkvSR = qkvS.rearrange("t p c -> p t c")
    ySR = yS.rearrange("t p c -> p t c")
    x1SR = x1S.rearrange("t p c -> p t c")
    outR = out.rearrange("t p c -> p t c")

    CNT = {s: 0 for s in SEMS}
    ops = {e: [] for e in ENGS}
    wait_list = []

    def E(eng, fn, inc=None, n=1, drain=False):
        if inc:
            CNT[inc] += n
        ops[eng].append((fn, inc, n, drain))
        return CNT[inc] if inc else None

    def W(eng, sem, val):
        if val is None or val <= 0:
            return
        wait_list.append((sem, val))
        ops[eng].append((lambda e, s=sem, v=val: e.wait_ge(SEMH[s], v), None, 0, False))

    def DMA(eng, o, i, sem):
        return E(eng, lambda e, O=o, I=i: e.dma_start(out=O, in_=I), inc=sem, n=16)

    def TT(o, a, b, alu, inc=None, drain=False):
        return E("vector", lambda e, O=o, A=a, B=b, L=alu: e.tensor_tensor(O, A, B, L), inc=inc, drain=drain)

    def TS(o, a, s1, s2, op0, op1=None, inc=None):
        if op1 is None:
            return E("vector", lambda e, O=o, A=a, X=s1, o0=op0: e.tensor_scalar(O, A, X, None, o0), inc=inc)
        return E("vector", lambda e, O=o, A=a, X=s1, Y=s2, o0=op0, o1=op1: e.tensor_scalar(O, A, X, Y, o0, o1), inc=inc)

    def VC(o, a, inc=None):
        return E("vector", lambda e, O=o, A=a: e.tensor_copy(O, A), inc=inc)

    def RECIP(o, a, inc=None):
        return E("vector", lambda e, O=o, A=a: e.reciprocal(O, A), inc=inc)

    def RED(o, a, inc=None, absv=False):
        return E("vector", lambda e, O=o, A=a, V=absv: e.tensor_reduce(O, A, AX.X, aop.add, apply_absolute_value=V), inc=inc)

    def SCAN(o, a, init, inc=None):
        return E("vector", lambda e, O=o, A=a, I=init: e.tensor_tensor_scan(O, A, A, I, aop.add, aop.bypass), inc=inc)

    def ACT(o, a, f, bias=0.0, scale=1.0, inc=None):
        return E("scalar", lambda e, O=o, A=a, F=f, B=bias, S=scale: e.activation(O, A, F, bias=B, scale=S), inc=inc)

    def SC(o, a, inc=None):
        return E("scalar", lambda e, O=o, A=a: e.copy(O, A), inc=inc)

    def SDRAIN():
        return E("scalar", lambda e: e.drain())

    def MM(o, l, r, st, sp, inc=None):
        return E("tensor", lambda e, O=o, L=l, R=r, S=st, T=sp: e.matmul(O, L, R, start=S, stop=T), inc=inc)

    def MS(eng, ap, val):
        return E(eng, lambda e, A=ap, V=val: e.memset(A, V))

    with ExitStack() as ctx:
        sbt = lambda n_, s, d: ctx.enter_context(nc.sbuf_tensor(n_, s, d))
        pst = lambda n_, s: ctx.enter_context(nc.psum_tensor(n_, s, F32))
        xz = sbt("xz", [P, 4096], F32)
        big = sbt("big", [P, 22528], F32)
        yzp = sbt("yzp", [P, 5632], F32)
        wza = sbt("wza", [P, 4096], F32)
        wzb = sbt("wzb", [P, 4096], F32)
        tzp = sbt("tzp", [P, 4096], F32)
        bzp = sbt("bzp", [P, 4096], F32)
        szw = sbt("szw", [1, 4096], F32)
        szp = sbt("szp", [1, 32], F32)
        cz = sbt("cz", [P, 128], F32)
        oc1 = sbt("oc1", [P, 1], F32)
        oc1b = sbt("oc1b", [P, 1], BF16)
        or1 = sbt("or1", [1, P], F32)
        mskb = sbt("mskb", [P, 1], F32)
        cst = sbt("cst", [P, 8], F32)
        WZ = [wza, wzb]

        pA = pst("pA", [P, 512])
        pB = pst("pB", [P, 512])
        pC = pst("pC", [P, 512])
        pBC = pst("pBC", [P, 512])
        pSa = pst("pSa", [1, 512])
        pSb = pst("pSb", [1, 512])
        pSc = pst("pSc", [1, 512])
        pSd = pst("pSd", [1, 512])

        SEMH = {s: ctx.enter_context(nc.semaphore(s)) for s in SEMS}

        def view(pool, off_b, dt, ncols):
            esz = 4 if dt == F32 else 2
            nf4 = (ncols * esz + 3) // 4
            ap = pool[:, off_b // 4: off_b // 4 + nf4]
            if dt != F32:
                ap = ap.bitcast(dt)
            return ap[:, 0:ncols]

        def view3(pool, off_b, dt, a, c):
            return view(pool, off_b, dt, a * c).rearrange("p (a c) -> p a c", a=a)

        def row(r, c0=0, n=1024):
            return szw[0:1, r * 1024 + c0: r * 1024 + c0 + n]

        def VF(n=3):
            for _ in range(n):
                VC(cst[:, 6:7], cst[:, 7:8])

        def barrier(engs=("vector", "scalar", "tensor")):
            for eng in engs:
                for s in ("sVE", "sSE", "sPE"):
                    W(eng, s, CNT[s])

        MS("vector", oc1[:, :], 1.0)
        MS("vector", oc1b[:, :], 1.0)
        MS("vector", or1[:, :], 1.0)
        MS("vector", cz[:, :], 0.0)
        MS("vector", cst[:, 0:1], EPS_LN)
        MS("vector", cst[:, 1:2], EPS_RMS)
        E("vector", lambda e: e.memset(cst[:, 2:3], -MAGIC), inc="sVE")
        nc.const_aps.register(F32, EPS_LN, cst[:, 0:1])
        nc.const_aps.register(F32, EPS_RMS, cst[:, 1:2])
        nc.const_aps.register(F32, -MAGIC, cst[:, 2:3])
        MS("vector", cst[:, 3:4], MAGIC)
        nc.const_aps.register(F32, MAGIC, cst[:, 3:4])
        ldMsk = DMA("sync", mskb[:, :], msk[:, :], "sLD")

        def BCAST(dst, srcrow, n):
            W("tensor", "sVE", CNT["sVE"])
            c0_ = 0
            while c0_ < n:
                nn = min(512, n - c0_)
                W("tensor", "sSE", CNT["sSE"])
                c = MM(pBC[0:P, 0:nn], or1[0:1, :], srcrow[:, c0_:c0_ + nn], True, True, inc="sPE")
                W("scalar", "sPE", c)
                SC(dst[:, c0_:c0_ + nn], pBC[0:P, 0:nn], inc="sSE")
                c0_ += nn
            return CNT["sSE"]

        st = {"tz": {}, "w": {}, "eb": {}, "ps": {}, "p2": {}, "yb": {}, "sg": 0,
              "x1e": 0, "oe": 0, "xz_v": 0, "xz_s": 0, "statS": 0, "wq_s1": {}}

        wz_last_v = 0
        wz_last_s = 0
        for widx, (wsrc, wdst, nrt, ncols) in enumerate([(w1, wq1S, NT, 8192), (w2, wq2S, 64, D)]):
            nel = float(nrt * P * ncols)
            if ncols == 8192:
                tiles = [(t, 1, c) for t in range(nrt) for c in (0, 4096)]
            else:
                tiles = [(t, 2, 0) for t in range(0, nrt, 2)]
            relv = {}
            for i, (t, na, c0) in enumerate(tiles):
                par = i % 2
                q_ = "sync" if par == 0 else "gpsimd"
                if i >= 2:
                    W(q_, "sVE", relv[par])
                else:
                    W(q_, "sVE", wz_last_v)
                    W(q_, "sSE", wz_last_s)
                if na == 1:
                    src = wsrc[t, :, c0:c0 + 4096]
                else:
                    src = wsrc[t:t + 2, :, :]
                dst = WZ[par][:, 0:4096] if na == 1 else view3(WZ[par], 0, F32, 2, ncols)
                if par == 0:
                    lc = DMA("sync", dst, src, "sLD")
                    W("vector", "sLD", lc)
                else:
                    lc = DMA("gpsimd", dst, src, "sWD")
                    W("vector", "sWD", lc)
                RED(cz[:, 96 + par:97 + par], WZ[par][:, 0:4096], absv=True)
                VF()
                a, b = (98, 99) if i % 2 == 0 else (99, 98)
                relv[par] = TT(cz[:, b:b + 1], cz[:, a:a + 1], cz[:, 96 + par:97 + par], aop.add, inc="sVE")
            wz_last_v = CNT["sVE"]
            fin = 99 if len(tiles) % 2 == 1 else 98
            W("tensor", "sVE", CNT["sVE"])
            c = MM(pSa[0:1, 0:1], oc1[:, 0:1], cz[:, fin:fin + 1], True, True, inc="sPE")
            W("vector", "sPE", c)
            TS(row(0, 0, 1), pSa[0:1, 0:1], 1.0 / nel, 1e-5, aop.mult, aop.max)
            VF()
            VC(szp[0:1, widx:widx + 1], row(0, 0, 1))
            RECIP(row(1, 0, 1), row(0, 0, 1))
            VF()
            VC(szp[0:1, 2 + widx:3 + widx], row(1, 0, 1), inc="sVE")
            BCAST(cst[:, 4 + widx:5 + widx], szp[0:1, 2 + widx:3 + widx], 1)
            E("vector", lambda e: e.memset(cz[:, 98:100], 0.0), inc="sVE")
            swbc = cst[:, 4 + widx:5 + widx]
            oz2d = lambda p_: view(bzp, p_ * 8192, FP16, 4096)
            q1v = lambda p_: view(big, p_ * 16384, F32, 4096)
            q2v = lambda p_: view(big, 32768 + p_ * 16384, F32, 4096)
            q3v = lambda p_: view(big, p_ * 16384, F32, 4096)
            rels = {}
            relst = {}
            for i, (t, na, c0) in enumerate(tiles):
                par = i % 2
                if na == 1:
                    src = wsrc[t, :, c0:c0 + 4096]
                    ddst = wdst[t, :, c0:c0 + 4096]
                else:
                    src = wsrc[t:t + 2, :, :]
                    ddst = wdst[t:t + 2, :, :]
                dst = WZ[par][:, 0:4096] if na == 1 else view3(WZ[par], 0, F32, 2, ncols)
                if par == 0:
                    if i >= 2:
                        W("sync", "sVE", rels[par])
                    else:
                        W("sync", "sVE", wz_last_v)
                        W("sync", "sSE", wz_last_s)
                    lc = DMA("sync", dst, src, "sLD")
                    W("vector", "sLD", lc)
                else:
                    if i >= 2:
                        W("gpsimd", "sVE", rels[par])
                    else:
                        W("gpsimd", "sVE", wz_last_v)
                        W("gpsimd", "sSE", wz_last_s)
                    lc = DMA("gpsimd", dst, src, "sWD")
                    W("vector", "sWD", lc)
                W("vector", "sSE", st["wq_s1"].get(par, 0))
                cv = TS(q1v(par), WZ[par][:, 0:4096], swbc, None, aop.mult, inc="sVE")
                rels[par] = cv
                W("scalar", "sVE", cv)
                ACT(q2v(par), q1v(par), AF.Copy, bias=MAGIC, scale=1.0, inc="sSE")
                W("vector", "sSE", CNT["sSE"])
                cv2 = TS(q3v(par), q2v(par), MAGIC + 1.0, MAGIC - 1.0, aop.min, aop.max, inc="sVE")
                W("scalar", "sVE", cv2)
                if relst.get(par) is not None:
                    W("scalar", "sSTA", relst[par])
                c2 = ACT(oz2d(par)[:, 0:4096], q3v(par), AF.Copy, bias=-MAGIC, scale=1.0, inc="sSE")
                st["wq_s1"][par] = c2
                SDRAIN()
                if na == 1:
                    relst[par] = DMA("scalar", ddst, oz2d(par)[:, 0:4096], "sSTA")
                else:
                    relst[par] = DMA("scalar", ddst, view3(bzp, par * 8192, FP16, 2, ncols), "sSTA")
            wz_last_v = CNT["sVE"]
            wz_last_s = CNT["sSE"]
        wq_stores_done = CNT["sSTA"]

        xr_tile = lambda ct: view(big, ct * 2048, BF16, CH)
        sq_par = lambda p_: view(tzp, p_ * 2048, BF16, CH)
        xbf_par = lambda p_: view(tzp, 4096 + p_ * 2048, BF16, CH)
        aff_t = view(tzp, 8192, F32, CH)
        sqq_par = lambda p_: view(tzp, 12288 + p_ * 1024, BF16, 512)
        a2bc = view(bzp, 0, F32, CH)
        b2bc = view(bzp, 4096, F32, CH)
        rqbc = view(bzp, 8192, F32, CH)
        rkbc = view(bzp, 12288, F32, CH)
        eb_par = lambda p_: view3(big, 61440 + p_ * 8192, BF16, 4, CH)
        eb_sl = lambda p_, m, h: view(big, 61440 + p_ * 8192 + m * 2048 + h * 1024, BF16, 512)


        def x_pass(xR_, c0, consumer, rel_sem):
            rel = {}
            for j in range(8):
                par = j % 2
                if j >= 2:
                    W("sync", rel_sem, rel[par])
                else:
                    W("sync", "sVE", st["xz_v"])
                    W("sync", "sSE", st["xz_s"])
                dst = view3(xz, par * 8192, F32, 2, CH)
                ldc = DMA("sync", dst, xR_[:, 2 * j:2 * j + 2, c0:c0 + CH], "sLD")
                for k in range(2):
                    ct = 2 * j + k
                    xap = view(xz, (par * 2 + k) * 4096, F32, CH)
                    rel[par] = consumer(ct, xap, ldc)
            st["xz_v"] = CNT["sVE"]
            st["xz_s"] = CNT["sSE"]

        def emit_ln_affine(xR_, c0, save_main):
            pS = {0: pSa, 1: pSb}
            pQ = {0: pSc, 1: pSd}
            W("scalar", "sVE", CNT["sVE"])

            def stat_consumer(ct, xap, ldc):
                par = ct % 2
                W("scalar", "sLD", ldc)
                W("scalar", "sPE", st["tz"].get(("st", par), 0))
                ACT(sq_par(par), xap, AF.Square)
                c2 = SC(xbf_par(par), xap, inc="sSE")
                W("tensor", "sSE", c2)
                for h in (0, 1):
                    MM(pS[h][0:1, :], oc1b[:, 0:1], xbf_par(par)[:, h * 512:(h + 1) * 512], ct == 0, ct == 15)
                    cc = MM(pQ[h][0:1, :], oc1b[:, 0:1], sq_par(par)[:, h * 512:(h + 1) * 512], ct == 0, ct == 15,
                            inc="sPE" if h == 1 else None)
                st["tz"][("st", par)] = cc
                return c2
            x_pass(xR_, c0, stat_consumer, "sSE")
            cstats = CNT["sPE"]
            W("vector", "sPE", cstats)
            W("vector", "sSTV", st["statS"])
            for h in (0, 1):
                TS(row(0, h * 512, 512), pS[h][0:1, :], 1.0 / 2048.0, None, aop.mult)
                TS(row(1, h * 512, 512), pQ[h][0:1, :], 1.0 / 2048.0, None, aop.mult)
            TT(row(2), row(0), row(0), aop.mult)
            cv = TT(row(1), row(1), row(2), aop.subtract, inc="sVE")
            W("scalar", "sVE", cv)
            ACT(row(2), row(1), AF.Sqrt, bias=EPS_LN, scale=1.0, inc="sSE")
            W("vector", "sSE", CNT["sSE"])
            RECIP(row(3), row(2))
            TT(row(2), row(1), row(3), aop.mult)
            cv = TT(row(1), row(2), row(3), aop.mult, inc="sVE")
            W("scalar", "sVE", cv)
            ACT(row(2), row(1), AF.Sqrt, bias=EPS_RMS, scale=1.0, inc="sSE")
            W("vector", "sSE", CNT["sSE"])
            RECIP(row(1), row(2))
            TT(row(2), row(3), row(1), aop.mult)
            TT(row(1), row(0), row(2), aop.mult, inc="sVE")
            BCAST(a2bc, row(2), CH)
            BCAST(b2bc, row(1), CH)
            if save_main:
                W("vector", "sPE", CNT["sPE"])
                cbm = TT(row(2), row(0), row(3), aop.mult, inc="sVE", drain=True)
                W("scalar", "sVE", cbm)
                DMA("scalar", statS[0:1, c0:c0 + CH], row(3), "sSTV")
                st["statS"] = DMA("scalar", statS[1:2, c0:c0 + CH], row(2), "sSTV")
            W("vector", "sSE", CNT["sSE"])
            W("vector", "sPE", CNT["sPE"])

            def aff_consumer(ct, xap, ldc):
                W("vector", "sLD", ldc)
                TT(aff_t, xap, a2bc, aop.mult)
                return TT(xr_tile(ct), aff_t, b2bc, aop.subtract, inc="sVE")
            x_pass(xR_, c0, aff_consumer, "sVE")

        def emit_qkv(c0, groups, rowbase, qgrp, kgrp):
            prot = [pA, pB, pC]
            pri = 0
            W("tensor", "sVE", CNT["sVE"])
            for gi, (g, oc0) in enumerate(groups):
                par = gi % 2
                W("gpsimd", "sPE", st["w"].get(("qkv", par), 0))
                W("gpsimd", "sSE", st["w"].get(("qkvS", par), 0))
                wdst = view3(WZ[par], 0, BF16, NT, 512)
                lw = DMA("gpsimd", wdst, wqkvR[:, :, oc0:oc0 + 512], "sWD")
                epar = gi % 2
                W("scalar", "sSTA", st["eb"].get(epar, 0))
                W("tensor", "sWD", lw)
                lastmm = None
                is_q = g in qgrp
                is_k = g in kgrp
                for m in range(4):
                    for h in (0, 1):
                        pp = prot[pri % 3]
                        pri += 1
                        W("tensor", "sSE", st["ps"].get(("qk", id(pp)), 0))
                        for t in range(NT):
                            lastmm = MM(pp[:, :], wdst[:, t, m * 128:(m + 1) * 128],
                                        xr_tile(t)[:, h * 512:(h + 1) * 512], t == 0, t == NT - 1,
                                        inc="sPE" if t == NT - 1 else None)
                        W("scalar", "sPE", lastmm)
                        ce = SC(eb_sl(epar, m, h), pp[:, :], inc="sSE")
                        if is_q or is_k:
                            spar = (m * 2 + h) % 2
                            W("scalar", "sPE", st["tz"].get(("sqq", spar), 0))
                            cs = ACT(sqq_par(spar), pp[:, :], AF.Square, inc="sSE")
                            pQ = (pSa if h == 0 else pSb) if is_q else (pSc if h == 0 else pSd)
                            first = (g == (qgrp[0] if is_q else kgrp[0]) and m == 0)
                            last = (g == (qgrp[-1] if is_q else kgrp[-1]) and m == 3)
                            W("tensor", "sSE", cs)
                            cq = MM(pQ[0:1, :], oc1b[:, 0:1], sqq_par(spar)[:, :], first, last, inc="sPE")
                            st["tz"][("sqq", spar)] = cq
                            ce = cs
                        st["ps"][("qk", id(pp))] = ce
                st["w"][("qkv", par)] = lastmm
                SDRAIN()
                sb = DMA("scalar", qkvSR[:, rowbase + g * 4: rowbase + (g + 1) * 4, c0:c0 + CH], eb_par(epar), "sSTA")
                st["eb"][epar] = sb

        def emit_rms_rows(pH0, pH1, dst_bc, r0, r1):
            W("scalar", "sPE", CNT["sPE"])
            ACT(row(r0, 0, 512), pH0[0:1, :], AF.Sqrt, bias=EPS_RMS, scale=1.0 / 2048.0)
            ACT(row(r0, 512, 512), pH1[0:1, :], AF.Sqrt, bias=EPS_RMS, scale=1.0 / 2048.0, inc="sSE")
            W("vector", "sSE", CNT["sSE"])
            RECIP(row(r1), row(r0), inc="sVE")
            BCAST(dst_bc, row(r1), CH)

        barrier()
        W("gpsimd", "sSE", CNT["sSE"])
        W("gpsimd", "sVE", CNT["sVE"])
        tz0 = view(tzp, 0, F32, CH)
        tz1 = view(tzp, 4096, F32, CH)
        tz2 = view(tzp, 8192, F32, CH)
        tz3 = view(tzp, 12288, F32, CH)
        for pci in range(2):
            c0 = pci * CH
            emit_ln_affine(xpR, c0, save_main=False)
            emit_qkv(c0, [(g, 2048 + g * 512) for g in range(8)], 48, qgrp=[], kgrp=[0, 1, 2, 3])
            emit_rms_rows(pSc, pSd, rkbc, 0, 1)
            W("sync", "sSTA", CNT["sSTA"])
            W("vector", "sSE", CNT["sSE"])

            def ms(ct, k):
                if ct % 2 == 0:
                    return view(tzp, k * 4096, F32, CH) if k < 4 else view(big, 16384, F32, CH)
                return view(big, (k * 4096) if k < 4 else 20480, F32, CH)
            ce = {}
            vp = {}

            def pre1(ct):
                par = ct % 2
                W("sync", "sVE", st["p2"].get(par, 0))
                dst = view3(big, 32768 + par * 6144, BF16, 2, CH)
                lc = DMA("sync", dst, qkvSR[:, 48 + ct:48 + ct + 17:16, c0:c0 + CH], "sLD")
                kap = view(big, 32768 + par * 6144, BF16, CH)
                vp[ct] = view(big, 32768 + par * 6144 + 2048, BF16, CH)
                W("vector", "sLD", lc)
                cv = TT(ms(ct, 0), kap, rkbc, aop.mult, inc="sVE")
                W("scalar", "sVE", cv)
                ce[ct] = ACT(ms(ct, 1), ms(ct, 0), AF.Exp, inc="sSE")

            def pre2(ct):
                W("vector", "sSE", ce[ct])
                cr0 = TT(ms(ct, 0), ms(ct, 1), vp[ct], aop.mult, inc="sVE")
                st["p2"][ct % 2] = cr0
                RED(cz[:, pci * 16 + ct: pci * 16 + ct + 1], ms(ct, 1))
                RED(cz[:, 32 + pci * 16 + ct: 32 + pci * 16 + ct + 1], ms(ct, 0), inc="sVE")
            pre1(0)
            for ct in range(1, NT):
                pre1(ct)
                pre2(ct - 1)
            pre2(NT - 1)
        W("vector", "sLD", ldMsk)
        VF(3)
        TT(cz[:, 64:80], cz[:, 0:16], cz[:, 16:32], aop.add)
        VF(3)
        TT(cz[:, 80:96], cz[:, 32:48], cz[:, 48:64], aop.add)
        VF(3)
        TS(cz[:, 96:112], cz[:, 64:80], mskb[:, 0:1], None, aop.mult)
        TS(cz[:, 112:128], cz[:, 80:96], mskb[:, 0:1], None, aop.mult)
        VF(3)

        sta_c = {}
        rqbc_c = {0: view(bzp, 8192, F32, CH), 1: view(bzp, 0, F32, CH)}
        rkbc_c = {0: view(bzp, 12288, F32, CH), 1: view(bzp, 4096, F32, CH)}
        for ci in range(2):
            c0 = ci * CH
            emit_ln_affine(xmR, c0, save_main=True)
            emit_qkv(c0, [(g, g * 512) for g in range(12)], 0, qgrp=[0, 1, 2, 3], kgrp=[4, 5, 6, 7])
            sta_c[ci] = CNT["sSTA"]
            emit_rms_rows(pSa, pSb, rqbc_c[ci], 0, 1)
            emit_rms_rows(pSc, pSd, rkbc_c[ci], 2, 3)
        for ci in range(2):
            c0 = ci * CH
            rqbc2 = rqbc_c[ci]
            rkbc2 = rkbc_c[ci]
            W("sync", "sSTA", sta_c[ci])
            W("vector", "sSE", CNT["sSE"])

            def ms(ct, k):
                if ct % 2 == 0:
                    return view(tzp, k * 4096, F32, CH) if k < 4 else view(yzp, 0, F32, CH)
                if k < 3:
                    return view(big, 77824 + k * 4096, F32, CH)
                return view(yzp, 4096 + (k - 3) * 4096, F32, CH)
            ce = {}
            cse = {}
            qp = {}
            vp = {}

            def mx1(ct):
                par = ct % 2
                W("sync", "sVE", st["p2"].get(par, 0))
                dst = view3(big, 32768 + par * 6144, BF16, 3, CH)
                lc = DMA("sync", dst, qkvSR[:, ct:ct + 33:16, c0:c0 + CH], "sLD")
                qp[ct] = view(big, 32768 + par * 6144, BF16, CH)
                kap = view(big, 32768 + par * 6144 + 2048, BF16, CH)
                vp[ct] = view(big, 32768 + par * 6144 + 4096, BF16, CH)
                W("vector", "sLD", lc)
                cv = TT(ms(ct, 0), kap, rkbc2, aop.mult, inc="sVE")
                W("scalar", "sVE", cv)
                ce[ct] = ACT(ms(ct, 1), ms(ct, 0), AF.Exp, inc="sSE")

            def mx2(ct):
                ypar = (ct // 4) % 2
                cq = TT(ms(ct, 2), qp[ct], rqbc2, aop.mult, inc="sVE")
                W("scalar", "sVE", cq)
                cse[ct] = ACT(ms(ct, 3), ms(ct, 2), AF.Sigmoid, inc="sSE")
                W("vector", "sSE", ce[ct])
                cwv = TT(ms(ct, 0), ms(ct, 1), vp[ct], aop.mult, inc="sVE")
                st["p2"][ct % 2] = cwv
                SCAN(ms(ct, 4), ms(ct, 0), cz[:, 112 + ct:113 + ct])
                SCAN(ms(ct, 0), ms(ct, 1), cz[:, 96 + ct:97 + ct])
                VC(cz[:, 112 + ct:113 + ct], ms(ct, 4)[:, CH - 1:CH])
                VC(cz[:, 96 + ct:97 + ct], ms(ct, 0)[:, CH - 1:CH])
                TS(ms(ct, 1), ms(ct, 0), 1e-6, None, aop.add)
                RECIP(ms(ct, 0), ms(ct, 1))
                W("vector", "sSE", cse[ct])
                TT(ms(ct, 1), ms(ct, 3), ms(ct, 4), aop.mult)
                if ct % 4 == 0:
                    W("vector", "sSTV", st["yb"].get(ypar, 0))
                yb = view(big, 45056 + ypar * 8192 + (ct % 4) * 2048, BF16, CH)
                cyb = TT(yb, ms(ct, 1), ms(ct, 0), aop.mult, inc="sVE", drain=(ct % 4 == 3))
                if ct % 4 == 3:
                    W("scalar", "sVE", cyb)
                    sb = DMA("scalar", ySR[:, ct - 3:ct + 1, c0:c0 + CH],
                             view3(big, 45056 + ypar * 8192, BF16, 4, CH), "sSTV")
                    st["yb"][ypar] = sb
            mx1(0)
            for ct in range(1, NT):
                mx1(ct)
                mx2(ct - 1)
            mx2(NT - 1)

        ug_tile = lambda j, Cc: view(big, j * 1408, FP16, Cc)
        barrier()
        W("sync", "sVE", CNT["sVE"])
        W("sync", "sSTV", CNT["sSTV"])
        W("sync", "sSTA", CNT["sSTA"])
        for c3i, (o3, Cc) in enumerate(C3LIST):
            hl = Cc // 2
            halves = [(0, hl), (hl, hl)]
            W("sync", "sPE", CNT["sPE"])
            l1 = DMA("sync", szw[0:1, 0:Cc], statS[0:1, o3:o3 + Cc], "sLD")
            l2 = DMA("sync", szw[0:1, 1024:1024 + Cc], statS[1:2, o3:o3 + Cc], "sLD")
            W("vector", "sLD", l2)
            TS(row(2, 0, Cc), row(0, 0, Cc), 1.0, None, aop.add, inc="sVE")
            rs1bc = view(bzp, 0, F32, Cc)
            bmbc = view(bzp, 4096, F32, Cc)
            BCAST(rs1bc, row(2, 0, Cc), Cc)
            BCAST(bmbc, row(1, 1024 - 1024 + 0, Cc) if False else szw[0:1, 1024:1024 + Cc], Cc)
            ly = DMA("sync", view3(yzp, 0, BF16, NT, Cc), ySR[:, :, o3:o3 + Cc], "sLD")
            ytl = lambda t: view(yzp, t * (Cc * 2), BF16, Cc)
            W("vector", "sPE", st["ps"].get("ug", 0))
            W("vector", "sSTA", CNT["sSTA"])
            W("vector", "sSTV", CNT["sSTV"])
            sgap = view(tzp, 0, F32, 352)
            pupair = [pA, pB]
            pgpair = [pC, pBC]
            W("tensor", "sSE", CNT["sSE"])
            W("tensor", "sLD", ly)
            st["w"][("sw", 0)] = st["w"][("sw", 1)] = CNT["sPE"]
            for grp in range(32):
                par = grp % 2
                W("gpsimd", "sPE", st["w"].get(("sw", par), 0))
                du = view3(WZ[par], 0, BF16, NT, 256)
                dg = view3(WZ[par], 8192, BF16, NT, 256)
                DMA("gpsimd", du, wswR[:, :, grp * 256:(grp + 1) * 256], "sWD")
                lw = DMA("gpsimd", dg, wswR[:, :, 8192 + grp * 256: 8192 + (grp + 1) * 256], "sWD")
                W("tensor", "sWD", lw)
                lastmm = None
                for j2 in range(2):
                    j = grp * 2 + j2
                    for hi, (hc0, hn) in enumerate(halves):
                        pu = pupair[(j2 + hi) % 2]
                        pg = pgpair[(j2 + hi) % 2]
                        W("tensor", "sVE", st["ps"].get(("p3u", id(pu)), 0))
                        W("tensor", "sSE", st["ps"].get(("p3g", id(pg)), 0))
                        for t in range(NT):
                            MM(pu[:, 0:hn], du[:, t, j2 * 128:(j2 + 1) * 128], ytl(t)[:, hc0:hc0 + hn],
                               t == 0, t == NT - 1, inc="sPE" if t == NT - 1 else None)
                        for t in range(NT):
                            lastmm = MM(pg[:, 0:hn], dg[:, t, j2 * 128:(j2 + 1) * 128], ytl(t)[:, hc0:hc0 + hn],
                                        t == 0, t == NT - 1, inc="sPE" if t == NT - 1 else None)
                        W("scalar", "sPE", lastmm)
                        W("scalar", "sVE", st["sg"])
                        cs = ACT(sgap[:, 0:hn], pg[:, 0:hn], AF.Silu, inc="sSE")
                        st["ps"][("p3g", id(pg))] = cs
                        W("vector", "sSE", cs)
                        cu = TT(ug_tile(j, Cc)[:, hc0:hc0 + hn], pu[:, 0:hn], sgap[:, 0:hn], aop.mult, inc="sVE")
                        st["sg"] = cu
                        st["ps"][("p3u", id(pu))] = cu
                st["w"][("sw", par)] = lastmm
            pocs = [pA, pB, pC, pBC]
            st["w"][("wo", 0)] = st["w"][("wo", 1)] = CNT["sPE"]
            e1 = view(tzp, 1408, F32, Cc)
            e2 = view(tzp, 4224, F32, Cc)
            x1e = lambda k_: view(tzp, 7040 + k_ * Cc * 4, F32, Cc)
            W("tensor", "sSE", CNT["sSE"])
            W("tensor", "sVE", CNT["sVE"])
            xl = {}
            for ocg in range(8):
                for o2 in range(2):
                    o = ocg * 2 + o2
                    xsl = view(xz, ((ocg % 2) * 2 + o2) * 4096, F32, Cc)
                    W("sync", "sVE", st["ps"].get(("p3x", ocg % 2, o2), st["xz_v"]))
                    xl[o2] = DMA("sync", xsl, xmR[:, o, o3:o3 + Cc], "sLD")
                lastmm = None
                for isg in range(4):
                    par = isg % 2
                    W("gpsimd", "sPE", st["w"].get(("wo", par), 0))
                    dw = view3(WZ[par], 0, FP16, NT, 256)
                    lw = DMA("gpsimd", dw, wouR[:, isg * 16:(isg + 1) * 16, ocg * 256:(ocg + 1) * 256], "sWD")
                    W("tensor", "sWD", lw)
                    if isg == 0:
                        for pp in pocs:
                            W("tensor", "sVE", st["ps"].get(("p3o", id(pp)), 0))
                    for o2 in range(2):
                        for hi, (hc0, hn) in enumerate(halves):
                            pp = pocs[o2 * 2 + hi]
                            for t in range(NT):
                                lastmm = MM(pp[:, 0:hn], dw[:, t, o2 * 128:(o2 + 1) * 128],
                                            ug_tile(isg * 16 + t, Cc)[:, hc0:hc0 + hn],
                                            isg == 0 and t == 0, isg == 3 and t == NT - 1,
                                            inc="sPE" if (t == NT - 1 and o2 == 1 and hi == 1) else None)
                    st["w"][("wo", par)] = lastmm
                W("vector", "sPE", CNT["sPE"])
                for o2 in range(2):
                    xsl = view(xz, ((ocg % 2) * 2 + o2) * 4096, F32, Cc)
                    W("vector", "sLD", xl[o2])
                    TT(e1, xsl, rs1bc, aop.mult)
                    TT(e2, e1, bmbc, aop.subtract)
                    if o2 == 0:
                        W("vector", "sSTV", st["x1e"])
                    for hi, (hc0, hn) in enumerate(halves):
                        pp = pocs[o2 * 2 + hi]
                        cvx = TT(x1e(o2)[:, hc0:hc0 + hn], pp[:, 0:hn], e2[:, hc0:hc0 + hn], aop.add, inc="sVE",
                                 drain=(o2 == 1 and hi == 1))
                        st["ps"][("p3o", id(pp))] = cvx
                        st["ps"][("p3x", ocg % 2, o2)] = cvx
                x1es = view3(tzp, 7040, F32, 2, Cc)
                W("scalar", "sVE", CNT["sVE"])
                st["x1e"] = DMA("scalar", x1SR[:, ocg * 2:(ocg + 1) * 2, o3:o3 + Cc], x1es, "sSTV")
            st["ps"]["ug"] = CNT["sPE"]
        x1_stores = CNT["sSTV"]

        htile = lambda j: view(big, j * 1024, FP16, C4)
        xq1t = lambda t: view(big, 65536 + t * 1024, FP16, C4)
        mm_b = lambda k_: view(tzp, 8192 + k_ * 2048, F32, C4)
        barrier()
        W("sync", "sSTA", wq_stores_done)
        W("sync", "sSTV", x1_stores)
        st["ps"]["ffn1"] = CNT["sPE"]
        st["ps"]["ffn2"] = CNT["sPE"]
        st["w"][("f1", 0)] = st["w"][("f1", 1)] = CNT["sPE"]
        st["w"][("f2", 0)] = st["w"][("f2", 1)] = CNT["sPE"]
        st["xz_v"] = CNT["sVE"]
        st["xz_s"] = CNT["sSE"]

        def ptree():
            E("vector", lambda e: e.drain(), inc="sVE")
            for stp in range(7):
                sz = 64 >> stp
                W("sync", "sVE", CNT["sVE"])
                l1 = DMA("sync", mm_b(1)[0:sz, :], mm_b(0)[sz:2 * sz, :], "sLD")
                l2 = DMA("sync", mm_b(3)[0:sz, :], mm_b(2)[sz:2 * sz, :], "sLD")
                W("vector", "sLD", l2)
                TT(mm_b(0)[0:sz, :], mm_b(0)[0:sz, :], mm_b(1)[0:sz, :], aop.max)
                TT(mm_b(2)[0:sz, :], mm_b(2)[0:sz, :], mm_b(3)[0:sz, :], aop.min, inc="sVE", drain=True)
        for c4i in range(4):
            o4 = c4i * C4

            def x1_pass(consumer, rel_sem):
                rel = {}
                for jj in range(4):
                    par = jj % 2
                    if jj >= 2:
                        W("sync", rel_sem, rel[par])
                    else:
                        W("sync", "sVE", st["xz_v"])
                        W("sync", "sSE", st["xz_s"])
                    dst = view3(xz, par * 8192, F32, 4, C4)
                    ldc = DMA("sync", dst, x1SR[:, 4 * jj:4 * jj + 4, o4:o4 + C4], "sLD")
                    for k in range(4):
                        ct = 4 * jj + k
                        xap = view(xz, par * 8192 + k * 2048, F32, C4)
                        rel[par] = consumer(ct, xap, ldc)
                st["xz_v"] = CNT["sVE"]
                st["xz_s"] = CNT["sSE"]
            def stat_c(ct, xap, ldc):
                par = ct % 2
                W("scalar", "sLD", ldc)
                W("scalar", "sPE", st["tz"].get(("st4", par), 0))
                ACT(view(tzp, par * 2048, BF16, C4), xap, AF.Square)
                c2 = SC(view(tzp, 4096 + par * 2048, BF16, C4), xap, inc="sSE")
                W("tensor", "sSE", c2)
                MM(pSa[0:1, :], oc1b[:, 0:1], view(tzp, 4096 + par * 2048, BF16, C4), ct == 0, ct == 15)
                cc = MM(pSb[0:1, :], oc1b[:, 0:1], view(tzp, par * 2048, BF16, C4), ct == 0, ct == 15, inc="sPE")
                st["tz"][("st4", par)] = cc
                return c2
            x1_pass(stat_c, "sSE")
            W("vector", "sPE", CNT["sPE"])
            TS(row(0, 0, 512), pSa[0:1, :], 1.0 / 2048.0, None, aop.mult)
            TS(row(1, 0, 512), pSb[0:1, :], 1.0 / 2048.0, None, aop.mult)
            TT(row(2, 0, 512), row(0, 0, 512), row(0, 0, 512), aop.mult)
            cv = TT(row(1, 0, 512), row(1, 0, 512), row(2, 0, 512), aop.subtract, inc="sVE")
            W("scalar", "sVE", cv)
            ACT(row(2, 0, 512), row(1, 0, 512), AF.Sqrt, bias=EPS_LN, scale=1.0, inc="sSE")
            W("vector", "sSE", CNT["sSE"])
            RECIP(row(0, 512, 512), row(2, 0, 512), inc="sVE")
            m4bc = view(bzp, 0, F32, C4)
            BCAST(m4bc, row(0, 0, 512), C4)
            W("vector", "sSE", CNT["sSE"])
            dbuf = view(tzp, 0, F32, C4)

            def mm_c(ct, xap, ldc):
                W("vector", "sLD", ldc)
                TT(dbuf, xap, m4bc, aop.subtract)
                if ct == 0:
                    VC(mm_b(0), dbuf)
                    cc = VC(mm_b(2), dbuf, inc="sVE")
                else:
                    TT(mm_b(0), mm_b(0), dbuf, aop.max)
                    cc = TT(mm_b(2), mm_b(2), dbuf, aop.min, inc="sVE")
                return cc
            x1_pass(mm_c, "sVE")
            ptree()
            mxf, mnf = mm_b(0)[0:1, :], mm_b(2)[0:1, :]
            TS(row(2, 0, 512), mnf, -1.0, None, aop.mult)
            TT(row(2, 512, 512), row(2, 0, 512), mxf, aop.max)
            TT(row(1, 0, 512), row(2, 512, 512), row(0, 512, 512), aop.mult)
            TS(row(3, 0, 512), row(1, 0, 512), 1e-5, None, aop.max)
            RECIP(row(3, 512, 512), row(3, 0, 512))
            TS(row(1, 0, 512), row(3, 512, 512), 127.0, None, aop.mult)
            TT(row(1, 512, 512), row(1, 0, 512), row(0, 512, 512), aop.mult)
            TT(row(2, 0, 512), row(0, 0, 512), row(1, 512, 512), aop.mult)
            TS(row(2, 512, 512), row(3, 0, 512), 1.0 / 127.0, None, aop.mult)
            TS(row(3, 512, 512), row(2, 512, 512), szp[0:1, 0:1], None, aop.mult, inc="sVE")
            abc = view(bzp, 2048, F32, C4)
            bbc = view(bzp, 4096, F32, C4)
            d1bc = view(bzp, 6144, F32, C4)
            BCAST(abc, row(1, 512, 512), C4)
            BCAST(bbc, row(2, 0, 512), C4)
            BCAST(d1bc, row(3, 512, 512), C4)
            W("vector", "sSE", CNT["sSE"])
            W("vector", "sPE", st["ps"]["ffn1"])
            t1 = view(tzp, 0, F32, C4)
            t2 = view(tzp, 2048, F32, C4)

            def q_c(ct, xap, ldc):
                W("vector", "sLD", ldc)
                TT(t1, xap, abc, aop.mult)
                TT(t2, t1, bbc, aop.subtract)
                return TS(xq1t(ct), t2, MAGIC, MAGIC, aop.add, aop.subtract, inc="sVE")
            x1_pass(q_c, "sVE")
            xq1_done = CNT["sVE"]
            prot = [pA, pB, pC]
            pri = 0
            zsl = lambda p_: view(tzp, 4096 + p_ * 2048, F32, C4)
            W("scalar", "sPE", st["ps"]["ffn2"])
            W("tensor", "sVE", xq1_done)
            st["w"][("f1", 0)] = st["w"][("f1", 1)] = CNT["sPE"]
            hm_init = False
            for g in range(16):
                par = g % 2
                W("sync", "sPE", st["w"].get(("f1", par), 0))
                dw = view3(WZ[par], 0, FP16, NT, 512)
                lw = DMA("sync", dw, wq1R[:, :, g * 512:(g + 1) * 512], "sLD")
                W("tensor", "sLD", lw)
                lastmm = None
                for j2 in range(4):
                    j = g * 4 + j2
                    pp = prot[pri % 3]
                    pri += 1
                    W("tensor", "sVE", st["ps"].get(("f1", id(pp)), 0))
                    for t in range(NT):
                        lastmm = MM(pp[:, :], dw[:, t, j2 * 128:(j2 + 1) * 128], xq1t(t), t == 0, t == NT - 1,
                                    inc="sPE" if t == NT - 1 else None)
                    zp = j % 2
                    W("vector", "sPE", lastmm)
                    W("vector", "sSE", st["tz"].get(("z", zp), 0))
                    cz1 = TT(zsl(zp), pp[:, :], d1bc, aop.mult, inc="sVE")
                    st["ps"][("f1", id(pp))] = cz1
                    W("scalar", "sVE", cz1)
                    cs = ACT(htile(j), zsl(zp), AF.Gelu, inc="sSE")
                    st["tz"][("z", zp)] = cs
                    W("vector", "sSE", cs)
                    if not hm_init:
                        VC(mm_b(0), htile(j))
                        VC(mm_b(2), htile(j), inc="sVE")
                        hm_init = True
                    else:
                        TT(mm_b(0), mm_b(0), htile(j), aop.max)
                        TT(mm_b(2), mm_b(2), htile(j), aop.min, inc="sVE")
                st["w"][("f1", par)] = lastmm
            st["ps"]["ffn1"] = CNT["sPE"]
            ptree()
            mxf, mnf = mm_b(0)[0:1, :], mm_b(2)[0:1, :]
            TS(row(2, 0, 512), mnf, -1.0, None, aop.mult)
            TT(row(2, 512, 512), row(2, 0, 512), mxf, aop.max)
            TS(row(3, 0, 512), row(2, 512, 512), 1e-5, None, aop.max)
            RECIP(row(3, 512, 512), row(3, 0, 512))
            TS(row(1, 0, 512), row(3, 512, 512), 127.0, None, aop.mult)
            TS(row(2, 512, 512), row(3, 0, 512), 1.0 / 127.0, None, aop.mult)
            TS(row(3, 512, 512), row(2, 512, 512), szp[0:1, 1:2], None, aop.mult, inc="sVE")
            s2bc = view(bzp, 0, F32, C4)
            d2bc = view(bzp, 2048, F32, C4)
            BCAST(s2bc, row(1, 0, 512), C4)
            BCAST(d2bc, row(3, 512, 512), C4)
            W("vector", "sSE", CNT["sSE"])
            for j in range(64):
                TT(t1, htile(j), s2bc, aop.mult)
                TS(htile(j), t1, MAGIC, MAGIC, aop.add, aop.subtract, inc="sVE" if j == 63 else None)
            hq_done = CNT["sVE"]
            pocs = [pA, pB, pC, pBC]
            st["w"][("f2", 0)] = st["w"][("f2", 1)] = CNT["sPE"]
            oeb = lambda k_: view(yzp, k_ * 2048, F32, C4)
            f2l = {}
            for ocg in range(4):
                W("sync", "sVE", st["ps"].get(("f2x", ocg % 2), st["xz_v"]))
                dstx = view3(xz, (ocg % 2) * 8192, F32, 4, C4)
                f2l[ocg] = DMA("sync", dstx, x1SR[:, 4 * ocg:4 * ocg + 4, o4:o4 + C4], "sLD")
                lastmm = None
                for isg in range(4):
                    par = isg % 2
                    W("sync", "sPE", st["w"].get(("f2", par), 0))
                    dw = view3(WZ[par], 0, FP16, NT, 512)
                    lw = DMA("sync", dw, wq2R[:, isg * 16:(isg + 1) * 16, ocg * 512:(ocg + 1) * 512], "sLD")
                    W("tensor", "sLD", lw)
                    if isg == 0:
                        W("tensor", "sVE", hq_done)
                        for pp in pocs:
                            W("tensor", "sVE", st["ps"].get(("f2", id(pp)), 0))
                    for oi in range(4):
                        pp = pocs[oi]
                        for t in range(NT):
                            lastmm = MM(pp[:, :], dw[:, t, oi * 128:(oi + 1) * 128],
                                        htile(isg * 16 + t), isg == 0 and t == 0, isg == 3 and t == NT - 1,
                                        inc="sPE" if (t == NT - 1 and oi == 3) else None)
                    st["w"][("f2", par)] = lastmm
                W("vector", "sPE", CNT["sPE"])
                W("vector", "sLD", f2l[ocg])
                for oi in range(4):
                    o = ocg * 4 + oi
                    pp = pocs[oi]
                    TT(t1, pp[:, :], d2bc, aop.mult)
                    xap = view(xz, (ocg % 2) * 8192 + oi * 2048, F32, C4)
                    if oi % 2 == 0:
                        W("vector", "sSTV", st["oe"])
                    cvx = TT(oeb(oi % 2), t1, xap, aop.add, inc="sVE", drain=(oi % 2 == 1))
                    st["ps"][("f2", id(pp))] = cvx
                    st["ps"][("f2x", ocg % 2)] = cvx
                    if oi % 2 == 1:
                        oes = view3(yzp, 0, F32, 2, C4)
                        W("scalar", "sVE", cvx)
                        st["oe"] = DMA("scalar", outR[:, o - 1:o + 1, o4:o4 + C4], oes, "sSTV")
            st["ps"]["ffn2"] = CNT["sPE"]
            st["xz_v"] = CNT["sVE"]

        if dbg:
            dcz = nc.dram_tensor("dbg_cz", [P, 128], F32, kind="ExternalOutput")
            dsw = nc.dram_tensor("dbg_szw", [1, 4096], F32, kind="ExternalOutput")
            dsp = nc.dram_tensor("dbg_szp", [1, 32], F32, kind="ExternalOutput")
            dh = nc.dram_tensor("dbg_h", [P, 32768], FP16, kind="ExternalOutput")
            dxq = nc.dram_tensor("dbg_xq", [P, 8192], FP16, kind="ExternalOutput")
            W("sync", "sVE", CNT["sVE"])
            W("sync", "sSE", CNT["sSE"])
            W("sync", "sPE", CNT["sPE"])
            DMA("sync", dcz[:, :], cz[:, :], "sLD")
            DMA("sync", dsw[:, :], szw[:, :], "sLD")
            DMA("sync", dsp[:, :], szp[:, :], "sLD")
            DMA("sync", dh[:, :], view(big, 0, FP16, 32768), "sLD")
            DMA("sync", dxq[:, :], view(big, 65536, FP16, 8192), "sLD")
        W("sync", "sSTV", CNT["sSTV"])
        W("sync", "sSTA", CNT["sSTA"])
        W("sync", "sLD", CNT["sLD"])

        for s, v in wait_list:
            assert v <= CNT[s], f"wait {s} {v} > final {CNT[s]}"

        with nc.Block() as block:
            def replay(lst):
                def go(e):
                    for fn, inc, n, drn in lst:
                        ins = fn(e)
                        if inc:
                            if drn:
                                e.drain().then_inc(SEMH[inc], n)
                            else:
                                ins.then_inc(SEMH[inc], n)
                return go
            block.sync(replay(ops["sync"]))
            block.gpsimd(replay(ops["gpsimd"]))
            block.vector(replay(ops["vector"]))
            block.scalar(replay(ops["scalar"]))
            block.tensor(replay(ops["tensor"]))
    return nc


_NC_CACHE = None


def kernel(L0_attn_x, L0_attn_w_qkv, L0_attn_w_swiglu, L0_attn_w_out, L0_ffn_w1, L0_ffn_w2):
    global _NC_CACHE
    x = np.asarray(L0_attn_x, dtype=np.float32)
    wqkvH = np.ascontiguousarray(np.asarray(L0_attn_w_qkv, np.float32).T).reshape(NT, P, 6144)
    wswH = np.ascontiguousarray(np.asarray(L0_attn_w_swiglu, np.float32).T).reshape(NT, P, 16384)
    wouH = np.ascontiguousarray(np.asarray(L0_attn_w_out, np.float32).T).reshape(64, P, D)
    w1H = np.ascontiguousarray(np.asarray(L0_ffn_w1, np.float32).T).reshape(NT, P, 8192)
    w2H = np.ascontiguousarray(np.asarray(L0_ffn_w2, np.float32).T).reshape(64, P, D)
    if _NC_CACHE is None:
        _NC_CACHE = build()
    nc = _NC_CACHE
    in_maps = []
    for c in range(8):
        b, h = c // 2, c % 2
        xT = np.ascontiguousarray(x[b].T)
        xmH = np.ascontiguousarray(xT[:, h * 2048:(h + 1) * 2048]).reshape(NT, P, D)
        xpH = np.ascontiguousarray(xT[:, 0:2048]).reshape(NT, P, D)
        mk = np.full((P, 1), float(h), np.float32)
        in_maps.append({"xm": xmH, "xp": xpH, "msk": mk, "wqkv": wqkvH, "wsw": wswH,
                        "wou": wouH, "w1": w1H, "w2": w2H})
    res = run_bass_kernel_spmd(nc, in_maps, core_ids=list(range(8)))
    outp = np.empty((4, 4096, D), np.float32)
    for c in range(8):
        b, h = c // 2, c % 2
        o = res.results[c]["out"].reshape(D, D)
        outp[b, h * 2048:(h + 1) * 2048, :] = o.T
    return outp


# version 07
# speedup vs baseline: 1.0218x; 1.0038x over previous
import numpy as np
import concourse.bass as bass
import concourse.mybir as mybir
from concourse.bass_utils import run_bass_kernel_spmd
from concourse.alu_op_type import AluOpType as aop
from contextlib import ExitStack

F32 = mybir.dt.float32
BF16 = mybir.dt.bfloat16
FP16 = mybir.dt.float16
AF = mybir.ActivationFunctionType
AX = mybir.AxisListType

P = 128
D = 2048
NT = 16
CH = 1024
MAGIC = float(1.5 * 2.0 ** 23)
EPS_RMS = 1.1920929e-07
EPS_LN = 1e-5
C3LIST = [(0, 704), (704, 704), (1408, 640)]
C4 = 512

ENGS = ["sync", "gpsimd", "vector", "scalar", "tensor"]
SEMS = ["sLD", "sWD", "sSTA", "sSTV", "sPE", "sVE", "sSE"]


def build(dbg=False):
    nc = bass.Bass("TRN2", target_bir_lowering=False)
    SCR = "ExternalOutput" if dbg else "Internal"
    xm = nc.dram_tensor("xm", [NT, P, D], F32, kind="ExternalInput")
    xp = nc.dram_tensor("xp", [NT, P, D], F32, kind="ExternalInput")
    msk = nc.dram_tensor("msk", [P, 1], F32, kind="ExternalInput")
    wqkv = nc.dram_tensor("wqkv", [NT, P, 6144], F32, kind="ExternalInput")
    wsw = nc.dram_tensor("wsw", [NT, P, 16384], F32, kind="ExternalInput")
    wou = nc.dram_tensor("wou", [64, P, D], F32, kind="ExternalInput")
    w1 = nc.dram_tensor("w1", [NT, P, 8192], F32, kind="ExternalInput")
    w2 = nc.dram_tensor("w2", [64, P, D], F32, kind="ExternalInput")
    out = nc.dram_tensor("out", [NT, P, D], F32, kind="ExternalOutput")
    qkvS = nc.dram_tensor("qkvS", [80, P, D], BF16, kind=SCR)
    yS = nc.dram_tensor("yS", [NT, P, D], BF16, kind=SCR)
    x1S = nc.dram_tensor("x1S", [NT, P, D], F32, kind=SCR)
    statS = nc.dram_tensor("statS", [2, D], F32, kind=SCR)
    wq1S = nc.dram_tensor("wq1S", [NT, P, 8192], FP16, kind=SCR)
    wq2S = nc.dram_tensor("wq2S", [64, P, D], FP16, kind=SCR)

    xmR = xm.rearrange("t p c -> p t c")
    xpR = xp.rearrange("t p c -> p t c")
    wqkvR = wqkv.rearrange("t p c -> p t c")
    wswR = wsw.rearrange("t p c -> p t c")
    wouR = wou.rearrange("t p c -> p t c")
    wq1R = wq1S.rearrange("t p c -> p t c")
    wq2R = wq2S.rearrange("t p c -> p t c")
    qkvSR = qkvS.rearrange("t p c -> p t c")
    ySR = yS.rearrange("t p c -> p t c")
    x1SR = x1S.rearrange("t p c -> p t c")
    outR = out.rearrange("t p c -> p t c")

    CNT = {s: 0 for s in SEMS}
    ops = {e: [] for e in ENGS}
    wait_list = []

    def E(eng, fn, inc=None, n=1, drain=False):
        if inc:
            CNT[inc] += n
        ops[eng].append((fn, inc, n, drain))
        return CNT[inc] if inc else None

    def W(eng, sem, val):
        if val is None or val <= 0:
            return
        wait_list.append((sem, val))
        ops[eng].append((lambda e, s=sem, v=val: e.wait_ge(SEMH[s], v), None, 0, False))

    def DMA(eng, o, i, sem):
        return E(eng, lambda e, O=o, I=i: e.dma_start(out=O, in_=I), inc=sem, n=16)

    def TT(o, a, b, alu, inc=None, drain=False):
        return E("vector", lambda e, O=o, A=a, B=b, L=alu: e.tensor_tensor(O, A, B, L), inc=inc, drain=drain)

    def TS(o, a, s1, s2, op0, op1=None, inc=None):
        if op1 is None:
            return E("vector", lambda e, O=o, A=a, X=s1, o0=op0: e.tensor_scalar(O, A, X, None, o0), inc=inc)
        return E("vector", lambda e, O=o, A=a, X=s1, Y=s2, o0=op0, o1=op1: e.tensor_scalar(O, A, X, Y, o0, o1), inc=inc)

    def VC(o, a, inc=None):
        return E("vector", lambda e, O=o, A=a: e.tensor_copy(O, A), inc=inc)

    def RECIP(o, a, inc=None):
        return E("vector", lambda e, O=o, A=a: e.reciprocal(O, A), inc=inc)

    def RED(o, a, inc=None, absv=False):
        return E("vector", lambda e, O=o, A=a, V=absv: e.tensor_reduce(O, A, AX.X, aop.add, apply_absolute_value=V), inc=inc)

    def SCAN(o, a, init, inc=None):
        return E("vector", lambda e, O=o, A=a, I=init: e.tensor_tensor_scan(O, A, A, I, aop.add, aop.bypass), inc=inc)

    def ACT(o, a, f, bias=0.0, scale=1.0, inc=None):
        return E("scalar", lambda e, O=o, A=a, F=f, B=bias, S=scale: e.activation(O, A, F, bias=B, scale=S), inc=inc)

    def SC(o, a, inc=None):
        return E("scalar", lambda e, O=o, A=a: e.copy(O, A), inc=inc)

    def SDRAIN():
        return E("scalar", lambda e: e.drain())

    def MM(o, l, r, st, sp, inc=None):
        return E("tensor", lambda e, O=o, L=l, R=r, S=st, T=sp: e.matmul(O, L, R, start=S, stop=T), inc=inc)

    def MS(eng, ap, val):
        return E(eng, lambda e, A=ap, V=val: e.memset(A, V))

    with ExitStack() as ctx:
        sbt = lambda n_, s, d: ctx.enter_context(nc.sbuf_tensor(n_, s, d))
        pst = lambda n_, s: ctx.enter_context(nc.psum_tensor(n_, s, F32))
        xz = sbt("xz", [P, 4096], F32)
        big = sbt("big", [P, 22528], F32)
        yzp = sbt("yzp", [P, 5632], F32)
        wza = sbt("wza", [P, 4096], F32)
        wzb = sbt("wzb", [P, 4096], F32)
        tzp = sbt("tzp", [P, 4096], F32)
        bzp = sbt("bzp", [P, 4096], F32)
        szw = sbt("szw", [1, 4096], F32)
        szp = sbt("szp", [1, 32], F32)
        cz = sbt("cz", [P, 128], F32)
        oc1 = sbt("oc1", [P, 1], F32)
        oc1b = sbt("oc1b", [P, 1], BF16)
        or1 = sbt("or1", [1, P], F32)
        mskb = sbt("mskb", [P, 1], F32)
        cst = sbt("cst", [P, 8], F32)
        WZ = [wza, wzb]

        pA = pst("pA", [P, 512])
        pB = pst("pB", [P, 512])
        pC = pst("pC", [P, 512])
        pBC = pst("pBC", [P, 512])
        pSa = pst("pSa", [1, 512])
        pSb = pst("pSb", [1, 512])
        pSc = pst("pSc", [1, 512])
        pSd = pst("pSd", [1, 512])

        SEMH = {s: ctx.enter_context(nc.semaphore(s)) for s in SEMS}

        def view(pool, off_b, dt, ncols):
            esz = 4 if dt == F32 else 2
            nf4 = (ncols * esz + 3) // 4
            ap = pool[:, off_b // 4: off_b // 4 + nf4]
            if dt != F32:
                ap = ap.bitcast(dt)
            return ap[:, 0:ncols]

        def view3(pool, off_b, dt, a, c):
            return view(pool, off_b, dt, a * c).rearrange("p (a c) -> p a c", a=a)

        def row(r, c0=0, n=1024):
            return szw[0:1, r * 1024 + c0: r * 1024 + c0 + n]

        def VF(n=3):
            for _ in range(n):
                VC(cst[:, 6:7], cst[:, 7:8])

        def barrier(engs=("vector", "scalar", "tensor")):
            for eng in engs:
                for s in ("sVE", "sSE", "sPE"):
                    W(eng, s, CNT[s])

        MS("vector", oc1[:, :], 1.0)
        MS("vector", oc1b[:, :], 1.0)
        MS("vector", or1[:, :], 1.0)
        MS("vector", cz[:, :], 0.0)
        MS("vector", cst[:, 0:1], EPS_LN)
        MS("vector", cst[:, 1:2], EPS_RMS)
        E("vector", lambda e: e.memset(cst[:, 2:3], -MAGIC), inc="sVE")
        nc.const_aps.register(F32, EPS_LN, cst[:, 0:1])
        nc.const_aps.register(F32, EPS_RMS, cst[:, 1:2])
        nc.const_aps.register(F32, -MAGIC, cst[:, 2:3])
        MS("vector", cst[:, 3:4], MAGIC)
        nc.const_aps.register(F32, MAGIC, cst[:, 3:4])
        ldMsk = DMA("sync", mskb[:, :], msk[:, :], "sLD")

        def BCAST(dst, srcrow, n):
            W("tensor", "sVE", CNT["sVE"])
            c0_ = 0
            while c0_ < n:
                nn = min(512, n - c0_)
                W("tensor", "sSE", CNT["sSE"])
                c = MM(pBC[0:P, 0:nn], or1[0:1, :], srcrow[:, c0_:c0_ + nn], True, True, inc="sPE")
                W("scalar", "sPE", c)
                SC(dst[:, c0_:c0_ + nn], pBC[0:P, 0:nn], inc="sSE")
                c0_ += nn
            return CNT["sSE"]

        st = {"tz": {}, "w": {}, "eb": {}, "ps": {}, "p2": {}, "yb": {}, "sg": 0,
              "x1e": 0, "oe": 0, "xz_v": 0, "xz_s": 0, "statS": 0, "wq_s1": {}}

        wz_last_v = 0
        wz_last_s = 0
        for widx, (wsrc, wdst, nrt, ncols) in enumerate([(w1, wq1S, NT, 8192), (w2, wq2S, 64, D)]):
            nel = float(nrt * P * ncols)
            if ncols == 8192:
                tiles = [(t, 1, c) for t in range(nrt) for c in (0, 4096)]
            else:
                tiles = [(t, 2, 0) for t in range(0, nrt, 2)]
            relv = {}
            for i, (t, na, c0) in enumerate(tiles):
                par = i % 2
                q_ = "sync" if par == 0 else "gpsimd"
                if i >= 2:
                    W(q_, "sVE", relv[par])
                else:
                    W(q_, "sVE", wz_last_v)
                    W(q_, "sSE", wz_last_s)
                if na == 1:
                    src = wsrc[t, :, c0:c0 + 4096]
                else:
                    src = wsrc[t:t + 2, :, :]
                dst = WZ[par][:, 0:4096] if na == 1 else view3(WZ[par], 0, F32, 2, ncols)
                if par == 0:
                    lc = DMA("sync", dst, src, "sLD")
                    W("vector", "sLD", lc)
                else:
                    lc = DMA("gpsimd", dst, src, "sWD")
                    W("vector", "sWD", lc)
                RED(cz[:, 96 + par:97 + par], WZ[par][:, 0:4096], absv=True)
                VF()
                a, b = (98, 99) if i % 2 == 0 else (99, 98)
                relv[par] = TT(cz[:, b:b + 1], cz[:, a:a + 1], cz[:, 96 + par:97 + par], aop.add, inc="sVE")
            wz_last_v = CNT["sVE"]
            fin = 99 if len(tiles) % 2 == 1 else 98
            W("tensor", "sVE", CNT["sVE"])
            c = MM(pSa[0:1, 0:1], oc1[:, 0:1], cz[:, fin:fin + 1], True, True, inc="sPE")
            W("vector", "sPE", c)
            TS(row(0, 0, 1), pSa[0:1, 0:1], 1.0 / nel, 1e-5, aop.mult, aop.max)
            VF()
            VC(szp[0:1, widx:widx + 1], row(0, 0, 1))
            RECIP(row(1, 0, 1), row(0, 0, 1))
            VF()
            VC(szp[0:1, 2 + widx:3 + widx], row(1, 0, 1), inc="sVE")
            BCAST(cst[:, 4 + widx:5 + widx], szp[0:1, 2 + widx:3 + widx], 1)
            E("vector", lambda e: e.memset(cz[:, 98:100], 0.0), inc="sVE")
            swbc = cst[:, 4 + widx:5 + widx]
            oz2d = lambda p_: view(bzp, p_ * 8192, FP16, 4096)
            q1v = lambda p_: view(big, p_ * 16384, F32, 4096)
            q2v = lambda p_: view(big, 32768 + p_ * 16384, F32, 4096)
            q3v = lambda p_: view(big, p_ * 16384, F32, 4096)
            rels = {}
            relst = {}
            for i, (t, na, c0) in enumerate(tiles):
                par = i % 2
                if na == 1:
                    src = wsrc[t, :, c0:c0 + 4096]
                    ddst = wdst[t, :, c0:c0 + 4096]
                else:
                    src = wsrc[t:t + 2, :, :]
                    ddst = wdst[t:t + 2, :, :]
                dst = WZ[par][:, 0:4096] if na == 1 else view3(WZ[par], 0, F32, 2, ncols)
                if par == 0:
                    if i >= 2:
                        W("sync", "sVE", rels[par])
                    else:
                        W("sync", "sVE", wz_last_v)
                        W("sync", "sSE", wz_last_s)
                    lc = DMA("sync", dst, src, "sLD")
                    W("vector", "sLD", lc)
                else:
                    if i >= 2:
                        W("gpsimd", "sVE", rels[par])
                    else:
                        W("gpsimd", "sVE", wz_last_v)
                        W("gpsimd", "sSE", wz_last_s)
                    lc = DMA("gpsimd", dst, src, "sWD")
                    W("vector", "sWD", lc)
                W("vector", "sSE", st["wq_s1"].get(par, 0))
                cv = TS(q1v(par), WZ[par][:, 0:4096], swbc, None, aop.mult, inc="sVE")
                rels[par] = cv
                W("scalar", "sVE", cv)
                ACT(q2v(par), q1v(par), AF.Copy, bias=MAGIC, scale=1.0, inc="sSE")
                W("vector", "sSE", CNT["sSE"])
                cv2 = TS(q3v(par), q2v(par), MAGIC + 1.0, MAGIC - 1.0, aop.min, aop.max, inc="sVE")
                W("scalar", "sVE", cv2)
                if relst.get(par) is not None:
                    W("scalar", "sSTA", relst[par])
                c2 = ACT(oz2d(par)[:, 0:4096], q3v(par), AF.Copy, bias=-MAGIC, scale=1.0, inc="sSE")
                st["wq_s1"][par] = c2
                SDRAIN()
                if na == 1:
                    relst[par] = DMA("scalar", ddst, oz2d(par)[:, 0:4096], "sSTA")
                else:
                    relst[par] = DMA("scalar", ddst, view3(bzp, par * 8192, FP16, 2, ncols), "sSTA")
            wz_last_v = CNT["sVE"]
            wz_last_s = CNT["sSE"]
        wq_stores_done = CNT["sSTA"]

        xr_tile = lambda ct: view(big, ct * 2048, BF16, CH)
        sq_par = lambda p_: view(tzp, p_ * 2048, BF16, CH)
        xbf_par = lambda p_: view(tzp, 4096 + p_ * 2048, BF16, CH)
        aff_t = view(tzp, 8192, F32, CH)
        sqq_par = lambda p_: view(tzp, 12288 + p_ * 1024, BF16, 512)
        a2bc = view(bzp, 0, F32, CH)
        b2bc = view(bzp, 4096, F32, CH)
        rqbc = view(bzp, 8192, F32, CH)
        rkbc = view(bzp, 12288, F32, CH)
        eb_par = lambda p_: view3(big, 61440 + p_ * 8192, BF16, 4, CH)
        eb_sl = lambda p_, m, h: view(big, 61440 + p_ * 8192 + m * 2048 + h * 1024, BF16, 512)


        def x_pass(xR_, c0, consumer, rel_sem):
            rel = {}
            for j in range(8):
                par = j % 2
                if j >= 2:
                    W("sync", rel_sem, rel[par])
                else:
                    W("sync", "sVE", st["xz_v"])
                    W("sync", "sSE", st["xz_s"])
                dst = view3(xz, par * 8192, F32, 2, CH)
                ldc = DMA("sync", dst, xR_[:, 2 * j:2 * j + 2, c0:c0 + CH], "sLD")
                for k in range(2):
                    ct = 2 * j + k
                    xap = view(xz, (par * 2 + k) * 4096, F32, CH)
                    rel[par] = consumer(ct, xap, ldc)
            st["xz_v"] = CNT["sVE"]
            st["xz_s"] = CNT["sSE"]

        def emit_ln_affine(xR_, c0, save_main):
            pS = {0: pSa, 1: pSb}
            pQ = {0: pSc, 1: pSd}
            W("scalar", "sVE", CNT["sVE"])

            def stat_consumer(ct, xap, ldc):
                par = ct % 2
                W("scalar", "sLD", ldc)
                W("scalar", "sPE", st["tz"].get(("st", par), 0))
                ACT(sq_par(par), xap, AF.Square)
                c2 = SC(xbf_par(par), xap, inc="sSE")
                W("tensor", "sSE", c2)
                for h in (0, 1):
                    MM(pS[h][0:1, :], oc1b[:, 0:1], xbf_par(par)[:, h * 512:(h + 1) * 512], ct == 0, ct == 15)
                    cc = MM(pQ[h][0:1, :], oc1b[:, 0:1], sq_par(par)[:, h * 512:(h + 1) * 512], ct == 0, ct == 15,
                            inc="sPE" if h == 1 else None)
                st["tz"][("st", par)] = cc
                return c2
            x_pass(xR_, c0, stat_consumer, "sSE")
            cstats = CNT["sPE"]
            W("vector", "sPE", cstats)
            W("vector", "sSTV", st["statS"])
            for h in (0, 1):
                TS(row(0, h * 512, 512), pS[h][0:1, :], 1.0 / 2048.0, None, aop.mult)
                TS(row(1, h * 512, 512), pQ[h][0:1, :], 1.0 / 2048.0, None, aop.mult)
            TT(row(2), row(0), row(0), aop.mult)
            cv = TT(row(1), row(1), row(2), aop.subtract, inc="sVE")
            W("scalar", "sVE", cv)
            ACT(row(2), row(1), AF.Sqrt, bias=EPS_LN, scale=1.0, inc="sSE")
            W("vector", "sSE", CNT["sSE"])
            RECIP(row(3), row(2))
            TT(row(2), row(1), row(3), aop.mult)
            cv = TT(row(1), row(2), row(3), aop.mult, inc="sVE")
            W("scalar", "sVE", cv)
            ACT(row(2), row(1), AF.Sqrt, bias=EPS_RMS, scale=1.0, inc="sSE")
            W("vector", "sSE", CNT["sSE"])
            RECIP(row(1), row(2))
            TT(row(2), row(3), row(1), aop.mult)
            TT(row(1), row(0), row(2), aop.mult, inc="sVE")
            BCAST(a2bc, row(2), CH)
            BCAST(b2bc, row(1), CH)
            if save_main:
                W("vector", "sPE", CNT["sPE"])
                cbm = TT(row(2), row(0), row(3), aop.mult, inc="sVE", drain=True)
                W("scalar", "sVE", cbm)
                DMA("scalar", statS[0:1, c0:c0 + CH], row(3), "sSTV")
                st["statS"] = DMA("scalar", statS[1:2, c0:c0 + CH], row(2), "sSTV")
            W("vector", "sSE", CNT["sSE"])
            W("vector", "sPE", CNT["sPE"])

            def aff_consumer(ct, xap, ldc):
                W("vector", "sLD", ldc)
                TT(aff_t, xap, a2bc, aop.mult)
                return TT(xr_tile(ct), aff_t, b2bc, aop.subtract, inc="sVE")
            x_pass(xR_, c0, aff_consumer, "sVE")

        def emit_qkv(c0, groups, rowbase, qgrp, kgrp):
            prot = [pA, pB, pC]
            pri = 0
            W("tensor", "sVE", CNT["sVE"])
            for gi, (g, oc0) in enumerate(groups):
                par = gi % 2
                W("gpsimd", "sPE", st["w"].get(("qkv", par), 0))
                W("gpsimd", "sSE", st["w"].get(("qkvS", par), 0))
                wdst = view3(WZ[par], 0, BF16, NT, 512)
                lw = DMA("gpsimd", wdst, wqkvR[:, :, oc0:oc0 + 512], "sWD")
                epar = gi % 2
                W("scalar", "sSTA", st["eb"].get(epar, 0))
                W("tensor", "sWD", lw)
                lastmm = None
                is_q = g in qgrp
                is_k = g in kgrp
                for m in range(4):
                    for h in (0, 1):
                        pp = prot[pri % 3]
                        pri += 1
                        W("tensor", "sSE", st["ps"].get(("qk", id(pp)), 0))
                        for t in range(NT):
                            lastmm = MM(pp[:, :], wdst[:, t, m * 128:(m + 1) * 128],
                                        xr_tile(t)[:, h * 512:(h + 1) * 512], t == 0, t == NT - 1,
                                        inc="sPE" if t == NT - 1 else None)
                        W("scalar", "sPE", lastmm)
                        ce = SC(eb_sl(epar, m, h), pp[:, :], inc="sSE")
                        if is_q or is_k:
                            spar = (m * 2 + h) % 2
                            W("scalar", "sPE", st["tz"].get(("sqq", spar), 0))
                            cs = ACT(sqq_par(spar), pp[:, :], AF.Square, inc="sSE")
                            pQ = (pSa if h == 0 else pSb) if is_q else (pSc if h == 0 else pSd)
                            first = (g == (qgrp[0] if is_q else kgrp[0]) and m == 0)
                            last = (g == (qgrp[-1] if is_q else kgrp[-1]) and m == 3)
                            W("tensor", "sSE", cs)
                            cq = MM(pQ[0:1, :], oc1b[:, 0:1], sqq_par(spar)[:, :], first, last, inc="sPE")
                            st["tz"][("sqq", spar)] = cq
                            ce = cs
                        st["ps"][("qk", id(pp))] = ce
                st["w"][("qkv", par)] = lastmm
                SDRAIN()
                sb = DMA("scalar", qkvSR[:, rowbase + g * 4: rowbase + (g + 1) * 4, c0:c0 + CH], eb_par(epar), "sSTA")
                st["eb"][epar] = sb

        def emit_rms_rows(pH0, pH1, dst_bc, r0, r1):
            W("scalar", "sPE", CNT["sPE"])
            ACT(row(r0, 0, 512), pH0[0:1, :], AF.Sqrt, bias=EPS_RMS, scale=1.0 / 2048.0)
            ACT(row(r0, 512, 512), pH1[0:1, :], AF.Sqrt, bias=EPS_RMS, scale=1.0 / 2048.0, inc="sSE")
            W("vector", "sSE", CNT["sSE"])
            RECIP(row(r1), row(r0), inc="sVE")
            BCAST(dst_bc, row(r1), CH)

        barrier()
        W("gpsimd", "sSE", CNT["sSE"])
        W("gpsimd", "sVE", CNT["sVE"])
        tz0 = view(tzp, 0, F32, CH)
        tz1 = view(tzp, 4096, F32, CH)
        tz2 = view(tzp, 8192, F32, CH)
        tz3 = view(tzp, 12288, F32, CH)
        rkbc_p = {0: view(bzp, 12288, F32, CH), 1: view(bzp, 8192, F32, CH)}
        sta_p = {}
        for pci in range(2):
            c0 = pci * CH
            emit_ln_affine(xpR, c0, save_main=False)
            emit_qkv(c0, [(g, 2048 + g * 512) for g in range(8)], 48, qgrp=[], kgrp=[0, 1, 2, 3])
            sta_p[pci] = CNT["sSTA"]
            emit_rms_rows(pSc, pSd, rkbc_p[pci], 0, 1)
        for pci in range(2):
            c0 = pci * CH
            W("sync", "sSTA", sta_p[pci])
            W("vector", "sSE", CNT["sSE"])

            def ms(ct, k):
                if ct % 2 == 0:
                    return view(tzp, k * 4096, F32, CH) if k < 4 else view(yzp, 0, F32, CH)
                if k < 3:
                    return view(big, 77824 + k * 4096, F32, CH)
                return view(yzp, 4096 + (k - 3) * 4096, F32, CH)
            ce = {}
            vp = {}

            def pre1(ct):
                par = ct % 2
                W("sync", "sVE", st["p2"].get(par, 0))
                dst = view3(big, 32768 + par * 6144, BF16, 2, CH)
                lc = DMA("sync", dst, qkvSR[:, 48 + ct:48 + ct + 17:16, c0:c0 + CH], "sLD")
                kap = view(big, 32768 + par * 6144, BF16, CH)
                vp[ct] = view(big, 32768 + par * 6144 + 2048, BF16, CH)
                W("vector", "sLD", lc)
                cv = TT(ms(ct, 0), kap, rkbc_p[pci], aop.mult, inc="sVE")
                W("scalar", "sVE", cv)
                ce[ct] = ACT(ms(ct, 1), ms(ct, 0), AF.Exp, inc="sSE")

            def pre2(ct):
                W("vector", "sSE", ce[ct])
                cr0 = TT(ms(ct, 0), ms(ct, 1), vp[ct], aop.mult, inc="sVE")
                st["p2"][ct % 2] = cr0
                RED(cz[:, pci * 16 + ct: pci * 16 + ct + 1], ms(ct, 1))
                RED(cz[:, 32 + pci * 16 + ct: 32 + pci * 16 + ct + 1], ms(ct, 0), inc="sVE")
            pre1(0)
            for ct in range(1, NT):
                pre1(ct)
                pre2(ct - 1)
            pre2(NT - 1)
        W("vector", "sLD", ldMsk)
        VF(3)
        TT(cz[:, 64:80], cz[:, 0:16], cz[:, 16:32], aop.add)
        VF(3)
        TT(cz[:, 80:96], cz[:, 32:48], cz[:, 48:64], aop.add)
        VF(3)
        TS(cz[:, 96:112], cz[:, 64:80], mskb[:, 0:1], None, aop.mult)
        TS(cz[:, 112:128], cz[:, 80:96], mskb[:, 0:1], None, aop.mult)
        VF(3)

        sta_c = {}
        rqbc_c = {0: view(bzp, 8192, F32, CH), 1: view(bzp, 0, F32, CH)}
        rkbc_c = {0: view(bzp, 12288, F32, CH), 1: view(bzp, 4096, F32, CH)}
        for ci in range(2):
            c0 = ci * CH
            emit_ln_affine(xmR, c0, save_main=True)
            emit_qkv(c0, [(g, g * 512) for g in range(12)], 0, qgrp=[0, 1, 2, 3], kgrp=[4, 5, 6, 7])
            sta_c[ci] = CNT["sSTA"]
            emit_rms_rows(pSa, pSb, rqbc_c[ci], 0, 1)
            emit_rms_rows(pSc, pSd, rkbc_c[ci], 2, 3)
        for ci in range(2):
            c0 = ci * CH
            rqbc2 = rqbc_c[ci]
            rkbc2 = rkbc_c[ci]
            W("sync", "sSTA", sta_c[ci])
            W("vector", "sSE", CNT["sSE"])

            def ms(ct, k):
                if ct % 2 == 0:
                    return view(tzp, k * 4096, F32, CH) if k < 4 else view(yzp, 0, F32, CH)
                if k < 3:
                    return view(big, 77824 + k * 4096, F32, CH)
                return view(yzp, 4096 + (k - 3) * 4096, F32, CH)
            ce = {}
            cse = {}
            qp = {}
            vp = {}

            def mx1(ct):
                par = ct % 2
                W("sync", "sVE", st["p2"].get(par, 0))
                dst = view3(big, 32768 + par * 6144, BF16, 3, CH)
                lc = DMA("sync", dst, qkvSR[:, ct:ct + 33:16, c0:c0 + CH], "sLD")
                qp[ct] = view(big, 32768 + par * 6144, BF16, CH)
                kap = view(big, 32768 + par * 6144 + 2048, BF16, CH)
                vp[ct] = view(big, 32768 + par * 6144 + 4096, BF16, CH)
                W("vector", "sLD", lc)
                cv = TT(ms(ct, 0), kap, rkbc2, aop.mult, inc="sVE")
                W("scalar", "sVE", cv)
                ce[ct] = ACT(ms(ct, 1), ms(ct, 0), AF.Exp, inc="sSE")

            def mx2(ct):
                ypar = (ct // 4) % 2
                cq = TT(ms(ct, 2), qp[ct], rqbc2, aop.mult, inc="sVE")
                W("scalar", "sVE", cq)
                cse[ct] = ACT(ms(ct, 3), ms(ct, 2), AF.Sigmoid, inc="sSE")
                W("vector", "sSE", ce[ct])
                cwv = TT(ms(ct, 0), ms(ct, 1), vp[ct], aop.mult, inc="sVE")
                st["p2"][ct % 2] = cwv
                SCAN(ms(ct, 4), ms(ct, 0), cz[:, 112 + ct:113 + ct])
                SCAN(ms(ct, 0), ms(ct, 1), cz[:, 96 + ct:97 + ct])
                VC(cz[:, 112 + ct:113 + ct], ms(ct, 4)[:, CH - 1:CH])
                VC(cz[:, 96 + ct:97 + ct], ms(ct, 0)[:, CH - 1:CH])
                TS(ms(ct, 1), ms(ct, 0), 1e-6, None, aop.add)
                RECIP(ms(ct, 0), ms(ct, 1))
                W("vector", "sSE", cse[ct])
                TT(ms(ct, 1), ms(ct, 3), ms(ct, 4), aop.mult)
                if ct % 4 == 0:
                    W("vector", "sSTV", st["yb"].get(ypar, 0))
                yb = view(big, 45056 + ypar * 8192 + (ct % 4) * 2048, BF16, CH)
                cyb = TT(yb, ms(ct, 1), ms(ct, 0), aop.mult, inc="sVE", drain=(ct % 4 == 3))
                if ct % 4 == 3:
                    W("scalar", "sVE", cyb)
                    sb = DMA("scalar", ySR[:, ct - 3:ct + 1, c0:c0 + CH],
                             view3(big, 45056 + ypar * 8192, BF16, 4, CH), "sSTV")
                    st["yb"][ypar] = sb
            mx1(0)
            for ct in range(1, NT):
                mx1(ct)
                mx2(ct - 1)
            mx2(NT - 1)

        ug_tile = lambda j, Cc: view(big, j * 1408, FP16, Cc)
        barrier()
        W("sync", "sVE", CNT["sVE"])
        W("sync", "sSTV", CNT["sSTV"])
        W("sync", "sSTA", CNT["sSTA"])
        for c3i, (o3, Cc) in enumerate(C3LIST):
            hl = Cc // 2
            halves = [(0, hl), (hl, hl)]
            W("sync", "sPE", CNT["sPE"])
            l1 = DMA("sync", szw[0:1, 0:Cc], statS[0:1, o3:o3 + Cc], "sLD")
            l2 = DMA("sync", szw[0:1, 1024:1024 + Cc], statS[1:2, o3:o3 + Cc], "sLD")
            W("vector", "sLD", l2)
            TS(row(2, 0, Cc), row(0, 0, Cc), 1.0, None, aop.add, inc="sVE")
            rs1bc = view(bzp, 0, F32, Cc)
            bmbc = view(bzp, 4096, F32, Cc)
            BCAST(rs1bc, row(2, 0, Cc), Cc)
            BCAST(bmbc, row(1, 1024 - 1024 + 0, Cc) if False else szw[0:1, 1024:1024 + Cc], Cc)
            ly = DMA("sync", view3(yzp, 0, BF16, NT, Cc), ySR[:, :, o3:o3 + Cc], "sLD")
            ytl = lambda t: view(yzp, t * (Cc * 2), BF16, Cc)
            W("vector", "sPE", st["ps"].get("ug", 0))
            W("vector", "sSTA", CNT["sSTA"])
            W("vector", "sSTV", CNT["sSTV"])
            sgap = view(tzp, 0, F32, 352)
            pupair = [pA, pB]
            pgpair = [pC, pBC]
            W("tensor", "sSE", CNT["sSE"])
            W("tensor", "sLD", ly)
            st["w"][("sw", 0)] = st["w"][("sw", 1)] = CNT["sPE"]
            for grp in range(32):
                par = grp % 2
                W("gpsimd", "sPE", st["w"].get(("sw", par), 0))
                du = view3(WZ[par], 0, BF16, NT, 256)
                dg = view3(WZ[par], 8192, BF16, NT, 256)
                DMA("gpsimd", du, wswR[:, :, grp * 256:(grp + 1) * 256], "sWD")
                lw = DMA("gpsimd", dg, wswR[:, :, 8192 + grp * 256: 8192 + (grp + 1) * 256], "sWD")
                W("tensor", "sWD", lw)
                lastmm = None
                for j2 in range(2):
                    j = grp * 2 + j2
                    for hi, (hc0, hn) in enumerate(halves):
                        pu = pupair[(j2 + hi) % 2]
                        pg = pgpair[(j2 + hi) % 2]
                        W("tensor", "sVE", st["ps"].get(("p3u", id(pu)), 0))
                        W("tensor", "sSE", st["ps"].get(("p3g", id(pg)), 0))
                        for t in range(NT):
                            MM(pu[:, 0:hn], du[:, t, j2 * 128:(j2 + 1) * 128], ytl(t)[:, hc0:hc0 + hn],
                               t == 0, t == NT - 1, inc="sPE" if t == NT - 1 else None)
                        for t in range(NT):
                            lastmm = MM(pg[:, 0:hn], dg[:, t, j2 * 128:(j2 + 1) * 128], ytl(t)[:, hc0:hc0 + hn],
                                        t == 0, t == NT - 1, inc="sPE" if t == NT - 1 else None)
                        W("scalar", "sPE", lastmm)
                        W("scalar", "sVE", st["sg"])
                        cs = ACT(sgap[:, 0:hn], pg[:, 0:hn], AF.Silu, inc="sSE")
                        st["ps"][("p3g", id(pg))] = cs
                        W("vector", "sSE", cs)
                        cu = TT(ug_tile(j, Cc)[:, hc0:hc0 + hn], pu[:, 0:hn], sgap[:, 0:hn], aop.mult, inc="sVE")
                        st["sg"] = cu
                        st["ps"][("p3u", id(pu))] = cu
                st["w"][("sw", par)] = lastmm
            pocs = [pA, pB, pC, pBC]
            st["w"][("wo", 0)] = st["w"][("wo", 1)] = CNT["sPE"]
            e1 = view(tzp, 1408, F32, Cc)
            e2 = view(tzp, 4224, F32, Cc)
            x1e = lambda k_: view(tzp, 7040 + k_ * Cc * 4, F32, Cc)
            W("tensor", "sSE", CNT["sSE"])
            W("tensor", "sVE", CNT["sVE"])
            xl = {}
            for ocg in range(8):
                for o2 in range(2):
                    o = ocg * 2 + o2
                    xsl = view(xz, ((ocg % 2) * 2 + o2) * 4096, F32, Cc)
                    W("sync", "sVE", st["ps"].get(("p3x", ocg % 2, o2), st["xz_v"]))
                    xl[o2] = DMA("sync", xsl, xmR[:, o, o3:o3 + Cc], "sLD")
                lastmm = None
                for isg in range(4):
                    par = isg % 2
                    W("gpsimd", "sPE", st["w"].get(("wo", par), 0))
                    dw = view3(WZ[par], 0, FP16, NT, 256)
                    lw = DMA("gpsimd", dw, wouR[:, isg * 16:(isg + 1) * 16, ocg * 256:(ocg + 1) * 256], "sWD")
                    W("tensor", "sWD", lw)
                    if isg == 0:
                        for pp in pocs:
                            W("tensor", "sVE", st["ps"].get(("p3o", id(pp)), 0))
                    for o2 in range(2):
                        for hi, (hc0, hn) in enumerate(halves):
                            pp = pocs[o2 * 2 + hi]
                            for t in range(NT):
                                lastmm = MM(pp[:, 0:hn], dw[:, t, o2 * 128:(o2 + 1) * 128],
                                            ug_tile(isg * 16 + t, Cc)[:, hc0:hc0 + hn],
                                            isg == 0 and t == 0, isg == 3 and t == NT - 1,
                                            inc="sPE" if (t == NT - 1 and o2 == 1 and hi == 1) else None)
                    st["w"][("wo", par)] = lastmm
                W("vector", "sPE", CNT["sPE"])
                for o2 in range(2):
                    xsl = view(xz, ((ocg % 2) * 2 + o2) * 4096, F32, Cc)
                    W("vector", "sLD", xl[o2])
                    TT(e1, xsl, rs1bc, aop.mult)
                    TT(e2, e1, bmbc, aop.subtract)
                    if o2 == 0:
                        W("vector", "sSTV", st["x1e"])
                    for hi, (hc0, hn) in enumerate(halves):
                        pp = pocs[o2 * 2 + hi]
                        cvx = TT(x1e(o2)[:, hc0:hc0 + hn], pp[:, 0:hn], e2[:, hc0:hc0 + hn], aop.add, inc="sVE",
                                 drain=(o2 == 1 and hi == 1))
                        st["ps"][("p3o", id(pp))] = cvx
                        st["ps"][("p3x", ocg % 2, o2)] = cvx
                x1es = view3(tzp, 7040, F32, 2, Cc)
                W("scalar", "sVE", CNT["sVE"])
                st["x1e"] = DMA("scalar", x1SR[:, ocg * 2:(ocg + 1) * 2, o3:o3 + Cc], x1es, "sSTV")
            st["ps"]["ug"] = CNT["sPE"]
        x1_stores = CNT["sSTV"]

        htile = lambda j: view(big, j * 1024, FP16, C4)
        xq1t = lambda t: view(big, 65536 + t * 1024, FP16, C4)
        mm_b = lambda k_: view(tzp, 8192 + k_ * 2048, F32, C4)
        barrier()
        W("sync", "sSTA", wq_stores_done)
        W("sync", "sSTV", x1_stores)
        st["ps"]["ffn1"] = CNT["sPE"]
        st["ps"]["ffn2"] = CNT["sPE"]
        st["w"][("f1", 0)] = st["w"][("f1", 1)] = CNT["sPE"]
        st["w"][("f2", 0)] = st["w"][("f2", 1)] = CNT["sPE"]
        st["xz_v"] = CNT["sVE"]
        st["xz_s"] = CNT["sSE"]

        def ptree():
            E("vector", lambda e: e.drain(), inc="sVE")
            for stp in range(7):
                sz = 64 >> stp
                W("sync", "sVE", CNT["sVE"])
                l1 = DMA("sync", mm_b(1)[0:sz, :], mm_b(0)[sz:2 * sz, :], "sLD")
                l2 = DMA("sync", mm_b(3)[0:sz, :], mm_b(2)[sz:2 * sz, :], "sLD")
                W("vector", "sLD", l2)
                TT(mm_b(0)[0:sz, :], mm_b(0)[0:sz, :], mm_b(1)[0:sz, :], aop.max)
                TT(mm_b(2)[0:sz, :], mm_b(2)[0:sz, :], mm_b(3)[0:sz, :], aop.min, inc="sVE", drain=True)
        for c4i in range(4):
            o4 = c4i * C4

            def x1_pass(consumer, rel_sem):
                rel = {}
                for jj in range(4):
                    par = jj % 2
                    if jj >= 2:
                        W("sync", rel_sem, rel[par])
                    else:
                        W("sync", "sVE", st["xz_v"])
                        W("sync", "sSE", st["xz_s"])
                    dst = view3(xz, par * 8192, F32, 4, C4)
                    ldc = DMA("sync", dst, x1SR[:, 4 * jj:4 * jj + 4, o4:o4 + C4], "sLD")
                    for k in range(4):
                        ct = 4 * jj + k
                        xap = view(xz, par * 8192 + k * 2048, F32, C4)
                        rel[par] = consumer(ct, xap, ldc)
                st["xz_v"] = CNT["sVE"]
                st["xz_s"] = CNT["sSE"]
            def stat_c(ct, xap, ldc):
                par = ct % 2
                W("scalar", "sLD", ldc)
                W("scalar", "sPE", st["tz"].get(("st4", par), 0))
                ACT(view(tzp, par * 2048, BF16, C4), xap, AF.Square)
                c2 = SC(view(tzp, 4096 + par * 2048, BF16, C4), xap, inc="sSE")
                W("tensor", "sSE", c2)
                MM(pSa[0:1, :], oc1b[:, 0:1], view(tzp, 4096 + par * 2048, BF16, C4), ct == 0, ct == 15)
                cc = MM(pSb[0:1, :], oc1b[:, 0:1], view(tzp, par * 2048, BF16, C4), ct == 0, ct == 15, inc="sPE")
                st["tz"][("st4", par)] = cc
                return c2
            x1_pass(stat_c, "sSE")
            W("vector", "sPE", CNT["sPE"])
            TS(row(0, 0, 512), pSa[0:1, :], 1.0 / 2048.0, None, aop.mult)
            TS(row(1, 0, 512), pSb[0:1, :], 1.0 / 2048.0, None, aop.mult)
            TT(row(2, 0, 512), row(0, 0, 512), row(0, 0, 512), aop.mult)
            cv = TT(row(1, 0, 512), row(1, 0, 512), row(2, 0, 512), aop.subtract, inc="sVE")
            W("scalar", "sVE", cv)
            ACT(row(2, 0, 512), row(1, 0, 512), AF.Sqrt, bias=EPS_LN, scale=1.0, inc="sSE")
            W("vector", "sSE", CNT["sSE"])
            RECIP(row(0, 512, 512), row(2, 0, 512), inc="sVE")
            m4bc = view(bzp, 0, F32, C4)
            BCAST(m4bc, row(0, 0, 512), C4)
            W("vector", "sSE", CNT["sSE"])
            dbuf = view(tzp, 0, F32, C4)

            def mm_c(ct, xap, ldc):
                W("vector", "sLD", ldc)
                TT(dbuf, xap, m4bc, aop.subtract)
                if ct == 0:
                    VC(mm_b(0), dbuf)
                    cc = VC(mm_b(2), dbuf, inc="sVE")
                else:
                    TT(mm_b(0), mm_b(0), dbuf, aop.max)
                    cc = TT(mm_b(2), mm_b(2), dbuf, aop.min, inc="sVE")
                return cc
            x1_pass(mm_c, "sVE")
            ptree()
            mxf, mnf = mm_b(0)[0:1, :], mm_b(2)[0:1, :]
            TS(row(2, 0, 512), mnf, -1.0, None, aop.mult)
            TT(row(2, 512, 512), row(2, 0, 512), mxf, aop.max)
            TT(row(1, 0, 512), row(2, 512, 512), row(0, 512, 512), aop.mult)
            TS(row(3, 0, 512), row(1, 0, 512), 1e-5, None, aop.max)
            RECIP(row(3, 512, 512), row(3, 0, 512))
            TS(row(1, 0, 512), row(3, 512, 512), 127.0, None, aop.mult)
            TT(row(1, 512, 512), row(1, 0, 512), row(0, 512, 512), aop.mult)
            TT(row(2, 0, 512), row(0, 0, 512), row(1, 512, 512), aop.mult)
            TS(row(2, 512, 512), row(3, 0, 512), 1.0 / 127.0, None, aop.mult)
            TS(row(3, 512, 512), row(2, 512, 512), szp[0:1, 0:1], None, aop.mult, inc="sVE")
            abc = view(bzp, 2048, F32, C4)
            bbc = view(bzp, 4096, F32, C4)
            d1bc = view(bzp, 6144, F32, C4)
            BCAST(abc, row(1, 512, 512), C4)
            BCAST(bbc, row(2, 0, 512), C4)
            BCAST(d1bc, row(3, 512, 512), C4)
            W("vector", "sSE", CNT["sSE"])
            W("vector", "sPE", st["ps"]["ffn1"])
            t1 = view(tzp, 0, F32, C4)
            t2 = view(tzp, 2048, F32, C4)

            def q_c(ct, xap, ldc):
                W("vector", "sLD", ldc)
                TT(t1, xap, abc, aop.mult)
                TT(t2, t1, bbc, aop.subtract)
                return TS(xq1t(ct), t2, MAGIC, MAGIC, aop.add, aop.subtract, inc="sVE")
            x1_pass(q_c, "sVE")
            xq1_done = CNT["sVE"]
            prot = [pA, pB, pC]
            pri = 0
            zsl = lambda p_: view(tzp, 4096 + p_ * 2048, F32, C4)
            W("scalar", "sPE", st["ps"]["ffn2"])
            W("tensor", "sVE", xq1_done)
            st["w"][("f1", 0)] = st["w"][("f1", 1)] = CNT["sPE"]
            hm_init = False
            for g in range(16):
                par = g % 2
                W("sync", "sPE", st["w"].get(("f1", par), 0))
                dw = view3(WZ[par], 0, FP16, NT, 512)
                lw = DMA("sync", dw, wq1R[:, :, g * 512:(g + 1) * 512], "sLD")
                W("tensor", "sLD", lw)
                lastmm = None
                for j2 in range(4):
                    j = g * 4 + j2
                    pp = prot[pri % 3]
                    pri += 1
                    W("tensor", "sVE", st["ps"].get(("f1", id(pp)), 0))
                    for t in range(NT):
                        lastmm = MM(pp[:, :], dw[:, t, j2 * 128:(j2 + 1) * 128], xq1t(t), t == 0, t == NT - 1,
                                    inc="sPE" if t == NT - 1 else None)
                    zp = j % 2
                    W("vector", "sPE", lastmm)
                    W("vector", "sSE", st["tz"].get(("z", zp), 0))
                    cz1 = TT(zsl(zp), pp[:, :], d1bc, aop.mult, inc="sVE")
                    st["ps"][("f1", id(pp))] = cz1
                    W("scalar", "sVE", cz1)
                    cs = ACT(htile(j), zsl(zp), AF.Gelu, inc="sSE")
                    st["tz"][("z", zp)] = cs
                    W("vector", "sSE", cs)
                    if not hm_init:
                        VC(mm_b(0), htile(j))
                        VC(mm_b(2), htile(j), inc="sVE")
                        hm_init = True
                    else:
                        TT(mm_b(0), mm_b(0), htile(j), aop.max)
                        TT(mm_b(2), mm_b(2), htile(j), aop.min, inc="sVE")
                st["w"][("f1", par)] = lastmm
            st["ps"]["ffn1"] = CNT["sPE"]
            ptree()
            mxf, mnf = mm_b(0)[0:1, :], mm_b(2)[0:1, :]
            TS(row(2, 0, 512), mnf, -1.0, None, aop.mult)
            TT(row(2, 512, 512), row(2, 0, 512), mxf, aop.max)
            TS(row(3, 0, 512), row(2, 512, 512), 1e-5, None, aop.max)
            RECIP(row(3, 512, 512), row(3, 0, 512))
            TS(row(1, 0, 512), row(3, 512, 512), 127.0, None, aop.mult)
            TS(row(2, 512, 512), row(3, 0, 512), 1.0 / 127.0, None, aop.mult)
            TS(row(3, 512, 512), row(2, 512, 512), szp[0:1, 1:2], None, aop.mult, inc="sVE")
            s2bc = view(bzp, 0, F32, C4)
            d2bc = view(bzp, 2048, F32, C4)
            BCAST(s2bc, row(1, 0, 512), C4)
            BCAST(d2bc, row(3, 512, 512), C4)
            W("vector", "sSE", CNT["sSE"])
            for j in range(64):
                TT(t1, htile(j), s2bc, aop.mult)
                TS(htile(j), t1, MAGIC, MAGIC, aop.add, aop.subtract, inc="sVE" if j == 63 else None)
            hq_done = CNT["sVE"]
            pocs = [pA, pB, pC, pBC]
            st["w"][("f2", 0)] = st["w"][("f2", 1)] = CNT["sPE"]
            oeb = lambda k_: view(yzp, k_ * 2048, F32, C4)
            f2l = {}
            for ocg in range(4):
                W("sync", "sVE", st["ps"].get(("f2x", ocg % 2), st["xz_v"]))
                dstx = view3(xz, (ocg % 2) * 8192, F32, 4, C4)
                f2l[ocg] = DMA("sync", dstx, x1SR[:, 4 * ocg:4 * ocg + 4, o4:o4 + C4], "sLD")
                lastmm = None
                for isg in range(4):
                    par = isg % 2
                    W("sync", "sPE", st["w"].get(("f2", par), 0))
                    dw = view3(WZ[par], 0, FP16, NT, 512)
                    lw = DMA("sync", dw, wq2R[:, isg * 16:(isg + 1) * 16, ocg * 512:(ocg + 1) * 512], "sLD")
                    W("tensor", "sLD", lw)
                    if isg == 0:
                        W("tensor", "sVE", hq_done)
                        for pp in pocs:
                            W("tensor", "sVE", st["ps"].get(("f2", id(pp)), 0))
                    for oi in range(4):
                        pp = pocs[oi]
                        for t in range(NT):
                            lastmm = MM(pp[:, :], dw[:, t, oi * 128:(oi + 1) * 128],
                                        htile(isg * 16 + t), isg == 0 and t == 0, isg == 3 and t == NT - 1,
                                        inc="sPE" if (t == NT - 1 and oi == 3) else None)
                    st["w"][("f2", par)] = lastmm
                W("vector", "sPE", CNT["sPE"])
                W("vector", "sLD", f2l[ocg])
                for oi in range(4):
                    o = ocg * 4 + oi
                    pp = pocs[oi]
                    TT(t1, pp[:, :], d2bc, aop.mult)
                    xap = view(xz, (ocg % 2) * 8192 + oi * 2048, F32, C4)
                    if oi % 2 == 0:
                        W("vector", "sSTV", st["oe"])
                    cvx = TT(oeb(oi % 2), t1, xap, aop.add, inc="sVE", drain=(oi % 2 == 1))
                    st["ps"][("f2", id(pp))] = cvx
                    st["ps"][("f2x", ocg % 2)] = cvx
                    if oi % 2 == 1:
                        oes = view3(yzp, 0, F32, 2, C4)
                        W("scalar", "sVE", cvx)
                        st["oe"] = DMA("scalar", outR[:, o - 1:o + 1, o4:o4 + C4], oes, "sSTV")
            st["ps"]["ffn2"] = CNT["sPE"]
            st["xz_v"] = CNT["sVE"]

        if dbg:
            dcz = nc.dram_tensor("dbg_cz", [P, 128], F32, kind="ExternalOutput")
            dsw = nc.dram_tensor("dbg_szw", [1, 4096], F32, kind="ExternalOutput")
            dsp = nc.dram_tensor("dbg_szp", [1, 32], F32, kind="ExternalOutput")
            dh = nc.dram_tensor("dbg_h", [P, 32768], FP16, kind="ExternalOutput")
            dxq = nc.dram_tensor("dbg_xq", [P, 8192], FP16, kind="ExternalOutput")
            W("sync", "sVE", CNT["sVE"])
            W("sync", "sSE", CNT["sSE"])
            W("sync", "sPE", CNT["sPE"])
            DMA("sync", dcz[:, :], cz[:, :], "sLD")
            DMA("sync", dsw[:, :], szw[:, :], "sLD")
            DMA("sync", dsp[:, :], szp[:, :], "sLD")
            DMA("sync", dh[:, :], view(big, 0, FP16, 32768), "sLD")
            DMA("sync", dxq[:, :], view(big, 65536, FP16, 8192), "sLD")
        W("sync", "sSTV", CNT["sSTV"])
        W("sync", "sSTA", CNT["sSTA"])
        W("sync", "sLD", CNT["sLD"])

        for s, v in wait_list:
            assert v <= CNT[s], f"wait {s} {v} > final {CNT[s]}"

        with nc.Block() as block:
            def replay(lst):
                def go(e):
                    for fn, inc, n, drn in lst:
                        ins = fn(e)
                        if inc:
                            if drn:
                                e.drain().then_inc(SEMH[inc], n)
                            else:
                                ins.then_inc(SEMH[inc], n)
                return go
            block.sync(replay(ops["sync"]))
            block.gpsimd(replay(ops["gpsimd"]))
            block.vector(replay(ops["vector"]))
            block.scalar(replay(ops["scalar"]))
            block.tensor(replay(ops["tensor"]))
    return nc


_NC_CACHE = None


def kernel(L0_attn_x, L0_attn_w_qkv, L0_attn_w_swiglu, L0_attn_w_out, L0_ffn_w1, L0_ffn_w2):
    global _NC_CACHE
    x = np.asarray(L0_attn_x, dtype=np.float32)
    wqkvH = np.ascontiguousarray(np.asarray(L0_attn_w_qkv, np.float32).T).reshape(NT, P, 6144)
    wswH = np.ascontiguousarray(np.asarray(L0_attn_w_swiglu, np.float32).T).reshape(NT, P, 16384)
    wouH = np.ascontiguousarray(np.asarray(L0_attn_w_out, np.float32).T).reshape(64, P, D)
    w1H = np.ascontiguousarray(np.asarray(L0_ffn_w1, np.float32).T).reshape(NT, P, 8192)
    w2H = np.ascontiguousarray(np.asarray(L0_ffn_w2, np.float32).T).reshape(64, P, D)
    if _NC_CACHE is None:
        _NC_CACHE = build()
    nc = _NC_CACHE
    in_maps = []
    for c in range(8):
        b, h = c // 2, c % 2
        xT = np.ascontiguousarray(x[b].T)
        xmH = np.ascontiguousarray(xT[:, h * 2048:(h + 1) * 2048]).reshape(NT, P, D)
        xpH = np.ascontiguousarray(xT[:, 0:2048]).reshape(NT, P, D)
        mk = np.full((P, 1), float(h), np.float32)
        in_maps.append({"xm": xmH, "xp": xpH, "msk": mk, "wqkv": wqkvH, "wsw": wswH,
                        "wou": wouH, "w1": w1H, "w2": w2H})
    res = run_bass_kernel_spmd(nc, in_maps, core_ids=list(range(8)))
    outp = np.empty((4, 4096, D), np.float32)
    for c in range(8):
        b, h = c // 2, c % 2
        o = res.results[c]["out"].reshape(D, D)
        outp[b, h * 2048:(h + 1) * 2048, :] = o.T
    return outp
